# version 04
# baseline (speedup 1.0000x reference)
import math
import contextlib
import numpy as np

import concourse.bass as bass
import concourse.mybir as mybir
from concourse.bass_utils import run_bass_kernel_spmd

F32 = mybir.dt.float32
BF16 = mybir.dt.bfloat16
ALU = mybir.AluOpType
ACT = mybir.ActivationFunctionType
AX = mybir.AxisListType

NT = 4096
D = 1024
MT = 512
NMT = NT // MT
E3 = 3072
FH = 2730
L0H = 4096
DI = 2048
DS = 16
DCONV = 4
DTR = 64
NE = 8
LN2HI = float(np.float32(0.6931471805599453))
LN2LO = float(np.float32(0.6931471805599453 - np.float64(np.float32(0.6931471805599453))))
LOG2E = float(np.float32(1.4426950408889634))
FLT_MIN = 1.17549435e-38

ENGS = ("sync", "vector", "scalar", "tensor", "gpsimd")

STOP_AFTER = None
TAPS = ()
DBG_OUT_SCRATCH = set()


class Prog:

    DMA_ENG = {"s_ld": "sync", "s_st": "sync", "s_g": "gpsimd"}

    def __init__(self, nc):
        self.nc = nc
        self.steps = {e: [] for e in ENGS}
        self.sem = {}
        self.cnt = {}
        self.waited = {e: {} for e in ENGS}
        self.pending_gate = {}

    def new_sem(self, name):
        self.sem[name] = self.nc.alloc_semaphore(name)
        self.cnt[name] = 0
        self.pending_gate[name] = 0

    def wait(self, eng, name, val=None):
        v = self.cnt[name] if val is None else val
        if v <= 0 or self.waited[eng].get(name, 0) >= v:
            return
        if name in self.DMA_ENG:
            self.pending_gate[name] = max(self.pending_gate[name], v)
        self.waited[eng][name] = v
        h = self.sem[name]
        self.steps[eng].append(lambda E, h=h, v=v: E.wait_ge(h, v))

    def emit(self, eng, fn, inc=None, inc_n=1):
        if inc is None:
            self.steps[eng].append(fn)
            return None
        self.cnt[inc] += inc_n
        h = self.sem[inc]

        def wrapped(E, fn=fn, h=h, n=inc_n):
            ins = fn(E)
            ins.then_inc(h, n)
        self.steps[eng].append(wrapped)
        return self.cnt[inc]

    def dma(self, eng, out, in_, inc, slow=False):
        g = self.pending_gate.get(inc, 0)
        if g > self.waited[eng].get(inc, 0):
            self.waited[eng][inc] = g
            h0 = self.sem[inc]
            self.steps[eng].append(lambda E, h=h0, v=g: E.wait_ge(h, v))
        if slow:
            return self.emit(eng, lambda E, o=out, i=in_: E.dma_start(
                out=o, in_=i, allow_slow_non_contiguous=True), inc=inc, inc_n=16)
        return self.emit(eng, lambda E, o=out, i=in_: E.dma_start(out=o, in_=i),
                         inc=inc, inc_n=16)

    def finalize(self, block):
        for e in ENGS:
            items = self.steps[e]
            if not items:
                continue
            def run(E, items=items):
                for it in items:
                    it(E)
            getattr(block, e)(run)


class PsumPool:

    def __init__(self, prog, nbanks=4):
        self.prog = prog
        nc = prog.nc
        self.tiles = [nc.alloc_psum_tensor(f"pb{i}", [128, 512], F32).ap()
                      for i in range(nbanks)]
        self.free_at = [None] * nbanks
        self.rr = 0
        self.n = nbanks

    def acquire(self):
        i = self.rr % self.n
        self.rr += 1
        if self.free_at[i] is not None:
            sem, val = self.free_at[i]
            self.prog.wait("tensor", sem, val)
        return i, self.tiles[i]

    def release(self, i, sem, val):
        self.free_at[i] = (sem, val)

    def manual(self, i):
        return self.tiles[i]


def ceil_div(a, b):
    return (a + b - 1) // b



def ev(prog, fn, inc="s_v", eng="vector"):
    return prog.emit(eng, fn, inc=inc)


def emit_elup1(prog, z_ap, tmp_ap, out_ap, ksum_ap=None):
    ev(prog, lambda E: E.tensor_scalar_min(tmp_ap, z_ap, 0.0))
    prog.wait("scalar", "s_v")
    prog.emit("scalar", lambda E: E.activation(tmp_ap, tmp_ap, ACT.Exp), inc="s_s")
    prog.wait("vector", "s_s")
    if ksum_ap is None:
        ev(prog, lambda E: E.scalar_tensor_tensor(out_ap, z_ap, 0.0, tmp_ap, ALU.max, ALU.add))
    else:
        ev(prog, lambda E: E.scalar_tensor_tensor(out_ap, z_ap, 0.0, tmp_ap, ALU.max, ALU.add,
                                                  accum_out=ksum_ap))


def emit_softplus(prog, x_ap, tmp_ap, out_ap):
    prog.wait("scalar", "s_v")
    prog.emit("scalar", lambda E: E.activation(tmp_ap, x_ap, ACT.Abs), inc="s_s")
    prog.emit("scalar", lambda E: E.activation(tmp_ap, tmp_ap, ACT.Exp, scale=-1.0), inc="s_s")
    prog.emit("scalar", lambda E: E.activation(tmp_ap, tmp_ap, ACT.Ln, bias=1.0), inc="s_s")
    prog.wait("vector", "s_s")
    ev(prog, lambda E: E.scalar_tensor_tensor(out_ap, x_ap, 0.0, tmp_ap, ALU.max, ALU.add))


def emit_quantize(prog, x_ap, t, out_ap, man, min_e, max_val, signed=True,
                  nb_e=4, sat=1000.0):
    T0, T1, T2, T3, T4, T5 = t[:6]
    nb_r = man + 2
    prog.wait("scalar", "s_v")
    if signed:
        prog.emit("scalar", lambda E: E.activation(T0, x_ap, ACT.Sign), inc="s_s")
    prog.emit("scalar", lambda E: E.activation(T1, x_ap, ACT.Abs), inc="s_s")
    prog.wait("vector", "s_s")
    ev(prog, lambda E: E.tensor_scalar(T1, T1, FLT_MIN, float(sat), ALU.max, ALU.min))
    prog.wait("scalar", "s_v")
    prog.emit("scalar", lambda E: E.activation(T1, T1, ACT.Ln), inc="s_s")
    prog.wait("vector", "s_s")
    ev(prog, lambda E: E.tensor_scalar(T1, T1, LOG2E, float(-min_e), ALU.mult, ALU.add))
    ev(prog, lambda E: E.tensor_scalar(T3, T1, 0.0, float(2 ** nb_e) - 1e-4,
                                       ALU.max, ALU.min))
    ev(prog, lambda E: E.tensor_copy(T1, T3))
    for k in range(nb_e - 1, -1, -1):
        b = float(1 << k)
        ev(prog, lambda E, b=b: E.tensor_scalar(T4, T1, b, b, ALU.is_ge, ALU.mult))
        ev(prog, lambda E: E.tensor_tensor(T1, T1, T4, ALU.subtract))
    ev(prog, lambda E: E.tensor_tensor(T2, T3, T1, ALU.subtract))
    C = float(man - min_e)
    ev(prog, lambda E: E.tensor_scalar(T1, T2, -1.0, C, ALU.mult, ALU.add))
    ev(prog, lambda E: E.tensor_scalar(T3, T1, LN2HI, None, ALU.mult))
    ev(prog, lambda E: E.scalar_tensor_tensor(T3, T1, LN2LO, T3, ALU.mult, ALU.add))
    prog.wait("scalar", "s_v")
    prog.emit("scalar", lambda E: E.activation(T1, T3, ACT.Exp), inc="s_s")
    prog.emit("scalar", lambda E: E.activation(T3, T3, ACT.Exp, scale=-1.0), inc="s_s")
    prog.wait("vector", "s_s")
    ev(prog, lambda E: E.tensor_tensor(T1, x_ap, T1, ALU.mult))
    if signed:
        prog.wait("scalar", "s_v")
        prog.emit("scalar", lambda E: E.activation(T1, T1, ACT.Abs), inc="s_s")
        prog.wait("vector", "s_s")
        ev(prog, lambda E: E.tensor_scalar(T1, T1, 0.5 + 1e-9, None, ALU.add))
    else:
        ev(prog, lambda E: E.tensor_scalar(T1, T1, 0.5 + 1e-9, None, ALU.add))
    ev(prog, lambda E: E.tensor_copy(T2, T1))
    for k in range(nb_r - 1, -1, -1):
        b = float(1 << k)
        ev(prog, lambda E, b=b: E.tensor_scalar(T4, T1, b, b, ALU.is_ge, ALU.mult))
        ev(prog, lambda E: E.tensor_tensor(T1, T1, T4, ALU.subtract))
    ev(prog, lambda E: E.tensor_single_scalar(T5, T1, 2e-9, ALU.is_le))
    ev(prog, lambda E: E.tensor_tensor(T5, T5, T4, ALU.mult))
    ev(prog, lambda E: E.tensor_tensor(T1, T1, T5, ALU.add))
    ev(prog, lambda E: E.tensor_tensor(T1, T2, T1, ALU.subtract))
    ev(prog, lambda E: E.tensor_tensor(T1, T1, T3, ALU.mult))
    ovf_thr = max_val * (1.0 + 2.0 ** -(man + 2))
    ev(prog, lambda E: E.tensor_single_scalar(T2, T1, ovf_thr, ALU.is_gt))
    if signed:
        ev(prog, lambda E: E.tensor_tensor(T1, T1, T0, ALU.mult))
    prog.wait("scalar", "s_v")
    prog.emit("scalar", lambda E: E.activation(T5, T2, ACT.Exp, scale=710.0), inc="s_s")
    prog.wait("vector", "s_s")
    ev(prog, lambda E: E.tensor_tensor(out_ap, T1, T5, ALU.mult))



def build(stop_after=None, taps=()):
    nc = bass.Bass("TRN2", target_bir_lowering=False,
                   detect_race_conditions=False)
    stack = contextlib.ExitStack()

    def din(name, shape):
        return nc.dram_tensor(name, list(shape), F32, kind="ExternalInput").ap()

    x_in = din("x", (NT, D))
    Wqkv = din("L0_attn_Wqkv", (D, E3))
    bqkv = din("L0_attn_bqkv", (E3,))
    Wg = din("L0_attn_Wg", (D, FH))
    bg = din("L0_attn_bg", (FH,))
    Wu = din("L0_attn_Wu", (D, FH))
    bu = din("L0_attn_bu", (FH,))
    Wd = din("L0_attn_Wd", (FH, D))
    bd = din("L0_attn_bd", (D,))
    g1 = din("L0_attn_g1", (D,))
    b1 = din("L0_attn_b1", (D,))
    g2 = din("L0_attn_g2", (D,))
    b2 = din("L0_attn_b2", (D,))
    fw1 = din("L0_ffn_w1", (L0H, D))
    fb1 = din("L0_ffn_b1", (L0H,))
    fw2 = din("L0_ffn_w2", (D, L0H))
    fb2 = din("L0_ffn_b2", (D,))
    inpw = din("L1_attn_in_proj_w", (2 * DI, D))
    convw = din("L1_attn_conv_w", (DI, DCONV))
    convb = din("L1_attn_conv_b", (DI,))
    xpw = din("L1_attn_x_proj_w", (DTR + 2 * DS, DI))
    dtpw = din("L1_attn_dt_proj_w", (DI, DTR))
    dtpb = din("L1_attn_dt_proj_b", (DI,))
    alog = din("L1_attn_A_log", (DI, DS))
    dpar = din("L1_attn_D", (DI,))
    outpw = din("L1_attn_out_proj_w", (D, DI))
    mWg = din("L1_ffn_Wg", (D, NE))
    mbg = din("L1_ffn_bg", (NE,))
    mW1 = din("L1_ffn_W1", (NE, D, D))
    mb1 = din("L1_ffn_b1", (NE, D))
    mW2 = din("L1_ffn_W2", (NE, D, D))
    mb2 = din("L1_ffn_b2", (NE, D))
    ident_in = din("ident", (128, 128))

    out_hbm = nc.dram_tensor("out", [NT, D], F32, kind="ExternalOutput").ap()

    def scr(name, shape, dt=F32):
        kind = "ExternalOutput" if name in DBG_OUT_SCRATCH else "Internal"
        return nc.dram_tensor(name, list(shape), dt, kind=kind).ap()

    xn_s = scr("xn_s", (D, NT))
    qp_s = scr("qp_s", (D, NT))
    kv_s = scr("kv_s", (D, D))
    ksum_s = scr("ksum_s", (1, D))
    xnew_s = scr("xnew_s", (D, NT))
    xn1_s = scr("xn1_s", (D, NT))
    xc_s = scr("xc_s", (DI, NT))
    zs_s = scr("zs_s", (DI, NT))
    delta_s = scr("delta_s", (DI, NT))
    bc_s = scr("bc_s", (2 * DS, NT))
    y_s = scr("y_s", (DI, NT))
    g2_s = scr("g2_s", (D, NT))
    w1T_s = scr("w1T_s", (D, L0H))
    w2qT_s = scr("w2qT_s", (L0H, D))
    inpT_s = scr("inpT_s", (D, 2 * DI))
    outpT_s = scr("outpT_s", (DI, D))
    xpT_s = scr("xpT_s", (DI, DTR + 2 * DS))
    dtpT_s = scr("dtpT_s", (DTR, DI))

    tap_out = {}
    for tname in taps:
        shp = {"xn": (D, NT), "qp": (D, NT), "kv": (D, D), "ksum": (1, D),
               "w1T": (D, L0H), "w2qT": (L0H, D), "g": (D, NT), "x1": (D, NT), "x1r": (D, NT), "rden": (1, NT), "denr": (1, NT),
               "s0": (D, NT), "xnew": (D, NT), "xn1": (D, NT),
               "xc": (DI, NT), "zs": (DI, NT), "delta": (DI, NT),
               "bc": (2 * DS, NT), "yts": (DI, NT), "g2": (D, NT),
               "kpt": (128, 4096), "vtm": (128, 4096), "kpf": (128, 4096), "log8": (8, MT), "mtok": (128, 32), "mrow": (NE, MT),
               "sb_log": (8, MT), "sb_mtok": (128, 32), "sb_mrow": (1, MT),
               "sb_bcast2": (128, MT), "sb_ones1": (1, 128), "sb_lgt": (128, 32),
               }[tname]
        tap_out[tname] = nc.dram_tensor("tap_" + tname, list(shp), F32,
                                        kind="ExternalOutput").ap()

    def sb(name, shape, dt=F32):
        return nc.alloc_sbuf_tensor(name, list(shape), dt).ap()

    A = [sb(f"A{i}", (128, NMT, MT)) for i in range(4)]
    Af = [a.rearrange("p a b -> p (a b)") for a in A]
    GB = [sb("GB0", (128, 4, 544)), sb("GB1", (128, 4, 512)),
          sb("GB2", (128, 4, 512)), sb("GB3", (128, 2, 512))]
    QT = sb("QT", (128, 8, 512))
    WS = [sb(f"WS{i}", (128, 8, 128)) for i in range(2)]
    WB = [sb(f"WB{i}", (128, 2048)) for i in range(2)]
    ident = sb("identsb", (128, 128))
    ones1 = sb("ones1", (1, 128))
    ones_col = sb("ones_col", (128, 1))
    KV = sb("KV", (128, 8, 1024))
    KSUM = sb("KSUM", (128, 8))
    stat = sb("stat", (128, 32))
    rstat = sb("rstat", (1, 4 * MT))
    rowbuf = rstat
    BCAST = sb("BCAST", (128, 4, MT))
    carry = sb("carry", (128, 256))
    halo = sb("halo", (128, 16, 4))
    p_bqkv = sb("p_bqkv", (128, 24))
    p_bg = sb("p_bg", (128, 22))
    p_bu = sb("p_bu", (128, 22))
    p_bd = sb("p_bd", (128, 8))
    p_g1 = sb("p_g1", (128, 8))
    p_b1 = sb("p_b1", (128, 8))
    p_g2 = sb("p_g2", (128, 8))
    p_b2 = sb("p_b2", (128, 8))
    p_fb1 = sb("p_fb1", (128, 32))
    p_fb2q = sb("p_fb2q", (128, 8))
    p_convw = sb("p_convw", (128, 16, 4))
    p_convb = sb("p_convb", (128, 16))
    p_dtpb = sb("p_dtpb", (128, 16))
    p_aneg = sb("p_aneg", (128, 16, 16))
    p_dpar = sb("p_dpar", (128, 16))
    p_mb1 = sb("p_mb1", (128, NE, 8))
    p_mb2 = sb("p_mb2", (128, NE, 8))
    p_mbg = sb("p_mbg", (8, 1))
    p_mwg = sb("p_mwg", (128, 8, 8))
    DTW = sb("DTW", (64, DI))
    XPW = sb("XPW", (128, 16, 96))
    LOG = sb("LOG", (8, MT))
    LGT = sb("LGT", (128, 4, 8))
    MXC = sb("MXC", (128, 4))
    ONES8 = sb("ONES8", (128, 4, 8))
    MTOK = sb("MTOK", (128, 4, 8))
    qtmp = [QT[:, i, :] for i in range(8)]

    prog = Prog(nc)
    for s in ("s_ld", "s_st", "s_pe", "s_v", "s_s", "s_g"):
        prog.new_sem(s)
    psum = PsumPool(prog, nbanks=4)
    psBC = nc.alloc_psum_tensor("psBC", [128, 2048], F32).ap()

    W = prog.wait
    EM = prog.emit

    def ld(out, in_, slow=False):
        return prog.dma("sync", out, in_, "s_ld", slow=slow)

    def st(out, in_):
        return prog.dma("sync", out, in_, "s_st")

    EM("vector", lambda E: E.memset(ones1, 1.0), inc="s_v")
    EM("vector", lambda E: E.memset(ones_col, 1.0), inc="s_v")
    for cv in (1e-5, -0.5):
        cst = sb(f"cst_{cv}", (128, 1))
        nc.const_aps.register(F32, float(cv), cst)
        EM("vector", lambda E, t=cst, v=float(cv): E.memset(t, v), inc="s_v")
    W("scalar", "s_v")
    W("tensor", "s_v")
    ld(ident, ident_in)
    ld(p_bqkv, bqkv.rearrange("(j p) -> p j", p=128), slow=True)
    ld(p_bg[:, 0:21], bg[0:2688].rearrange("(j p) -> p j", p=128), slow=True)
    ld(p_bg[0:42, 21:22], bg[2688:2730].rearrange("(j p) -> p j", p=42), slow=True)
    ld(p_bu[:, 0:21], bu[0:2688].rearrange("(j p) -> p j", p=128), slow=True)
    ld(p_bu[0:42, 21:22], bu[2688:2730].rearrange("(j p) -> p j", p=42), slow=True)
    ld(p_bd, bd.rearrange("(j p) -> p j", p=128), slow=True)
    ld(p_g1, g1.rearrange("(j p) -> p j", p=128), slow=True)
    ld(p_b1, b1.rearrange("(j p) -> p j", p=128), slow=True)
    ld(p_g2, g2.rearrange("(j p) -> p j", p=128), slow=True)
    ld(p_b2, b2.rearrange("(j p) -> p j", p=128), slow=True)
    ld(p_fb1, fb1.rearrange("(j p) -> p j", p=128), slow=True)
    ld(p_fb2q, fb2.rearrange("(j p) -> p j", p=128), slow=True)
    ld(p_convw, convw.rearrange("(j p) c -> p j c", p=128), slow=True)
    ld(p_convb, convb.rearrange("(j p) -> p j", p=128), slow=True)
    ld(p_dtpb, dtpb.rearrange("(j p) -> p j", p=128), slow=True)
    ld(p_aneg, alog.rearrange("(j p) n -> p j n", p=128), slow=True)
    ld(p_dpar, dpar.rearrange("(j p) -> p j", p=128), slow=True)
    ld(p_mb1, mb1.rearrange("e (j p) -> p e j", p=128), slow=True)
    ld(p_mb2, mb2.rearrange("e (j p) -> p e j", p=128), slow=True)
    ld(p_mbg, mbg[:, None], slow=True)
    ld(p_mwg, mWg.rearrange("(dc p) e -> p dc e", p=128), slow=True)
    ld(rowbuf[:, 0:E3 - D], bqkv[D:E3][None, :])
    W("vector", "s_ld")
    W("scalar", "s_ld")
    EM("scalar", lambda E: E.activation(p_aneg, p_aneg, ACT.Exp), inc="s_s")
    W("vector", "s_s")
    ev(prog, lambda E: E.tensor_scalar_mul(p_aneg, p_aneg, -1.0))
    b2slab = QT[:, 6, :]
    EM("vector", lambda E: E.memset(b2slab, 1.0), inc="s_v")
    W("vector", "s_ld")
    ev(prog, lambda E: E.tensor_copy(b2slab[:, 0:8], p_fb2q))
    emit_quantize(prog, b2slab, [QT[:, i, :] for i in range(6)], b2slab,
                  man=10, min_e=-14, max_val=65504.0, signed=True,
                  nb_e=5, sat=1.0e5)
    ev(prog, lambda E: E.tensor_copy(p_fb2q, b2slab[:, 0:8]))

    prep_state = {}
    def prep_transpose(src_ap, dst_ap, R, C, quant=None):
        CB = min(C, 1024)
        stg = WS[0].rearrange("p a b -> p (a b)")[:, 0:512]
        wb_lastuse = prep_state.setdefault("wb_lastuse", [0, 0])
        for r0 in range(0, R, 128):
            rr = min(128, R - r0)
            wi = (r0 // 128) % 2
            wsl = WB[wi]
            for c0 in range(0, C, CB):
                cb = min(CB, C - c0)
                W("sync", "s_pe", wb_lastuse[wi])
                lv = ld(wsl[0:rr, 0:cb], src_ap[r0:r0 + rr, c0:c0 + cb])
                if quant is not None:
                    W("vector", "s_ld", lv)
                    W("scalar", "s_ld", lv)
                    quant(wsl[0:rr, 0:cb])
                    W("tensor", "s_v")
                W("tensor", "s_ld", lv)
                for cc in range(0, cb, 512):
                    nj = ceil_div(min(512, cb - cc), 128)
                    bi, ps = psum.acquire()
                    for j in range(nj):
                        jw = min(128, cb - cc - j * 128)
                        EM("tensor", lambda E, ps=ps, wsl=wsl, cc=cc, j=j, jw=jw, rr=rr: E.transpose(
                            ps[0:jw, j * 128:j * 128 + rr],
                            wsl[0:rr, cc + j * 128:cc + j * 128 + jw],
                            ident[0:rr, 0:rr]),
                            inc=("s_pe" if j == nj - 1 else None))
                    W("scalar", "s_pe")
                    W("scalar", "s_st")
                    vv = prog.emit("scalar", lambda E, ps=ps: E.copy(stg, ps),
                                   inc="s_s")
                    psum.release(bi, "s_s", vv)
                    W("sync", "s_s", vv)
                    for j in range(nj):
                        jw = min(128, cb - cc - j * 128)
                        st(dst_ap[c0 + cc + j * 128: c0 + cc + j * 128 + jw,
                                  r0:r0 + rr],
                           stg[0:jw, j * 128:j * 128 + rr])
                wb_lastuse[wi] = prog.cnt["s_pe"]

    def wq_quant_slab(ap_slab):
        for h in range(2):
            sl = ap_slab[:, h * 512:(h + 1) * 512]
            emit_quantize(prog, sl, [qtmp[i] for i in range(6)], sl,
                          man=3, min_e=-6, max_val=240.0, signed=True,
                          nb_e=4, sat=1000.0)

    prep_transpose(fw1, w1T_s, L0H, D)
    prep_transpose(fw2, w2qT_s, D, L0H, quant=wq_quant_slab)
    prep_transpose(inpw, inpT_s, 2 * DI, D)
    prep_transpose(outpw, outpT_s, D, DI)
    prep_transpose(xpw, xpT_s, DTR + 2 * DS, DI)
    prep_transpose(dtpw, dtpT_s, DI, DTR)
    W("sync", "s_st")
    ld(DTW, dtpT_s)
    ld(XPW, xpT_s.rearrange("(dc p) r -> p dc r", p=128))

    SQB = QT[:, 0:2, :].rearrange("p a b -> p (a b)")
    EM("vector", lambda E: E.memset(KV.rearrange("p a b -> p (a b)"), 0.0), inc="s_v")
    EM("vector", lambda E: E.memset(rowbuf[0:1, 0:1024], 0.0), inc="s_v")
    W("sync", "s_st")
    ks_started = [False]
    ws_lastuse = [0, 0]
    W("scalar", "s_v")
    for mt in range(NMT):
        XTf = Af[0]
        W("sync", "s_pe")
        W("vector", "s_pe")
        lv = ld(XTf.rearrange("p (s d) -> p s d", d=D),
                x_in[mt * MT:(mt + 1) * MT, :].rearrange("(s p) d -> p s d", p=128))
        W("vector", "s_ld", lv)
        W("scalar", "s_ld", lv)
        for s in range(4):
            xs = XTf[:, s * D:(s + 1) * D]
            ev(prog, lambda E, xs=xs, s=s: E.reduce_sum(stat[:, s:s + 1], xs, axis=AX.X))
            W("scalar", "s_v")
            prog.emit("scalar", lambda E, xs=xs: E.activation(SQB, xs, ACT.Square), inc="s_s")
            W("vector", "s_s")
            ev(prog, lambda E, s=s: E.reduce_sum(stat[:, 4 + s:5 + s], SQB, axis=AX.X))
        W("vector", "s_v")
        ev(prog, lambda E: E.tensor_scalar_mul(stat[:, 8:12], stat[:, 0:4], 1.0 / D))
        ev(prog, lambda E: E.tensor_scalar_mul(stat[:, 12:16], stat[:, 4:8], 1.0 / D))
        W("scalar", "s_v")
        prog.emit("scalar", lambda E: E.activation(stat[:, 16:20], stat[:, 8:12], ACT.Square), inc="s_s")
        W("vector", "s_s")
        ev(prog, lambda E: E.tensor_tensor(stat[:, 12:16], stat[:, 12:16], stat[:, 16:20], ALU.subtract))
        W("scalar", "s_v")
        prog.emit("scalar", lambda E: E.activation(stat[:, 12:16], stat[:, 12:16], ACT.Ln, bias=1e-5), inc="s_s")
        prog.emit("scalar", lambda E: E.activation(stat[:, 12:16], stat[:, 12:16], ACT.Exp, scale=-0.5), inc="s_s")
        W("vector", "s_s")
        XN = Af[1]
        for s in range(4):
            ev(prog, lambda E, s=s: E.tensor_scalar(
                XN[:, s * D:(s + 1) * D], XTf[:, s * D:(s + 1) * D],
                stat[:, 8 + s:9 + s], stat[:, 12 + s:13 + s], ALU.subtract, ALU.mult))
        XF = A[2]
        W("tensor", "s_v")
        for dc in range(8):
            bi, ps = psum.acquire()
            for s in range(4):
                EM("tensor", lambda E, ps=ps, dc=dc, s=s: E.transpose(
                    ps[:, s * 128:(s + 1) * 128],
                    XN[:, s * D + dc * 128: s * D + (dc + 1) * 128], ident),
                    inc=("s_pe" if s == 3 else None))
            W("scalar", "s_pe")
            v = prog.emit("scalar", lambda E, ps=ps, dc=dc: E.copy(XF[:, dc, :], ps), inc="s_s")
            psum.release(bi, "s_s", v)
        W("sync", "s_s")
        st(xn_s[:, mt * MT:(mt + 1) * MT].rearrange("(dc p) t -> p dc t", p=128),
           XF)
        KPF = Af[0]
        VF = Af[1]
        for ec in range(24):
            wsl = WS[ec % 2]
            W("sync", "s_pe", ws_lastuse[ec % 2])
            lv = ld(wsl, Wqkv[:, ec * 128:(ec + 1) * 128].rearrange("(dc p) e -> p dc e", p=128))
            W("tensor", "s_ld", lv)
            W("tensor", "s_s")
            bi, ps = psum.acquire()
            for dc in range(8):
                EM("tensor", lambda E, ps=ps, wsl=wsl, dc=dc: E.matmul(
                    ps, wsl[:, dc, :], XF[:, dc, :],
                    start=(dc == 0), stop=(dc == 7)),
                    inc=("s_pe" if dc == 7 else None))
            ws_lastuse[ec % 2] = prog.cnt["s_pe"]
            W("scalar", "s_pe")
            W("vector", "s_pe")
            if ec < 8:
                zb = QT[:, 6, :]
                W("scalar", "s_st")
                v1 = prog.emit("scalar", lambda E, ps=ps, ec=ec, zb=zb: E.activation(
                    zb, ps, ACT.Identity, bias=p_bqkv[:, ec:ec + 1]), inc="s_s")
                psum.release(bi, "s_s", v1)
                W("vector", "s_s", v1)
                emit_elup1(prog, zb, QT[:, 7, :], zb)
                W("sync", "s_v")
                st(qp_s[ec * 128:(ec + 1) * 128, mt * MT:(mt + 1) * MT], zb)
                W("vector", "s_st")
            else:
                dstf = KPF if ec < 16 else VF
                cc = (ec - 8) % 8
                zb = dstf[:, cc * 512:(cc + 1) * 512]
                v1 = prog.emit("scalar", lambda E, ps=ps, ec=ec, zb=zb: E.activation(
                    zb, ps, ACT.Identity, bias=p_bqkv[:, ec:ec + 1]), inc="s_s")
                psum.release(bi, "s_s", v1)
                if ec < 16:
                    W("vector", "s_s", v1)
                    emit_elup1(prog, zb, QT[:, 7, :], zb)
        KPT = A[3]
        KPTf = Af[3]
        VTf = Af[2]
        W("tensor", "s_v")
        W("tensor", "s_s")
        W("scalar", "s_st")
        for src, dstf in ((KPF, KPTf), (VF, VTf)):
            for tc in range(4):
                for eh in range(2):
                    bi, ps = psum.acquire()
                    for j in range(4):
                        ecol = eh * 4 + j
                        EM("tensor", lambda E, ps=ps, src=src, tc=tc, ecol=ecol, j=j: E.transpose(
                            ps[:, j * 128:(j + 1) * 128],
                            src[:, ecol * 512 + tc * 128: ecol * 512 + (tc + 1) * 128],
                            ident),
                            inc=("s_pe" if j == 3 else None))
                    W("scalar", "s_pe")
                    v = prog.emit("scalar", lambda E, ps=ps, dstf=dstf, tc=tc, eh=eh: E.copy(
                        dstf[:, tc * 1024 + eh * 512: tc * 1024 + (eh + 1) * 512], ps),
                        inc="s_s")
                    psum.release(bi, "s_s", v)
        W("tensor", "s_s")
        for dc in range(8):
            for ev2 in range(2):
                bi, ps = psum.acquire()
                for tc in range(4):
                    EM("tensor", lambda E, ps=ps, dc=dc, ev2=ev2, tc=tc: E.matmul(
                        ps, KPTf[:, tc * 1024 + dc * 128: tc * 1024 + (dc + 1) * 128],
                        VTf[:, tc * 1024 + ev2 * 512: tc * 1024 + (ev2 + 1) * 512],
                        start=(tc == 0), stop=(tc == 3)),
                        inc=("s_pe" if tc == 3 else None))
                W("vector", "s_pe")
                v = ev(prog, lambda E, ps=ps, dc=dc, ev2=ev2: E.tensor_tensor(
                    KV[:, dc, ev2 * 512:(ev2 + 1) * 512],
                    KV[:, dc, ev2 * 512:(ev2 + 1) * 512], ps, ALU.add))
                psum.release(bi, "s_v", v)
        for eh in range(2):
            bks, ps_ks = psum.acquire()
            for tc in range(4):
                EM("tensor", lambda E, tc=tc, eh=eh, ps=ps_ks: E.matmul(
                    ps[0:1, 0:512], ones_col,
                    KPTf[:, tc * 1024 + eh * 512: tc * 1024 + (eh + 1) * 512],
                    start=(tc == 0), stop=(tc == 3)),
                    inc=("s_pe" if tc == 3 else None))
            W("vector", "s_pe")
            vks = ev(prog, lambda E, eh=eh, ps=ps_ks: E.tensor_tensor(
                rowbuf[0:1, eh * 512:(eh + 1) * 512],
                rowbuf[0:1, eh * 512:(eh + 1) * 512], ps[0:1, 0:512], ALU.add))
            psum.release(bks, "s_v", vks)


    if "kpt" in tap_out:
        W("sync", "s_v")
        W("sync", "s_s")
        prog.dma("sync", tap_out["kpt"], Af[3], "s_st")
    if "vtm" in tap_out:
        W("sync", "s_v")
        W("sync", "s_s")
        prog.dma("sync", tap_out["vtm"], Af[2], "s_st")
    if "kpf" in tap_out:
        W("sync", "s_v")
        W("sync", "s_s")
        prog.dma("sync", tap_out["kpf"], Af[0], "s_st")
    W("sync", "s_v")
    st(ksum_s, rowbuf[0:1, 0:1024])
    st(kv_s.rearrange("(dc p) e -> p dc e", p=128), KV)
    W("sync", "s_st")
    lv = ld(KSUM, ksum_s[0, :].rearrange("(j p) -> p j", p=128), slow=True)

    if stop_after == "phA":
        _finish(prog, nc, psum, out_hbm, tap_out, locals())
        return _close(nc, prog, stack, tap_out)

    KVf = KV.rearrange("p a b -> p (a b)")
    QTf = QT.rearrange("p a b -> p (a b)")
    GBf = [g.rearrange("p a b -> p (a b)") for g in GB]
    q2 = [KVf[:, i * 2048:(i + 1) * 2048] for i in range(4)] + \
         [QTf[:, i * 2048:(i + 1) * 2048] for i in range(2)]
    SQ2 = [QT[:, 0, :], QT[:, 1, :]]

    def emit_fmln(src3, gcol3=None, bcol3=None):
        W("tensor", "s_v")
        W("scalar", "s_v")
        bmu, ps_mu = psum.acquire()
        for dc in range(8):
            EM("tensor", lambda E, ps=ps_mu, dc=dc: E.matmul(
                ps[0:1, :], ones_col, src3[:, dc, :],
                start=(dc == 0), stop=(dc == 7)),
                inc=("s_pe" if dc == 7 else None))
        bm2, ps_m2 = psum.acquire()
        sq_free = [0, 0]
        for dc in range(8):
            sq = SQ2[dc % 2]
            prog.wait("scalar", "s_pe", sq_free[dc % 2])
            vsq = prog.emit("scalar", lambda E, sq=sq, dc=dc: E.activation(
                sq, src3[:, dc, :], ACT.Square), inc="s_s")
            W("tensor", "s_s", vsq)
            sq_free[dc % 2] = EM("tensor", lambda E, ps=ps_m2, sq=sq, dc=dc: E.matmul(
                ps[0:1, :], ones_col, sq,
                start=(dc == 0), stop=(dc == 7)), inc="s_pe")
        r0 = rstat[0:1, 0:512]; r1 = rstat[0:1, 512:1024]
        r2 = rstat[0:1, 1024:1536]; r3 = rstat[0:1, 1536:2048]
        W("vector", "s_pe")
        ev(prog, lambda E, ps=ps_mu: E.tensor_scalar_mul(r0, ps[0:1, :], 1.0 / D))
        ev(prog, lambda E, ps=ps_m2: E.tensor_scalar_mul(r1, ps[0:1, :], 1.0 / D))
        v1 = ev(prog, lambda E: E.tensor_tensor(r3, r0, r0, ALU.mult))
        v2 = ev(prog, lambda E: E.tensor_tensor(r1, r1, r3, ALU.subtract))
        psum.release(bmu, "s_v", v1)
        psum.release(bm2, "s_v", v2)
        W("scalar", "s_v")
        prog.emit("scalar", lambda E: E.activation(r1, r1, ACT.Ln, bias=1e-5), inc="s_s")
        prog.emit("scalar", lambda E: E.activation(r1, r1, ACT.Exp, scale=-0.5), inc="s_s")
        W("vector", "s_s")
        ev(prog, lambda E: E.tensor_tensor(r2, r0, r1, ALU.mult))
        ev(prog, lambda E: E.tensor_scalar_mul(r2, r2, -1.0))
        W("tensor", "s_v")
        for i, row in ((0, r1), (1, r2)):
            bb, psb = psum.acquire()
            EM("tensor", lambda E, ps=psb, row=row: E.matmul(
                ps, ones1, row, start=True, stop=True), inc="s_pe")
            W("scalar", "s_pe")
            vv = prog.emit("scalar", lambda E, ps=psb, i=i: E.copy(BCAST[:, i, :], ps), inc="s_s")
            psum.release(bb, "s_s", vv)
        W("vector", "s_s")
        for dc in range(8):
            ev(prog, lambda E, dc=dc: E.tensor_tensor(src3[:, dc, :], src3[:, dc, :], BCAST[:, 0, :], ALU.mult))
            ev(prog, lambda E, dc=dc: E.tensor_tensor(src3[:, dc, :], src3[:, dc, :], BCAST[:, 1, :], ALU.add))
            if gcol3 is not None:
                ev(prog, lambda E, dc=dc: E.tensor_scalar(
                    src3[:, dc, :], src3[:, dc, :], gcol3[:, dc:dc + 1], bcol3[:, dc:dc + 1],
                    ALU.mult, ALU.add))

    mh_chunks = [(i * 128, 128, i) for i in range(21)] + [(2688, 42, 21)]
    ffn_groups = [mh_chunks[0:4], mh_chunks[4:8], mh_chunks[8:12],
                  mh_chunks[12:16], mh_chunks[16:20], mh_chunks[20:22]]

    for mt in range(NMT):
        t0c, t1c = mt * MT, (mt + 1) * MT
        XNB = A[0]
        GACC = A[1]
        W("sync", "s_v")
        W("sync", "s_s")
        W("sync", "s_pe")
        W("sync", "s_st")
        lvxn = ld(XNB, xn_s[:, t0c:t1c].rearrange("(dc p) t -> p dc t", p=128))
        W("vector", "s_st")
        EM("vector", lambda E: E.memset(Af[1], 0.0), inc="s_v")
        W("tensor", "s_ld", lvxn)
        for gi in range(8):
            HT = GBf[0][:, 0:2048]
            HT3 = GB[0]
            for sub in range(4):
                mh = gi * 4 + sub
                wsl = WS[mh % 2]
                W("sync", "s_pe", ws_lastuse[mh % 2])
                lv = ld(wsl, w1T_s[:, mh * 128:(mh + 1) * 128].rearrange(
                    "(dc p) h -> p dc h", p=128))
                W("tensor", "s_ld", lv)
                bi, ps = psum.acquire()
                for dc in range(8):
                    EM("tensor", lambda E, ps=ps, wsl=wsl, dc=dc: E.matmul(
                        ps, wsl[:, dc, :], XNB[:, dc, :],
                        start=(dc == 0), stop=(dc == 7)),
                        inc=("s_pe" if dc == 7 else None))
                ws_lastuse[mh % 2] = prog.cnt["s_pe"]
                W("scalar", "s_pe")
                prog.wait("scalar", "s_v")
                vv = prog.emit("scalar", lambda E, ps=ps, mh=mh, sub=sub: E.activation(
                    GBf[0][:, sub * 512:(sub + 1) * 512], ps, ACT.Relu,
                    bias=p_fb1[:, mh:mh + 1]), inc="s_s")
                psum.release(bi, "s_s", vv)
            W("vector", "s_s")
            emit_quantize(prog, HT.rearrange("p (a b) -> p a b", b=512),
                          [q.rearrange("p (a b) -> p a b", b=512) for q in q2],
                          HT.rearrange("p (a b) -> p a b", b=512),
                          man=3, min_e=-6, max_val=240.0, signed=False,
                          nb_e=4, sat=1000.0)
            wd_stage = KVf[:, 0:4096].rearrange("p (kh d) -> p kh d", d=1024)
            W("sync", "s_v")
            lv2 = ld(wd_stage, w2qT_s[gi * 512:(gi + 1) * 512, :].rearrange(
                "(kh p) d -> p kh d", p=128))
            W("tensor", "s_ld", lv2)
            W("tensor", "s_v")
            for md in range(8):
                bi, ps = psum.acquire()
                for kh in range(4):
                    EM("tensor", lambda E, ps=ps, wd=wd_stage, kh=kh, md=md: E.matmul(
                        ps, wd[:, kh, md * 128:(md + 1) * 128],
                        GBf[0][:, kh * 512:(kh + 1) * 512],
                        start=(kh == 0), stop=(kh == 3)),
                        inc=("s_pe" if kh == 3 else None))
                W("vector", "s_pe")
                vv = ev(prog, lambda E, ps=ps, md=md: E.tensor_tensor(
                    GACC[:, md, :], GACC[:, md, :], ps, ALU.add))
                psum.release(bi, "s_v", vv)
        for dc in range(8):
            ev(prog, lambda E, dc=dc: E.tensor_scalar_add(
                GACC[:, dc, :], GACC[:, dc, :], p_fb2q[:, dc:dc + 1]))
        GACCf = Af[1]
        for h in range(2):
            sl = GACCf[:, h * 2048:(h + 1) * 2048].rearrange("p (a b) -> p a b", b=512)
            emit_quantize(prog, sl, [q.rearrange("p (a b) -> p a b", b=512) for q in q2],
                          sl, man=10, min_e=-14, max_val=65504.0, signed=True,
                          nb_e=5, sat=1.0e5)
        W("scalar", "s_v")
        prog.emit("scalar", lambda E: E.activation(GACCf, GACCf, ACT.Silu), inc="s_s")
        if "g" in tap_out:
            vtp = prog.cnt["s_s"]
            W("sync", "s_s", vtp)
            prog.dma("sync", tap_out["g"][:, t0c:t1c].rearrange("(dc p) t -> p dc t", p=128),
                     A[1], "s_st")
            W("vector", "s_st")
        QP = A[2]
        X1 = A[3]
        lvqp = ld(QP, qp_s[:, t0c:t1c].rearrange("(dc p) t -> p dc t", p=128))
        bden, ps_den = psum.acquire()
        W("tensor", "s_ld", lvqp)
        for dc in range(8):
            EM("tensor", lambda E, ps=ps_den, dc=dc: E.matmul(
                ps[0:1, :], KSUM[:, dc:dc + 1], QP[:, dc, :],
                start=(dc == 0), stop=(dc == 7)),
                inc=("s_pe" if dc == 7 else None))
        W("vector", "s_pe")
        ev(prog, lambda E, ps=ps_den: E.tensor_scalar_add(rstat[0:1, 0:512], ps[0:1, :], 1e-6))
        if "denr" in tap_out:
            W("sync", "s_v")
            prog.dma("sync", tap_out["denr"][0:1, t0c:t1c], rstat[0:1, 0:512], "s_st")
            W("vector", "s_st")
        v1 = ev(prog, lambda E: E.reciprocal(rstat[0:1, 0:512], rstat[0:1, 0:512]))
        psum.release(bden, "s_v", v1)
        W("tensor", "s_v")
        brd, ps_rd = psum.acquire()
        EM("tensor", lambda E, ps=ps_rd: E.matmul(ps, ones1, rstat[0:1, 0:512],
                                                  start=True, stop=True), inc="s_pe")
        W("scalar", "s_pe")
        vrd = prog.emit("scalar", lambda E, ps=ps_rd: E.copy(BCAST[:, 2, :], ps), inc="s_s")
        psum.release(brd, "s_s", vrd)
        W("vector", "s_s", vrd)
        for me in range(8):
            wsl = WS[me % 2]
            W("sync", "s_pe", ws_lastuse[me % 2])
            lv = ld(wsl, kv_s[:, me * 128:(me + 1) * 128].rearrange(
                "(dc p) e -> p dc e", p=128))
            W("tensor", "s_ld", lv)
            bi, ps = psum.acquire()
            for dc in range(8):
                EM("tensor", lambda E, ps=ps, wsl=wsl, dc=dc: E.matmul(
                    ps, wsl[:, dc, :], QP[:, dc, :],
                    start=(dc == 0), stop=(dc == 7)),
                    inc=("s_pe" if dc == 7 else None))
            ws_lastuse[me % 2] = prog.cnt["s_pe"]
            W("vector", "s_pe")
            ev(prog, lambda E, ps=ps, me=me: E.tensor_tensor(
                X1[:, me, :], ps, BCAST[:, 2, :], ALU.mult))
            vv = ev(prog, lambda E, me=me: E.tensor_tensor(
                X1[:, me, :], X1[:, me, :], XNB[:, me, :], ALU.add))
            psum.release(bi, "s_v", vv)
        if "x1r" in tap_out:
            W("sync", "s_v")
            prog.dma("sync", tap_out["x1r"][:, t0c:t1c].rearrange("(dc p) t -> p dc t", p=128),
                     A[3], "s_st")
            W("vector", "s_st")
        if "rden" in tap_out:
            W("sync", "s_v")
            prog.dma("sync", tap_out["rden"][0:1, t0c:t1c], rstat[0:1, 0:512], "s_st")
            W("vector", "s_st")
        W("tensor", "s_v")
        emit_fmln(X1, p_g1, p_b1)
        if "x1" in tap_out:
            W("sync", "s_v")
            prog.dma("sync", tap_out["x1"][:, t0c:t1c].rearrange("(dc p) t -> p dc t", p=128),
                     A[3], "s_st")
        FFN = A[0]
        W("vector", "s_v")
        EM("vector", lambda E: E.memset(Af[0], 0.0), inc="s_v")
        for grp in ffn_groups:
            for si, (h0, w, bc) in enumerate(grp):
                for which, wmat, pb in (("g", Wg, p_bg), ("u", Wu, p_bu)):
                    wsl = WS[0] if which == "g" else WS[1]
                    W("sync", "s_pe", ws_lastuse[0 if which == "g" else 1])
                    lv = ld(wsl[:, :, 0:w], wmat[:, h0:h0 + w].rearrange(
                        "(dc p) h -> p dc h", p=128))
                    W("tensor", "s_ld", lv)
                    bi, ps = psum.acquire()
                    for dc in range(8):
                        EM("tensor", lambda E, ps=ps, wsl=wsl, dc=dc, w=w: E.matmul(
                            ps[0:w, :], wsl[:, dc, 0:w], X1[:, dc, :],
                            start=(dc == 0), stop=(dc == 7)),
                            inc=("s_pe" if dc == 7 else None))
                    ws_lastuse[0 if which == "g" else 1] = prog.cnt["s_pe"]
                    W("scalar", "s_pe")
                    prog.wait("scalar", "s_v")
                    if which == "g":
                        vv = prog.emit("scalar", lambda E, ps=ps, w=w, bc=bc, pb=pb: E.activation(
                            GB[1][0:w, 0, 0:512], ps[0:w, :],
                            ACT.Silu, bias=pb[0:w, bc:bc + 1]), inc="s_s")
                        psum.release(bi, "s_s", vv)
                    else:
                        vv = prog.emit("scalar", lambda E, ps=ps, w=w, bc=bc, pb=pb: E.activation(
                            GB[1][0:w, 1, 0:512], ps[0:w, :],
                            ACT.Identity, bias=pb[0:w, bc:bc + 1]), inc="s_s")
                        psum.release(bi, "s_s", vv)
                W("vector", "s_s")
                W("vector", "s_pe")
                ev(prog, lambda E, si=si, w=w: E.tensor_tensor(
                    GB[0][0:w, si, 0:512], GB[1][0:w, 0, 0:512], GB[1][0:w, 1, 0:512],
                    ALU.mult))
            gh0 = grp[0][0]
            gw = sum(w for (_, w, _) in grp)
            nkh = len(grp)
            wd_stage = KVf[:, 0:4096].rearrange("p (kh d) -> p kh d", d=1024)
            W("sync", "s_v")
            W("sync", "s_pe")
            lvw = None
            for ki, (h0, w, _) in enumerate(grp):
                lvw = ld(wd_stage[0:w, ki, :], Wd[h0:h0 + w, :])
            W("tensor", "s_ld", lvw)
            W("tensor", "s_v")
            for md in range(8):
                bi, ps = psum.acquire()
                for ki, (h0, w, _) in enumerate(grp):
                    is_last = (ki == nkh - 1)
                    EM("tensor", lambda E, ps=ps, wd=wd_stage, ki=ki, md=md, w=w,
                            first=(ki == 0), last=is_last: E.matmul(
                        ps, wd[0:w, ki, md * 128:(md + 1) * 128],
                        GB[0][0:w, ki, 0:512],
                        start=first, stop=last),
                        inc=("s_pe" if is_last else None))
                W("vector", "s_pe")
                vv = ev(prog, lambda E, ps=ps, md=md: E.tensor_tensor(
                    FFN[:, md, :], FFN[:, md, :], ps, ALU.add))
                psum.release(bi, "s_v", vv)
        for dc in range(8):
            ev(prog, lambda E, dc=dc: E.tensor_scalar_add(
                FFN[:, dc, :], FFN[:, dc, :], p_bd[:, dc:dc + 1]))
            ev(prog, lambda E, dc=dc: E.tensor_tensor(
                FFN[:, dc, :], FFN[:, dc, :], X1[:, dc, :], ALU.add))
        W("tensor", "s_v")
        emit_fmln(FFN, p_g2, p_b2)
        if "s0" in tap_out:
            W("sync", "s_v")
            prog.dma("sync", tap_out["s0"][:, t0c:t1c].rearrange("(dc p) t -> p dc t", p=128),
                     A[0], "s_st")
            W("vector", "s_st")
        for dc in range(8):
            ev(prog, lambda E, dc=dc: E.tensor_tensor(
                GACC[:, dc, :], A[0][:, dc, :], GACC[:, dc, :], ALU.mult))
        XT2 = A[2]
        W("sync", "s_v")
        W("sync", "s_pe")
        lvx = ld(Af[2].rearrange("p (s d) -> p s d", d=D),
                 x_in[t0c:t1c, :].rearrange("(s p) d -> p s d", p=128))
        W("tensor", "s_ld", lvx)
        XT2f = Af[2]
        for dc in range(8):
            bi, ps = psum.acquire()
            for s in range(4):
                EM("tensor", lambda E, ps=ps, dc=dc, s=s: E.transpose(
                    ps[:, s * 128:(s + 1) * 128],
                    XT2f[:, s * D + dc * 128: s * D + (dc + 1) * 128], ident),
                    inc=("s_pe" if s == 3 else None))
            W("vector", "s_pe")
            vv = ev(prog, lambda E, ps=ps, dc=dc: E.tensor_tensor(
                GB[2][:, 0, 0:512] if False else GB[2].rearrange("p a b -> p (a b)")[:, 0:512],
                ps, GACC[:, dc, :], ALU.add))
            psum.release(bi, "s_v", vv)
            W("sync", "s_v", vv)
            prog.dma("sync", xnew_s[dc * 128:(dc + 1) * 128, t0c:t1c],
                     GB[2].rearrange("p a b -> p (a b)")[:, 0:512], "s_st")
            W("vector", "s_st")
        XNEW = A[2]
        W("sync", "s_pe")
        W("sync", "s_st")
        lvn = ld(XNEW, xnew_s[:, t0c:t1c].rearrange("(dc p) t -> p dc t", p=128))
        W("vector", "s_ld", lvn)
        W("tensor", "s_ld", lvn)
        W("scalar", "s_ld", lvn)
        emit_fmln(XNEW, None, None)
        W("sync", "s_v")
        st(xn1_s[:, t0c:t1c].rearrange("(dc p) t -> p dc t", p=128), XNEW)

    if stop_after == "phB":
        _finish(prog, nc, psum, out_hbm, tap_out, locals())
        return _close(nc, prog, stack, tap_out)

    EM("vector", lambda E: E.memset(halo.rearrange("p a b -> p (a b)"), 0.0), inc="s_v")
    for mt in range(NMT):
        t0c, t1c = mt * MT, (mt + 1) * MT
        XN1 = A[0]
        W("sync", "s_v")
        W("sync", "s_s")
        W("sync", "s_pe")
        W("sync", "s_st")
        lvx = ld(XN1, xn1_s[:, t0c:t1c].rearrange("(dc p) t -> p dc t", p=128))
        W("tensor", "s_ld", lvx)
        XD = BCAST[0:96, 3, :]
        W("vector", "s_pe")
        W("vector", "s_st")
        EM("vector", lambda E, XD=XD: E.memset(XD, 0.0), inc="s_v")
        for grp in range(4):
            for sub in range(4):
                idx = grp * 4 + sub
                wsl = WS[idx % 2]
                W("sync", "s_pe", ws_lastuse[idx % 2])
                lv = ld(wsl, inpT_s[:, idx * 128:(idx + 1) * 128].rearrange(
                    "(dc p) e -> p dc e", p=128))
                W("tensor", "s_ld", lv)
                bi, ps = psum.acquire()
                for dc in range(8):
                    EM("tensor", lambda E, ps=ps, wsl=wsl, dc=dc, XN1=XN1: E.matmul(
                        ps, wsl[:, dc, :], XN1[:, dc, :],
                        start=(dc == 0), stop=(dc == 7)),
                        inc=("s_pe" if dc == 7 else None))
                ws_lastuse[idx % 2] = prog.cnt["s_pe"]
                W("scalar", "s_pe")
                prog.wait("scalar", "s_v")
                vv = prog.emit("scalar", lambda E, ps=ps, sub=sub: E.copy(
                    GB[0][:, sub, 3:515], ps), inc="s_s")
                psum.release(bi, "s_s", vv)
                prog.emit("scalar", lambda E, sub=sub, idx=idx: E.copy(
                    GB[0][:, sub, 0:3], halo[:, idx, 0:3]), inc="s_s")
                vhl = prog.emit("scalar", lambda E, sub=sub, idx=idx: E.copy(
                    halo[:, idx, 0:3], GB[0][:, sub, 512:515]), inc="s_s")
                W("vector", "s_s", vhl)
                XC = GB[1]
                W("vector", "s_st")
                ev(prog, lambda E, sub=sub, idx=idx: E.tensor_scalar(
                    XC[:, sub, 0:512], GB[0][:, sub, 0:512],
                    p_convw[:, idx, 0:1], None, ALU.mult))
                for j in range(1, 4):
                    ev(prog, lambda E, sub=sub, idx=idx, j=j: E.scalar_tensor_tensor(
                        XC[:, sub, 0:512], GB[0][:, sub, j:j + 512],
                        p_convw[:, idx, j:j + 1], XC[:, sub, 0:512],
                        ALU.mult, ALU.add))
                prog.wait("scalar", "s_v")
                vsl = prog.emit("scalar", lambda E, sub=sub, idx=idx: E.activation(
                    XC[:, sub, 0:512], XC[:, sub, 0:512], ACT.Silu,
                    bias=p_convb[:, idx:idx + 1]), inc="s_s")
            W("sync", "s_s")
            st(xc_s[grp * 512:(grp + 1) * 512, t0c:t1c].rearrange(
                "(s p) t -> p s t", p=128), GB[1][:, :, 0:512])
            bi, ps = psum.acquire()
            W("tensor", "s_s")
            for sub in range(4):
                idx = grp * 4 + sub
                EM("tensor", lambda E, ps=ps, idx=idx, sub=sub: E.matmul(
                    ps[0:96, :], XPW[:, idx, :], GB[1][:, sub, 0:512],
                    start=(sub == 0), stop=(sub == 3)),
                    inc=("s_pe" if sub == 3 else None))
            W("vector", "s_pe")
            vv = ev(prog, lambda E, ps=ps, XD=XD: E.tensor_tensor(
                XD, XD, ps[0:96, :], ALU.add))
            psum.release(bi, "s_v", vv)
            for sub in range(4):
                idx = 16 + grp * 4 + sub
                wsl = WS[idx % 2]
                W("sync", "s_pe", ws_lastuse[idx % 2])
                lv = ld(wsl, inpT_s[:, idx * 128:(idx + 1) * 128].rearrange(
                    "(dc p) e -> p dc e", p=128))
                W("tensor", "s_ld", lv)
                bi, ps = psum.acquire()
                for dc in range(8):
                    EM("tensor", lambda E, ps=ps, wsl=wsl, dc=dc, XN1=XN1: E.matmul(
                        ps, wsl[:, dc, :], XN1[:, dc, :],
                        start=(dc == 0), stop=(dc == 7)),
                        inc=("s_pe" if dc == 7 else None))
                ws_lastuse[idx % 2] = prog.cnt["s_pe"]
                W("scalar", "s_pe")
                prog.wait("scalar", "s_st")
                vv = prog.emit("scalar", lambda E, ps=ps, sub=sub: E.activation(
                    GB[2][:, sub, 0:512], ps, ACT.Silu), inc="s_s")
                psum.release(bi, "s_s", vv)
            W("sync", "s_s")
            st(zs_s[grp * 512:(grp + 1) * 512, t0c:t1c].rearrange(
                "(s p) t -> p s t", p=128), GB[2][:, :, 0:512])
        W("sync", "s_v")
        st(bc_s[:, t0c:t1c], BCAST[64:96, 3, :])
        for mdi in range(16):
            bi, ps = psum.acquire()
            EM("tensor", lambda E, ps=ps, mdi=mdi: E.matmul(
                ps, DTW[:, mdi * 128:(mdi + 1) * 128], XD[0:64, :],
                start=True, stop=True), inc="s_pe")
            W("scalar", "s_pe")
            prog.wait("scalar", "s_st")
            vv = prog.emit("scalar", lambda E, ps=ps, mdi=mdi: E.activation(
                GB[3][:, 0, 0:512], ps, ACT.Identity,
                bias=p_dtpb[:, mdi:mdi + 1]), inc="s_s")
            psum.release(bi, "s_s", vv)
            W("vector", "s_s", vv)
            emit_softplus(prog, GB[3][:, 0, 0:512], GB[3][:, 1, 0:512], GB[3][:, 0, 0:512])
            ev(prog, lambda E, mdi=mdi: E.tensor_scalar_add(
                GB[3][:, 0, 0:512], GB[3][:, 0, 0:512], p_dtpb[:, mdi:mdi + 1]))
            emit_softplus(prog, GB[3][:, 0, 0:512], GB[3][:, 1, 0:512], GB[3][:, 0, 0:512])
            W("sync", "s_v")
            st(delta_s[mdi * 128:(mdi + 1) * 128, t0c:t1c], GB[3][:, 0, 0:512])

    if stop_after == "phC1":
        _finish(prog, nc, psum, out_hbm, tap_out, locals())
        return _close(nc, prog, stack, tap_out)

    for mt in range(NMT):
        t0c, t1c = mt * MT, (mt + 1) * MT
        XN1 = A[0]
        G2 = A[2]
        HE = A[1]
        W("sync", "s_v")
        W("sync", "s_s")
        W("sync", "s_pe")
        W("sync", "s_st")
        lvx = ld(XN1, xn1_s[:, t0c:t1c].rearrange("(dc p) t -> p dc t", p=128))
        W("vector", "s_pe")
        W("vector", "s_s")
        W("vector", "s_st")
        EM("vector", lambda E: E.memset(Af[2], 0.0), inc="s_v")
        bi, ps = psum.acquire()
        W("tensor", "s_ld", lvx)
        for dc in range(8):
            EM("tensor", lambda E, ps=ps, dc=dc, XN1=XN1: E.matmul(
                ps[0:8, :], p_mwg[:, dc, :], XN1[:, dc, :],
                start=(dc == 0), stop=(dc == 7)),
                inc=("s_pe" if dc == 7 else None))
        W("scalar", "s_pe")
        vlg = prog.emit("scalar", lambda E, ps=ps: E.activation(
            LOG, ps[0:8, :], ACT.Identity, bias=p_mbg), inc="s_s")
        psum.release(bi, "s_s", vlg)
        W("vector", "s_s", vlg)
        EM("vector", lambda E: E.memset(ONES8.rearrange("p a b -> p (a b)"), 1.0), inc="s_v")
        W("tensor", "s_s", vlg)
        bi2, ps2 = psum.acquire()
        for s in range(4):
            EM("tensor", lambda E, ps=ps2, s=s: E.transpose(
                ps[0:128, s * 8:(s + 1) * 8], LOG[0:8, s * 128:(s + 1) * 128],
                ident[0:8, 0:8]), inc=("s_pe" if s == 3 else None))
        W("vector", "s_pe")
        vtr = ev(prog, lambda E, ps=ps2: E.tensor_copy(
            LGT.rearrange("p a b -> p (a b)"), ps[:, 0:32]))
        psum.release(bi2, "s_pe" if False else "s_v", vtr)
        for s in range(4):
            ev(prog, lambda E, s=s: E.reduce_max(MXC[:, s:s + 1], LGT[:, s, :], axis=AX.X))
        for s in range(4):
            ev(prog, lambda E, s=s: E.tensor_scalar(
                LGT[:, s, :], LGT[:, s, :], MXC[:, s:s + 1], None, ALU.is_ge))
        for s in range(4):
            ev(prog, lambda E, s=s: E.tensor_tensor_scan(
                MTOK[:, s, :], ONES8[:, s, :], LGT[:, s, :], 0.0, ALU.mult, ALU.add))
        for s in range(4):
            ev(prog, lambda E, s=s: E.tensor_tensor(
                MTOK[:, s, :], MTOK[:, s, :], LGT[:, s, :], ALU.subtract))
        for s in range(4):
            ev(prog, lambda E, s=s: E.tensor_single_scalar(
                MTOK[:, s, :], MTOK[:, s, :], 0.5, ALU.is_le))
        for s in range(4):
            ev(prog, lambda E, s=s: E.tensor_tensor(
                MTOK[:, s, :], MTOK[:, s, :], LGT[:, s, :], ALU.mult))
        for e in range(NE):
            W("tensor", "s_v")
            bb0, psb0 = psum.acquire()
            for s in range(4):
                EM("tensor", lambda E, ps=psb0, s=s, e=e: E.transpose(
                    ps[0:1, s * 128:(s + 1) * 128], MTOK[:, s, e:e + 1],
                    ident), inc=("s_pe" if s == 3 else None))
            W("scalar", "s_pe")
            vmr = prog.emit("scalar", lambda E, ps=psb0: E.copy(
                rstat[0:1, 1536:2048], ps[0:1, 0:512]), inc="s_s")
            psum.release(bb0, "s_s", vmr)
            if "mrow" in tap_out and mt == NMT - 1:
                W("sync", "s_s", vmr)
                prog.dma("sync", tap_out["mrow"][e:e + 1, :], rstat[0:1, 1536:2048], "s_st")
                W("scalar", "s_st")
            W("tensor", "s_s", vmr)
            bb, psb = psum.acquire()
            EM("tensor", lambda E, ps=psb: E.matmul(
                ps, ones1, rstat[0:1, 1536:2048], start=True, stop=True), inc="s_pe")
            W("scalar", "s_pe")
            vmb = prog.emit("scalar", lambda E, ps=psb: E.copy(BCAST[:, 2, :], ps), inc="s_s")
            psum.release(bb, "s_s", vmb)
            for mi in range(8):
                wsl = WS[mi % 2]
                W("sync", "s_pe", ws_lastuse[mi % 2])
                lv = ld(wsl, mW1[e, :, mi * 128:(mi + 1) * 128].rearrange(
                    "(dc p) i -> p dc i", p=128))
                W("tensor", "s_ld", lv)
                bi, ps = psum.acquire()
                for dc in range(8):
                    EM("tensor", lambda E, ps=ps, wsl=wsl, dc=dc, XN1=XN1: E.matmul(
                        ps, wsl[:, dc, :], XN1[:, dc, :],
                        start=(dc == 0), stop=(dc == 7)),
                        inc=("s_pe" if dc == 7 else None))
                ws_lastuse[mi % 2] = prog.cnt["s_pe"]
                W("scalar", "s_pe")
                prog.wait("scalar", "s_pe")
                vv = prog.emit("scalar", lambda E, ps=ps, e=e, mi=mi: E.activation(
                    HE[:, mi, :], ps, ACT.Gelu, bias=p_mb1[:, e, mi:mi + 1]), inc="s_s")
                psum.release(bi, "s_s", vv)
            W("tensor", "s_s")
            for md in range(8):
                wsl = WS[md % 2]
                W("sync", "s_pe", ws_lastuse[md % 2])
                lv = ld(wsl, mW2[e, :, md * 128:(md + 1) * 128].rearrange(
                    "(ic p) h -> p ic h", p=128))
                W("tensor", "s_ld", lv)
                bi, ps = psum.acquire()
                for ic in range(8):
                    EM("tensor", lambda E, ps=ps, wsl=wsl, ic=ic: E.matmul(
                        ps, wsl[:, ic, :], HE[:, ic, :],
                        start=(ic == 0), stop=(ic == 7)),
                        inc=("s_pe" if ic == 7 else None))
                ws_lastuse[md % 2] = prog.cnt["s_pe"]
                W("scalar", "s_pe")
                prog.wait("scalar", "s_v")
                vv = prog.emit("scalar", lambda E, ps=ps, e=e, md=md: E.activation(
                    GB[0][:, 0, 0:512], ps, ACT.Identity,
                    bias=p_mb2[:, e, md:md + 1]), inc="s_s")
                psum.release(bi, "s_s", vv)
                W("vector", "s_s", vv)
                ev(prog, lambda E, md=md: E.tensor_tensor(
                    GB[0][:, 0, 0:512], GB[0][:, 0, 0:512], BCAST[:, 2, :], ALU.mult))
                ev(prog, lambda E, md=md, G2=G2: E.tensor_tensor(
                    G2[:, md, :], G2[:, md, :], GB[0][:, 0, 0:512], ALU.add))
        if "log8" in tap_out and mt == NMT - 1:
            W("sync", "s_s")
            prog.dma("sync", tap_out["log8"], LOG, "s_st")
        if "mtok" in tap_out and mt == NMT - 1:
            W("sync", "s_v")
            prog.dma("sync", tap_out["mtok"], MTOK.rearrange("p a b -> p (a b)"), "s_st")
        W("scalar", "s_v")
        vs2 = prog.emit("scalar", lambda E: E.activation(Af[2], Af[2], ACT.Silu), inc="s_s")
        W("sync", "s_s", vs2)
        st(g2_s[:, t0c:t1c].rearrange("(dc p) t -> p dc t", p=128), G2)

    if stop_after == "phD":
        _finish(prog, nc, psum, out_hbm, tap_out, locals())
        return _close(nc, prog, stack, tap_out)

    TH = NT // 2
    for dc in range(16):
        DLT = Af[0]
        XCC = Af[1]
        DU = Af[2]
        YA = Af[3]
        W("sync", "s_v")
        W("sync", "s_s")
        W("sync", "s_pe")
        W("sync", "s_st")
        lvd = ld(DLT.rearrange("p (h x) -> p h x", h=2), delta_s[dc * 128:(dc + 1) * 128, :].rearrange("p (h x) -> p h x", h=2))
        lvc = ld(XCC.rearrange("p (h x) -> p h x", h=2), xc_s[dc * 128:(dc + 1) * 128, :].rearrange("p (h x) -> p h x", h=2))
        W("vector", "s_ld", lvc)
        W("scalar", "s_ld", lvc)
        ev(prog, lambda E: E.tensor_tensor(DU, DLT, XCC, ALU.mult))
        W("vector", "s_st")
        EM("vector", lambda E: E.memset(YA, 0.0), inc="s_v")
        EB = GBf[0][:, 0:2048]
        FB = GBf[1][:, 0:2048]
        SCO = GBf[2][:, 0:2048]
        for half in range(2):
            hsl = slice(half * TH, (half + 1) * TH)
            for n in range(DS):
                W("sync", "s_pe")
                lvb = prog.dma("sync", rowbuf[0:1, 0:2048],
                               bc_s[n:n + 1, half * TH:(half + 1) * TH], "s_ld")
                W("tensor", "s_ld", lvb)
                W("tensor", "s_v")
                for q in range(4):
                    EM("tensor", lambda E, q=q: E.matmul(
                        psBC[:, q * 512:(q + 1) * 512], ones1,
                        rowbuf[0:1, q * 512:(q + 1) * 512],
                        start=True, stop=True),
                        inc=("s_pe" if q == 3 else None))
                prog.wait("scalar", "s_v")
                vse = prog.emit("scalar", lambda E, dc=dc, n=n, hsl=hsl: E.activation(
                    EB, DLT[:, hsl], ACT.Exp, scale=p_aneg[:, dc, n:n + 1]), inc="s_s")
                W("vector", "s_pe")
                ev(prog, lambda E, hsl=hsl: E.tensor_tensor(FB, DU[:, hsl], psBC, ALU.mult))
                W("vector", "s_s", vse)
                if half == 0:
                    ev(prog, lambda E: E.tensor_tensor_scan(SCO, EB, FB, 0.0, ALU.mult, ALU.add))
                    ev(prog, lambda E, n=n: E.tensor_copy(carry[:, n:n + 1], SCO[:, TH - 1:TH]))
                else:
                    ev(prog, lambda E, n=n: E.tensor_tensor_scan(
                        SCO, EB, FB, carry[:, n:n + 1], ALU.mult, ALU.add))
                W("sync", "s_pe")
                lvb2 = prog.dma("sync", rowbuf[0:1, 0:2048],
                                bc_s[DS + n:DS + n + 1, half * TH:(half + 1) * TH], "s_ld")
                W("tensor", "s_ld", lvb2)
                W("tensor", "s_v")
                for q in range(4):
                    EM("tensor", lambda E, q=q: E.matmul(
                        psBC[:, q * 512:(q + 1) * 512], ones1,
                        rowbuf[0:1, q * 512:(q + 1) * 512],
                        start=True, stop=True),
                        inc=("s_pe" if q == 3 else None))
                W("vector", "s_pe")
                ev(prog, lambda E: E.tensor_tensor(FB, SCO, psBC, ALU.mult))
                ev(prog, lambda E, hsl=hsl: E.tensor_tensor(YA[:, hsl], YA[:, hsl], FB, ALU.add))
        ev(prog, lambda E, dc=dc: E.scalar_tensor_tensor(
            YA, XCC, p_dpar[:, dc:dc + 1], YA, ALU.mult, ALU.add))
        ZSC = Af[1]
        W("sync", "s_v")
        lvz = ld(ZSC.rearrange("p (h x) -> p h x", h=2), zs_s[dc * 128:(dc + 1) * 128, :].rearrange("p (h x) -> p h x", h=2))
        W("vector", "s_ld", lvz)
        vy = ev(prog, lambda E: E.tensor_tensor(YA, YA, ZSC, ALU.mult))
        W("sync", "s_v", vy)
        st(y_s[dc * 128:(dc + 1) * 128, :].rearrange("p (h x) -> p h x", h=2),
           YA.rearrange("p (h x) -> p h x", h=2))

    if stop_after == "phC2":
        _finish(prog, nc, psum, out_hbm, tap_out, locals())
        return _close(nc, prog, stack, tap_out)

    for mt in range(NMT):
        t0c, t1c = mt * MT, (mt + 1) * MT
        W("sync", "s_v")
        W("sync", "s_s")
        W("sync", "s_pe")
        W("sync", "s_st")
        Y0, Y1 = A[0], A[1]
        XNEW = A[2]
        G2 = A[3]
        lv0 = ld(Y0, y_s[0:1024, t0c:t1c].rearrange("(dc p) t -> p dc t", p=128))
        lv1 = ld(Y1, y_s[1024:2048, t0c:t1c].rearrange("(dc p) t -> p dc t", p=128))
        lv2 = ld(XNEW, xnew_s[:, t0c:t1c].rearrange("(dc p) t -> p dc t", p=128))
        lv3 = ld(G2, g2_s[:, t0c:t1c].rearrange("(dc p) t -> p dc t", p=128))
        import os as _os
        if _os.environ.get("PHC3_LOADS_ONLY"):
            continue
        for me in range(8):
            wsl = WB[me % 2]
            W("sync", "s_pe", ws_lastuse[me % 2])
            lvw = prog.dma("sync", wsl.rearrange("p (k m) -> p k m", m=128),
                           outpT_s[:, me * 128:(me + 1) * 128].rearrange(
                               "(k p) m -> p k m", p=128), "s_ld")
            W("tensor", "s_ld", lvw)
            bi, ps = psum.acquire()
            for k in range(16):
                EM("tensor", lambda E, ps=ps, wsl=wsl, k=k: E.matmul(
                    ps, wsl[:, k * 128:(k + 1) * 128],
                    (Y0 if k < 8 else Y1)[:, k % 8, :],
                    start=(k == 0), stop=(k == 15)),
                    inc=("s_pe" if k == 15 else None))
            ws_lastuse[me % 2] = prog.cnt["s_pe"]
            W("vector", "s_pe")
            FIN = GB[0].rearrange("p a b -> p (a b)")[:, 0:512]
            W("vector", "s_st")
            ev(prog, lambda E, ps=ps, me=me, FIN=FIN, G2=G2: E.tensor_tensor(
                FIN, ps, G2[:, me, :], ALU.mult))
            vv = ev(prog, lambda E, me=me, FIN=FIN, XNEW=XNEW: E.tensor_tensor(
                FIN, FIN, XNEW[:, me, :], ALU.add))
            psum.release(bi, "s_v", vv)
            W("tensor", "s_v", vv)
            bi2, ps2 = psum.acquire()
            for s in range(4):
                EM("tensor", lambda E, ps=ps2, FIN=FIN, s=s: E.transpose(
                    ps[:, s * 128:(s + 1) * 128], FIN[:, s * 128:(s + 1) * 128],
                    ident), inc=("s_pe" if s == 3 else None))
            W("scalar", "s_pe")
            prog.wait("scalar", "s_st")
            OUTT = GB[1].rearrange("p a b -> p (a b)")[:, 0:512]
            vv2 = prog.emit("scalar", lambda E, ps=ps2, OUTT=OUTT: E.copy(OUTT, ps), inc="s_s")
            psum.release(bi2, "s_s", vv2)
            W("sync", "s_s", vv2)
            for s in range(4):
                prog.dma("sync",
                         out_hbm[t0c + s * 128:t0c + (s + 1) * 128,
                                 me * 128:(me + 1) * 128],
                         OUTT[0:128, s * 128:(s + 1) * 128], "s_st")
            W("scalar", "s_st")

    out_written = True
    _finish(prog, nc, psum, out_hbm, tap_out, locals())
    return _close(nc, prog, stack, tap_out)


def _finish(prog, nc, psum, out_hbm, tap_out, L):
    W = prog.wait
    prog.steps["sync"]
    bounce = L["Af"][0]
    srcmap = {"xn": L["xn_s"], "qp": L["qp_s"], "kv": L["kv_s"], "ksum": L["ksum_s"],
              "w1T": L["w1T_s"], "w2qT": L["w2qT_s"],
              "xnew": L["xnew_s"], "xn1": L["xn1_s"], "xc": L["xc_s"],
              "zs": L["zs_s"], "delta": L["delta_s"], "bc": L["bc_s"],
              "yts": L["y_s"], "g2": L["g2_s"]}
    for s in ("s_v", "s_s", "s_pe", "s_st", "s_ld"):
        W("sync", s)
    sbmap = {"sb_log": L.get("LOG"), "sb_bcast2": None, "sb_ones1": L.get("ones1")}
    if "MTOK" in L:
        sbmap["sb_mtok"] = L["MTOK"].rearrange("p a b -> p (a b)")
        sbmap["sb_lgt"] = L["LGT"].rearrange("p a b -> p (a b)")
        sbmap["sb_mrow"] = L["rstat"][0:1, 1536:2048]
        sbmap["sb_bcast2"] = L["BCAST"][:, 2, :]
    for name, dst in tap_out.items():
        src = sbmap.get(name)
        if src is not None:
            prog.dma("sync", dst, src, "s_st")
            W("sync", "s_st")
    for name, dst in tap_out.items():
        src = srcmap.get(name)
        if src is None:
            continue
        R, C = src.shape
        for r0 in range(0, R, 128):
            rr = min(128, R - r0)
            lv = prog.dma("sync", bounce[0:rr, 0:C], src[r0:r0 + rr, :], "s_ld")
            W("sync", "s_ld", lv)
            prog.dma("sync", dst[r0:r0 + rr, :], bounce[0:rr, 0:C], "s_st")
            W("sync", "s_st")
    if not L.get("out_written"):
        zb = L["Af"][1]
        prog.emit("vector", lambda E, zb=zb: E.memset(zb, 0.0), inc="s_v")
        W("sync", "s_v")
        for mt in range(NMT):
            prog.dma("sync",
                     out_hbm[mt * MT:(mt + 1) * MT, :].rearrange("(s p) d -> p s d", p=128),
                     zb.rearrange("p (s d) -> p s d", d=D)[:, 0:4, :], "s_st")
    W("sync", "s_st")


def _close(nc, prog, stack, tap_out):
    with nc.Block() as block:
        prog.finalize(block)
    return nc, list(tap_out)



def kernel(**inputs):
    xfull = np.ascontiguousarray(np.asarray(inputs["L0_attn_x"], dtype=np.float32))
    B = xfull.shape[0]
    nc, taps = build(stop_after=STOP_AFTER, taps=TAPS)
    ident = np.eye(128, dtype=np.float32)
    base = {k: np.ascontiguousarray(np.asarray(v, dtype=np.float32))
            for k, v in inputs.items() if k != "L0_attn_x"}
    in_maps = []
    for c in range(B):
        m = dict(base)
        m["x"] = xfull[c]
        m["ident"] = ident
        in_maps.append(m)
    res = run_bass_kernel_spmd(nc, in_maps, core_ids=list(range(B)))
    out = np.stack([r["out"] for r in res.results], axis=0)
    kernel.last_results = res.results
    return out


# version 05
# speedup vs baseline: 1.0186x; 1.0186x over previous
import math
import contextlib
import numpy as np

import concourse.bass as bass
import concourse.mybir as mybir
from concourse.bass_utils import run_bass_kernel_spmd

F32 = mybir.dt.float32
BF16 = mybir.dt.bfloat16
ALU = mybir.AluOpType
ACT = mybir.ActivationFunctionType
AX = mybir.AxisListType

NT = 4096
D = 1024
MT = 512
NMT = NT // MT
E3 = 3072
FH = 2730
L0H = 4096
DI = 2048
DS = 16
DCONV = 4
DTR = 64
NE = 8
LN2HI = float(np.float32(0.6931471805599453))
LN2LO = float(np.float32(0.6931471805599453 - np.float64(np.float32(0.6931471805599453))))
LOG2E = float(np.float32(1.4426950408889634))
FLT_MIN = 1.17549435e-38

ENGS = ("sync", "vector", "scalar", "tensor", "gpsimd")

STOP_AFTER = None
TAPS = ()
DBG_OUT_SCRATCH = set()


class Prog:

    DMA_ENG = {"s_ld": "sync", "s_st": "sync", "s_g": "gpsimd"}

    def __init__(self, nc):
        self.nc = nc
        self.steps = {e: [] for e in ENGS}
        self.sem = {}
        self.cnt = {}
        self.waited = {e: {} for e in ENGS}
        self.pending_gate = {}

    def new_sem(self, name):
        self.sem[name] = self.nc.alloc_semaphore(name)
        self.cnt[name] = 0
        self.pending_gate[name] = 0

    def wait(self, eng, name, val=None):
        v = self.cnt[name] if val is None else val
        if v <= 0 or self.waited[eng].get(name, 0) >= v:
            return
        if name in self.DMA_ENG:
            self.pending_gate[name] = max(self.pending_gate[name], v)
        self.waited[eng][name] = v
        h = self.sem[name]
        self.steps[eng].append(lambda E, h=h, v=v: E.wait_ge(h, v))

    def emit(self, eng, fn, inc=None, inc_n=1):
        if inc is None:
            self.steps[eng].append(fn)
            return None
        self.cnt[inc] += inc_n
        h = self.sem[inc]

        def wrapped(E, fn=fn, h=h, n=inc_n):
            ins = fn(E)
            ins.then_inc(h, n)
        self.steps[eng].append(wrapped)
        return self.cnt[inc]

    def dma(self, eng, out, in_, inc, slow=False):
        g = self.pending_gate.get(inc, 0)
        if g > self.waited[eng].get(inc, 0):
            self.waited[eng][inc] = g
            h0 = self.sem[inc]
            self.steps[eng].append(lambda E, h=h0, v=g: E.wait_ge(h, v))
        if slow:
            return self.emit(eng, lambda E, o=out, i=in_: E.dma_start(
                out=o, in_=i, allow_slow_non_contiguous=True), inc=inc, inc_n=16)
        return self.emit(eng, lambda E, o=out, i=in_: E.dma_start(out=o, in_=i),
                         inc=inc, inc_n=16)

    def finalize(self, block):
        for e in ENGS:
            items = self.steps[e]
            if not items:
                continue
            def run(E, items=items):
                for it in items:
                    it(E)
            getattr(block, e)(run)


class PsumPool:

    def __init__(self, prog, nbanks=4):
        self.prog = prog
        nc = prog.nc
        self.tiles = [nc.alloc_psum_tensor(f"pb{i}", [128, 512], F32).ap()
                      for i in range(nbanks)]
        self.free_at = [None] * nbanks
        self.rr = 0
        self.n = nbanks

    def acquire(self):
        i = self.rr % self.n
        self.rr += 1
        if self.free_at[i] is not None:
            sem, val = self.free_at[i]
            self.prog.wait("tensor", sem, val)
        return i, self.tiles[i]

    def release(self, i, sem, val):
        self.free_at[i] = (sem, val)

    def manual(self, i):
        return self.tiles[i]


def ceil_div(a, b):
    return (a + b - 1) // b



def ev(prog, fn, inc="s_v", eng="vector"):
    return prog.emit(eng, fn, inc=inc)


def emit_elup1(prog, z_ap, tmp_ap, out_ap, ksum_ap=None):
    ev(prog, lambda E: E.tensor_scalar_min(tmp_ap, z_ap, 0.0))
    prog.wait("scalar", "s_v")
    prog.emit("scalar", lambda E: E.activation(tmp_ap, tmp_ap, ACT.Exp), inc="s_s")
    prog.wait("vector", "s_s")
    if ksum_ap is None:
        ev(prog, lambda E: E.scalar_tensor_tensor(out_ap, z_ap, 0.0, tmp_ap, ALU.max, ALU.add))
    else:
        ev(prog, lambda E: E.scalar_tensor_tensor(out_ap, z_ap, 0.0, tmp_ap, ALU.max, ALU.add,
                                                  accum_out=ksum_ap))


def emit_softplus(prog, x_ap, tmp_ap, out_ap):
    prog.wait("scalar", "s_v")
    prog.emit("scalar", lambda E: E.activation(tmp_ap, x_ap, ACT.Abs), inc="s_s")
    prog.emit("scalar", lambda E: E.activation(tmp_ap, tmp_ap, ACT.Exp, scale=-1.0), inc="s_s")
    prog.emit("scalar", lambda E: E.activation(tmp_ap, tmp_ap, ACT.Ln, bias=1.0), inc="s_s")
    prog.wait("vector", "s_s")
    ev(prog, lambda E: E.scalar_tensor_tensor(out_ap, x_ap, 0.0, tmp_ap, ALU.max, ALU.add))


def emit_quantize(prog, x_ap, t, out_ap, man, min_e, max_val, signed=True,
                  nb_e=4, sat=1000.0):
    T0, T1, T2, T3, T4, T5 = t[:6]
    nb_r = man + 2
    prog.wait("scalar", "s_v")
    if signed:
        prog.emit("scalar", lambda E: E.activation(T0, x_ap, ACT.Sign), inc="s_s")
    prog.emit("scalar", lambda E: E.activation(T1, x_ap, ACT.Abs), inc="s_s")
    prog.wait("vector", "s_s")
    ev(prog, lambda E: E.tensor_scalar(T1, T1, FLT_MIN, float(sat), ALU.max, ALU.min))
    prog.wait("scalar", "s_v")
    prog.emit("scalar", lambda E: E.activation(T1, T1, ACT.Ln), inc="s_s")
    prog.wait("vector", "s_s")
    ev(prog, lambda E: E.tensor_scalar(T1, T1, LOG2E, float(-min_e), ALU.mult, ALU.add))
    ev(prog, lambda E: E.tensor_scalar(T3, T1, 0.0, float(2 ** nb_e) - 1e-4,
                                       ALU.max, ALU.min))
    ev(prog, lambda E: E.tensor_copy(T1, T3))
    for k in range(nb_e - 1, -1, -1):
        b = float(1 << k)
        ev(prog, lambda E, b=b: E.tensor_scalar(T4, T1, b, b, ALU.is_ge, ALU.mult))
        ev(prog, lambda E: E.tensor_tensor(T1, T1, T4, ALU.subtract))
    ev(prog, lambda E: E.tensor_tensor(T2, T3, T1, ALU.subtract))
    C = float(man - min_e)
    ev(prog, lambda E: E.tensor_scalar(T1, T2, -1.0, C, ALU.mult, ALU.add))
    ev(prog, lambda E: E.tensor_scalar(T3, T1, LN2HI, None, ALU.mult))
    ev(prog, lambda E: E.scalar_tensor_tensor(T3, T1, LN2LO, T3, ALU.mult, ALU.add))
    prog.wait("scalar", "s_v")
    prog.emit("scalar", lambda E: E.activation(T1, T3, ACT.Exp), inc="s_s")
    prog.emit("scalar", lambda E: E.activation(T3, T3, ACT.Exp, scale=-1.0), inc="s_s")
    prog.wait("vector", "s_s")
    ev(prog, lambda E: E.tensor_tensor(T1, x_ap, T1, ALU.mult))
    if signed:
        prog.wait("scalar", "s_v")
        prog.emit("scalar", lambda E: E.activation(T1, T1, ACT.Abs), inc="s_s")
        prog.wait("vector", "s_s")
        ev(prog, lambda E: E.tensor_scalar(T1, T1, 0.5 + 1e-9, None, ALU.add))
    else:
        ev(prog, lambda E: E.tensor_scalar(T1, T1, 0.5 + 1e-9, None, ALU.add))
    ev(prog, lambda E: E.tensor_copy(T2, T1))
    for k in range(nb_r - 1, -1, -1):
        b = float(1 << k)
        ev(prog, lambda E, b=b: E.tensor_scalar(T4, T1, b, b, ALU.is_ge, ALU.mult))
        ev(prog, lambda E: E.tensor_tensor(T1, T1, T4, ALU.subtract))
    ev(prog, lambda E: E.tensor_single_scalar(T5, T1, 2e-9, ALU.is_le))
    ev(prog, lambda E: E.tensor_tensor(T5, T5, T4, ALU.mult))
    ev(prog, lambda E: E.tensor_tensor(T1, T1, T5, ALU.add))
    ev(prog, lambda E: E.tensor_tensor(T1, T2, T1, ALU.subtract))
    ev(prog, lambda E: E.tensor_tensor(T1, T1, T3, ALU.mult))
    ovf_thr = max_val * (1.0 + 2.0 ** -(man + 2))
    ev(prog, lambda E: E.tensor_single_scalar(T2, T1, ovf_thr, ALU.is_gt))
    if signed:
        ev(prog, lambda E: E.tensor_tensor(T1, T1, T0, ALU.mult))
    prog.wait("scalar", "s_v")
    prog.emit("scalar", lambda E: E.activation(T5, T2, ACT.Exp, scale=710.0), inc="s_s")
    prog.wait("vector", "s_s")
    ev(prog, lambda E: E.tensor_tensor(out_ap, T1, T5, ALU.mult))



def build(stop_after=None, taps=()):
    nc = bass.Bass("TRN2", target_bir_lowering=False,
                   detect_race_conditions=False)
    stack = contextlib.ExitStack()

    def din(name, shape):
        return nc.dram_tensor(name, list(shape), F32, kind="ExternalInput").ap()

    x_in = din("x", (NT, D))
    Wqkv = din("L0_attn_Wqkv", (D, E3))
    bqkv = din("L0_attn_bqkv", (E3,))
    Wg = din("L0_attn_Wg", (D, FH))
    bg = din("L0_attn_bg", (FH,))
    Wu = din("L0_attn_Wu", (D, FH))
    bu = din("L0_attn_bu", (FH,))
    Wd = din("L0_attn_Wd", (FH, D))
    bd = din("L0_attn_bd", (D,))
    g1 = din("L0_attn_g1", (D,))
    b1 = din("L0_attn_b1", (D,))
    g2 = din("L0_attn_g2", (D,))
    b2 = din("L0_attn_b2", (D,))
    fw1 = din("L0_ffn_w1", (L0H, D))
    fb1 = din("L0_ffn_b1", (L0H,))
    fw2 = din("L0_ffn_w2", (D, L0H))
    fb2 = din("L0_ffn_b2", (D,))
    inpw = din("L1_attn_in_proj_w", (2 * DI, D))
    convw = din("L1_attn_conv_w", (DI, DCONV))
    convb = din("L1_attn_conv_b", (DI,))
    xpw = din("L1_attn_x_proj_w", (DTR + 2 * DS, DI))
    dtpw = din("L1_attn_dt_proj_w", (DI, DTR))
    dtpb = din("L1_attn_dt_proj_b", (DI,))
    alog = din("L1_attn_A_log", (DI, DS))
    dpar = din("L1_attn_D", (DI,))
    outpw = din("L1_attn_out_proj_w", (D, DI))
    mWg = din("L1_ffn_Wg", (D, NE))
    mbg = din("L1_ffn_bg", (NE,))
    mW1 = din("L1_ffn_W1", (NE, D, D))
    mb1 = din("L1_ffn_b1", (NE, D))
    mW2 = din("L1_ffn_W2", (NE, D, D))
    mb2 = din("L1_ffn_b2", (NE, D))
    ident_in = din("ident", (128, 128))

    out_hbm = nc.dram_tensor("out", [NT, D], F32, kind="ExternalOutput").ap()

    def scr(name, shape, dt=F32):
        kind = "ExternalOutput" if name in DBG_OUT_SCRATCH else "Internal"
        return nc.dram_tensor(name, list(shape), dt, kind=kind).ap()

    xn_s = scr("xn_s", (D, NT))
    qp_s = scr("qp_s", (D, NT))
    kv_s = scr("kv_s", (D, D))
    ksum_s = scr("ksum_s", (1, D))
    xnew_s = scr("xnew_s", (D, NT))
    xn1_s = scr("xn1_s", (D, NT))
    xc_s = scr("xc_s", (DI, NT))
    zs_s = scr("zs_s", (DI, NT))
    delta_s = scr("delta_s", (DI, NT))
    bc_s = scr("bc_s", (2 * DS, NT))
    y_s = scr("y_s", (DI, NT))
    g2_s = scr("g2_s", (D, NT))
    w1T_s = scr("w1T_s", (D, L0H))
    w2qT_s = scr("w2qT_s", (L0H, D))
    inpT_s = scr("inpT_s", (D, 2 * DI))
    outpT_s = scr("outpT_s", (DI, D))
    xpT_s = scr("xpT_s", (DI, DTR + 2 * DS))
    dtpT_s = scr("dtpT_s", (DTR, DI))

    tap_out = {}
    for tname in taps:
        shp = {"xn": (D, NT), "qp": (D, NT), "kv": (D, D), "ksum": (1, D),
               "w1T": (D, L0H), "w2qT": (L0H, D), "g": (D, NT), "x1": (D, NT), "x1r": (D, NT), "rden": (1, NT), "denr": (1, NT),
               "s0": (D, NT), "xnew": (D, NT), "xn1": (D, NT),
               "xc": (DI, NT), "zs": (DI, NT), "delta": (DI, NT),
               "bc": (2 * DS, NT), "yts": (DI, NT), "g2": (D, NT),
               "kpt": (128, 4096), "vtm": (128, 4096), "kpf": (128, 4096), "log8": (8, MT), "mtok": (128, 32), "mrow": (NE, MT),
               "sb_log": (8, MT), "sb_mtok": (128, 32), "sb_mrow": (1, MT),
               "sb_bcast2": (128, MT), "sb_ones1": (1, 128), "sb_lgt": (128, 32),
               }[tname]
        tap_out[tname] = nc.dram_tensor("tap_" + tname, list(shp), F32,
                                        kind="ExternalOutput").ap()

    def sb(name, shape, dt=F32):
        return nc.alloc_sbuf_tensor(name, list(shape), dt).ap()

    A = [sb(f"A{i}", (128, NMT, MT)) for i in range(4)]
    Af = [a.rearrange("p a b -> p (a b)") for a in A]
    GB = [sb("GB0", (128, 4, 544)), sb("GB1", (128, 4, 512)),
          sb("GB2", (128, 4, 512)), sb("GB3", (128, 2, 512))]
    QT = sb("QT", (128, 8, 512))
    WS = [sb(f"WS{i}", (128, 8, 128)) for i in range(2)]
    WB = [sb(f"WB{i}", (128, 2048)) for i in range(2)]
    ident = sb("identsb", (128, 128))
    ones1 = sb("ones1", (1, 128))
    ones_col = sb("ones_col", (128, 1))
    KV = sb("KV", (128, 8, 1024))
    KSUM = sb("KSUM", (128, 8))
    stat = sb("stat", (128, 32))
    rstat = sb("rstat", (1, 4 * MT))
    rowbuf = rstat
    BCAST = sb("BCAST", (128, 4, MT))
    carry = sb("carry", (128, 256))
    halo = sb("halo", (128, 16, 4))
    p_bqkv = sb("p_bqkv", (128, 24))
    p_bg = sb("p_bg", (128, 22))
    p_bu = sb("p_bu", (128, 22))
    p_bd = sb("p_bd", (128, 8))
    p_g1 = sb("p_g1", (128, 8))
    p_b1 = sb("p_b1", (128, 8))
    p_g2 = sb("p_g2", (128, 8))
    p_b2 = sb("p_b2", (128, 8))
    p_fb1 = sb("p_fb1", (128, 32))
    p_fb2q = sb("p_fb2q", (128, 8))
    p_convw = sb("p_convw", (128, 16, 4))
    p_convb = sb("p_convb", (128, 16))
    p_dtpb = sb("p_dtpb", (128, 16))
    p_aneg = sb("p_aneg", (128, 16, 16))
    p_dpar = sb("p_dpar", (128, 16))
    p_mb1 = sb("p_mb1", (128, NE, 8))
    p_mb2 = sb("p_mb2", (128, NE, 8))
    p_mbg = sb("p_mbg", (8, 1))
    p_mwg = sb("p_mwg", (128, 8, 8))
    DTW = sb("DTW", (64, DI))
    XPW = sb("XPW", (128, 16, 96))
    LOG = sb("LOG", (8, MT))
    LGT = sb("LGT", (128, 4, 8))
    MXC = sb("MXC", (128, 4))
    ONES8 = sb("ONES8", (128, 4, 8))
    MTOK = sb("MTOK", (128, 4, 8))
    qtmp = [QT[:, i, :] for i in range(8)]

    prog = Prog(nc)
    for s in ("s_ld", "s_st", "s_pe", "s_v", "s_s", "s_g"):
        prog.new_sem(s)
    psum = PsumPool(prog, nbanks=4)
    psBC = nc.alloc_psum_tensor("psBC", [128, 2048], F32).ap()

    W = prog.wait
    EM = prog.emit

    def ld(out, in_, slow=False):
        return prog.dma("sync", out, in_, "s_ld", slow=slow)

    def st(out, in_):
        return prog.dma("sync", out, in_, "s_st")

    EM("vector", lambda E: E.memset(ones1, 1.0), inc="s_v")
    EM("vector", lambda E: E.memset(ones_col, 1.0), inc="s_v")
    for cv in (1e-5, -0.5):
        cst = sb(f"cst_{cv}", (128, 1))
        nc.const_aps.register(F32, float(cv), cst)
        EM("vector", lambda E, t=cst, v=float(cv): E.memset(t, v), inc="s_v")
    W("scalar", "s_v")
    W("tensor", "s_v")
    ld(ident, ident_in)
    ld(p_bqkv, bqkv.rearrange("(j p) -> p j", p=128), slow=True)
    ld(p_bg[:, 0:21], bg[0:2688].rearrange("(j p) -> p j", p=128), slow=True)
    ld(p_bg[0:42, 21:22], bg[2688:2730].rearrange("(j p) -> p j", p=42), slow=True)
    ld(p_bu[:, 0:21], bu[0:2688].rearrange("(j p) -> p j", p=128), slow=True)
    ld(p_bu[0:42, 21:22], bu[2688:2730].rearrange("(j p) -> p j", p=42), slow=True)
    ld(p_bd, bd.rearrange("(j p) -> p j", p=128), slow=True)
    ld(p_g1, g1.rearrange("(j p) -> p j", p=128), slow=True)
    ld(p_b1, b1.rearrange("(j p) -> p j", p=128), slow=True)
    ld(p_g2, g2.rearrange("(j p) -> p j", p=128), slow=True)
    ld(p_b2, b2.rearrange("(j p) -> p j", p=128), slow=True)
    ld(p_fb1, fb1.rearrange("(j p) -> p j", p=128), slow=True)
    ld(p_fb2q, fb2.rearrange("(j p) -> p j", p=128), slow=True)
    ld(p_convw, convw.rearrange("(j p) c -> p j c", p=128), slow=True)
    ld(p_convb, convb.rearrange("(j p) -> p j", p=128), slow=True)
    ld(p_dtpb, dtpb.rearrange("(j p) -> p j", p=128), slow=True)
    ld(p_aneg, alog.rearrange("(j p) n -> p j n", p=128), slow=True)
    ld(p_dpar, dpar.rearrange("(j p) -> p j", p=128), slow=True)
    ld(p_mb1, mb1.rearrange("e (j p) -> p e j", p=128), slow=True)
    ld(p_mb2, mb2.rearrange("e (j p) -> p e j", p=128), slow=True)
    ld(p_mbg, mbg[:, None], slow=True)
    ld(p_mwg, mWg.rearrange("(dc p) e -> p dc e", p=128), slow=True)
    ld(rowbuf[:, 0:E3 - D], bqkv[D:E3][None, :])
    W("vector", "s_ld")
    W("scalar", "s_ld")
    EM("scalar", lambda E: E.activation(p_aneg, p_aneg, ACT.Exp), inc="s_s")
    W("vector", "s_s")
    ev(prog, lambda E: E.tensor_scalar_mul(p_aneg, p_aneg, -1.0))
    b2slab = QT[:, 6, :]
    EM("vector", lambda E: E.memset(b2slab, 1.0), inc="s_v")
    W("vector", "s_ld")
    ev(prog, lambda E: E.tensor_copy(b2slab[:, 0:8], p_fb2q))
    emit_quantize(prog, b2slab, [QT[:, i, :] for i in range(6)], b2slab,
                  man=10, min_e=-14, max_val=65504.0, signed=True,
                  nb_e=5, sat=1.0e5)
    ev(prog, lambda E: E.tensor_copy(p_fb2q, b2slab[:, 0:8]))

    prep_state = {}
    def prep_transpose(src_ap, dst_ap, R, C, quant=None):
        CB = min(C, 1024)
        stg = WS[0].rearrange("p a b -> p (a b)")[:, 0:512]
        wb_lastuse = prep_state.setdefault("wb_lastuse", [0, 0])
        for r0 in range(0, R, 128):
            rr = min(128, R - r0)
            wi = (r0 // 128) % 2
            wsl = WB[wi]
            for c0 in range(0, C, CB):
                cb = min(CB, C - c0)
                W("sync", "s_pe", wb_lastuse[wi])
                lv = ld(wsl[0:rr, 0:cb], src_ap[r0:r0 + rr, c0:c0 + cb])
                if quant is not None:
                    W("vector", "s_ld", lv)
                    W("scalar", "s_ld", lv)
                    quant(wsl[0:rr, 0:cb])
                    W("tensor", "s_v")
                W("tensor", "s_ld", lv)
                for cc in range(0, cb, 512):
                    nj = ceil_div(min(512, cb - cc), 128)
                    bi, ps = psum.acquire()
                    for j in range(nj):
                        jw = min(128, cb - cc - j * 128)
                        EM("tensor", lambda E, ps=ps, wsl=wsl, cc=cc, j=j, jw=jw, rr=rr: E.transpose(
                            ps[0:jw, j * 128:j * 128 + rr],
                            wsl[0:rr, cc + j * 128:cc + j * 128 + jw],
                            ident[0:rr, 0:rr]),
                            inc=("s_pe" if j == nj - 1 else None))
                    W("scalar", "s_pe")
                    W("scalar", "s_st")
                    vv = prog.emit("scalar", lambda E, ps=ps: E.copy(stg, ps),
                                   inc="s_s")
                    psum.release(bi, "s_s", vv)
                    W("sync", "s_s", vv)
                    for j in range(nj):
                        jw = min(128, cb - cc - j * 128)
                        st(dst_ap[c0 + cc + j * 128: c0 + cc + j * 128 + jw,
                                  r0:r0 + rr],
                           stg[0:jw, j * 128:j * 128 + rr])
                wb_lastuse[wi] = prog.cnt["s_pe"]

    def wq_quant_slab(ap_slab):
        for h in range(2):
            sl = ap_slab[:, h * 512:(h + 1) * 512]
            emit_quantize(prog, sl, [qtmp[i] for i in range(6)], sl,
                          man=3, min_e=-6, max_val=240.0, signed=True,
                          nb_e=4, sat=1000.0)

    prep_transpose(fw1, w1T_s, L0H, D)
    prep_transpose(fw2, w2qT_s, D, L0H, quant=wq_quant_slab)
    prep_transpose(inpw, inpT_s, 2 * DI, D)
    prep_transpose(outpw, outpT_s, D, DI)
    prep_transpose(xpw, xpT_s, DTR + 2 * DS, DI)
    prep_transpose(dtpw, dtpT_s, DI, DTR)
    W("sync", "s_st")
    ld(DTW, dtpT_s)
    ld(XPW, xpT_s.rearrange("(dc p) r -> p dc r", p=128))

    SQB = QT[:, 0:2, :].rearrange("p a b -> p (a b)")
    EM("vector", lambda E: E.memset(KV.rearrange("p a b -> p (a b)"), 0.0), inc="s_v")
    EM("vector", lambda E: E.memset(rowbuf[0:1, 0:1024], 0.0), inc="s_v")
    W("sync", "s_st")
    ks_started = [False]
    ws_lastuse = [0, 0]
    W("scalar", "s_v")
    for mt in range(NMT):
        XTf = Af[0]
        W("sync", "s_pe")
        W("vector", "s_pe")
        lv = ld(XTf.rearrange("p (s d) -> p s d", d=D),
                x_in[mt * MT:(mt + 1) * MT, :].rearrange("(s p) d -> p s d", p=128))
        W("vector", "s_ld", lv)
        W("scalar", "s_ld", lv)
        for s in range(4):
            xs = XTf[:, s * D:(s + 1) * D]
            ev(prog, lambda E, xs=xs, s=s: E.reduce_sum(stat[:, s:s + 1], xs, axis=AX.X))
            W("scalar", "s_v")
            prog.emit("scalar", lambda E, xs=xs: E.activation(SQB, xs, ACT.Square), inc="s_s")
            W("vector", "s_s")
            ev(prog, lambda E, s=s: E.reduce_sum(stat[:, 4 + s:5 + s], SQB, axis=AX.X))
        W("vector", "s_v")
        ev(prog, lambda E: E.tensor_scalar_mul(stat[:, 8:12], stat[:, 0:4], 1.0 / D))
        ev(prog, lambda E: E.tensor_scalar_mul(stat[:, 12:16], stat[:, 4:8], 1.0 / D))
        W("scalar", "s_v")
        prog.emit("scalar", lambda E: E.activation(stat[:, 16:20], stat[:, 8:12], ACT.Square), inc="s_s")
        W("vector", "s_s")
        ev(prog, lambda E: E.tensor_tensor(stat[:, 12:16], stat[:, 12:16], stat[:, 16:20], ALU.subtract))
        W("scalar", "s_v")
        prog.emit("scalar", lambda E: E.activation(stat[:, 12:16], stat[:, 12:16], ACT.Ln, bias=1e-5), inc="s_s")
        prog.emit("scalar", lambda E: E.activation(stat[:, 12:16], stat[:, 12:16], ACT.Exp, scale=-0.5), inc="s_s")
        W("vector", "s_s")
        XN = Af[1]
        for s in range(4):
            ev(prog, lambda E, s=s: E.tensor_scalar(
                XN[:, s * D:(s + 1) * D], XTf[:, s * D:(s + 1) * D],
                stat[:, 8 + s:9 + s], stat[:, 12 + s:13 + s], ALU.subtract, ALU.mult))
        XF = A[2]
        W("tensor", "s_v")
        for dc in range(8):
            bi, ps = psum.acquire()
            for s in range(4):
                EM("tensor", lambda E, ps=ps, dc=dc, s=s: E.transpose(
                    ps[:, s * 128:(s + 1) * 128],
                    XN[:, s * D + dc * 128: s * D + (dc + 1) * 128], ident),
                    inc=("s_pe" if s == 3 else None))
            W("scalar", "s_pe")
            v = prog.emit("scalar", lambda E, ps=ps, dc=dc: E.copy(XF[:, dc, :], ps), inc="s_s")
            psum.release(bi, "s_s", v)
        W("sync", "s_s")
        st(xn_s[:, mt * MT:(mt + 1) * MT].rearrange("(dc p) t -> p dc t", p=128),
           XF)
        KPF = Af[0]
        VF = Af[1]
        for ec in range(24):
            wsl = WS[ec % 2]
            W("sync", "s_pe", ws_lastuse[ec % 2])
            lv = ld(wsl, Wqkv[:, ec * 128:(ec + 1) * 128].rearrange("(dc p) e -> p dc e", p=128))
            W("tensor", "s_ld", lv)
            W("tensor", "s_s")
            bi, ps = psum.acquire()
            for dc in range(8):
                EM("tensor", lambda E, ps=ps, wsl=wsl, dc=dc: E.matmul(
                    ps, wsl[:, dc, :], XF[:, dc, :],
                    start=(dc == 0), stop=(dc == 7)),
                    inc=("s_pe" if dc == 7 else None))
            ws_lastuse[ec % 2] = prog.cnt["s_pe"]
            W("scalar", "s_pe")
            W("vector", "s_pe")
            if ec < 8:
                zb = QT[:, 6, :]
                W("scalar", "s_st")
                v1 = prog.emit("scalar", lambda E, ps=ps, ec=ec, zb=zb: E.activation(
                    zb, ps, ACT.Identity, bias=p_bqkv[:, ec:ec + 1]), inc="s_s")
                psum.release(bi, "s_s", v1)
                W("vector", "s_s", v1)
                emit_elup1(prog, zb, QT[:, 7, :], zb)
                W("sync", "s_v")
                st(qp_s[ec * 128:(ec + 1) * 128, mt * MT:(mt + 1) * MT], zb)
                W("vector", "s_st")
            else:
                dstf = KPF if ec < 16 else VF
                cc = (ec - 8) % 8
                zb = dstf[:, cc * 512:(cc + 1) * 512]
                v1 = prog.emit("scalar", lambda E, ps=ps, ec=ec, zb=zb: E.activation(
                    zb, ps, ACT.Identity, bias=p_bqkv[:, ec:ec + 1]), inc="s_s")
                psum.release(bi, "s_s", v1)
                if ec < 16:
                    W("vector", "s_s", v1)
                    emit_elup1(prog, zb, QT[:, 7, :], zb)
        KPT = A[3]
        KPTf = Af[3]
        VTf = Af[2]
        W("tensor", "s_v")
        W("tensor", "s_s")
        W("scalar", "s_st")
        for src, dstf in ((KPF, KPTf), (VF, VTf)):
            for tc in range(4):
                for eh in range(2):
                    bi, ps = psum.acquire()
                    for j in range(4):
                        ecol = eh * 4 + j
                        EM("tensor", lambda E, ps=ps, src=src, tc=tc, ecol=ecol, j=j: E.transpose(
                            ps[:, j * 128:(j + 1) * 128],
                            src[:, ecol * 512 + tc * 128: ecol * 512 + (tc + 1) * 128],
                            ident),
                            inc=("s_pe" if j == 3 else None))
                    W("scalar", "s_pe")
                    v = prog.emit("scalar", lambda E, ps=ps, dstf=dstf, tc=tc, eh=eh: E.copy(
                        dstf[:, tc * 1024 + eh * 512: tc * 1024 + (eh + 1) * 512], ps),
                        inc="s_s")
                    psum.release(bi, "s_s", v)
        W("tensor", "s_s")
        for dc in range(8):
            for ev2 in range(2):
                bi, ps = psum.acquire()
                for tc in range(4):
                    EM("tensor", lambda E, ps=ps, dc=dc, ev2=ev2, tc=tc: E.matmul(
                        ps, KPTf[:, tc * 1024 + dc * 128: tc * 1024 + (dc + 1) * 128],
                        VTf[:, tc * 1024 + ev2 * 512: tc * 1024 + (ev2 + 1) * 512],
                        start=(tc == 0), stop=(tc == 3)),
                        inc=("s_pe" if tc == 3 else None))
                W("vector", "s_pe")
                v = ev(prog, lambda E, ps=ps, dc=dc, ev2=ev2: E.tensor_tensor(
                    KV[:, dc, ev2 * 512:(ev2 + 1) * 512],
                    KV[:, dc, ev2 * 512:(ev2 + 1) * 512], ps, ALU.add))
                psum.release(bi, "s_v", v)
        for eh in range(2):
            bks, ps_ks = psum.acquire()
            for tc in range(4):
                EM("tensor", lambda E, tc=tc, eh=eh, ps=ps_ks: E.matmul(
                    ps[0:1, 0:512], ones_col,
                    KPTf[:, tc * 1024 + eh * 512: tc * 1024 + (eh + 1) * 512],
                    start=(tc == 0), stop=(tc == 3)),
                    inc=("s_pe" if tc == 3 else None))
            W("vector", "s_pe")
            vks = ev(prog, lambda E, eh=eh, ps=ps_ks: E.tensor_tensor(
                rowbuf[0:1, eh * 512:(eh + 1) * 512],
                rowbuf[0:1, eh * 512:(eh + 1) * 512], ps[0:1, 0:512], ALU.add))
            psum.release(bks, "s_v", vks)


    if "kpt" in tap_out:
        W("sync", "s_v")
        W("sync", "s_s")
        prog.dma("sync", tap_out["kpt"], Af[3], "s_st")
    if "vtm" in tap_out:
        W("sync", "s_v")
        W("sync", "s_s")
        prog.dma("sync", tap_out["vtm"], Af[2], "s_st")
    if "kpf" in tap_out:
        W("sync", "s_v")
        W("sync", "s_s")
        prog.dma("sync", tap_out["kpf"], Af[0], "s_st")
    W("sync", "s_v")
    st(ksum_s, rowbuf[0:1, 0:1024])
    st(kv_s.rearrange("(dc p) e -> p dc e", p=128), KV)
    W("sync", "s_st")
    lv = ld(KSUM, ksum_s[0, :].rearrange("(j p) -> p j", p=128), slow=True)

    if stop_after == "phA":
        _finish(prog, nc, psum, out_hbm, tap_out, locals())
        return _close(nc, prog, stack, tap_out)

    KVf = KV.rearrange("p a b -> p (a b)")
    QTf = QT.rearrange("p a b -> p (a b)")
    GBf = [g.rearrange("p a b -> p (a b)") for g in GB]
    q2 = [KVf[:, i * 2048:(i + 1) * 2048] for i in range(4)] + \
         [QTf[:, i * 2048:(i + 1) * 2048] for i in range(2)]
    SQ2 = [QT[:, 0, :], QT[:, 1, :]]

    def emit_fmln(src3, gcol3=None, bcol3=None):
        W("tensor", "s_v")
        W("scalar", "s_v")
        bmu, ps_mu = psum.acquire()
        for dc in range(8):
            EM("tensor", lambda E, ps=ps_mu, dc=dc: E.matmul(
                ps[0:1, :], ones_col, src3[:, dc, :],
                start=(dc == 0), stop=(dc == 7)),
                inc=("s_pe" if dc == 7 else None))
        bm2, ps_m2 = psum.acquire()
        sq_free = [0, 0]
        for dc in range(8):
            sq = SQ2[dc % 2]
            prog.wait("scalar", "s_pe", sq_free[dc % 2])
            vsq = prog.emit("scalar", lambda E, sq=sq, dc=dc: E.activation(
                sq, src3[:, dc, :], ACT.Square), inc="s_s")
            W("tensor", "s_s", vsq)
            sq_free[dc % 2] = EM("tensor", lambda E, ps=ps_m2, sq=sq, dc=dc: E.matmul(
                ps[0:1, :], ones_col, sq,
                start=(dc == 0), stop=(dc == 7)), inc="s_pe")
        r0 = rstat[0:1, 0:512]; r1 = rstat[0:1, 512:1024]
        r2 = rstat[0:1, 1024:1536]; r3 = rstat[0:1, 1536:2048]
        W("vector", "s_pe")
        ev(prog, lambda E, ps=ps_mu: E.tensor_scalar_mul(r0, ps[0:1, :], 1.0 / D))
        ev(prog, lambda E, ps=ps_m2: E.tensor_scalar_mul(r1, ps[0:1, :], 1.0 / D))
        v1 = ev(prog, lambda E: E.tensor_tensor(r3, r0, r0, ALU.mult))
        v2 = ev(prog, lambda E: E.tensor_tensor(r1, r1, r3, ALU.subtract))
        psum.release(bmu, "s_v", v1)
        psum.release(bm2, "s_v", v2)
        W("scalar", "s_v")
        prog.emit("scalar", lambda E: E.activation(r1, r1, ACT.Ln, bias=1e-5), inc="s_s")
        prog.emit("scalar", lambda E: E.activation(r1, r1, ACT.Exp, scale=-0.5), inc="s_s")
        W("vector", "s_s")
        ev(prog, lambda E: E.tensor_tensor(r2, r0, r1, ALU.mult))
        ev(prog, lambda E: E.tensor_scalar_mul(r2, r2, -1.0))
        W("tensor", "s_v")
        for i, row in ((0, r1), (1, r2)):
            bb, psb = psum.acquire()
            EM("tensor", lambda E, ps=psb, row=row: E.matmul(
                ps, ones1, row, start=True, stop=True), inc="s_pe")
            W("scalar", "s_pe")
            vv = prog.emit("scalar", lambda E, ps=psb, i=i: E.copy(BCAST[:, i, :], ps), inc="s_s")
            psum.release(bb, "s_s", vv)
        W("vector", "s_s")
        for dc in range(8):
            ev(prog, lambda E, dc=dc: E.tensor_tensor(src3[:, dc, :], src3[:, dc, :], BCAST[:, 0, :], ALU.mult))
            ev(prog, lambda E, dc=dc: E.tensor_tensor(src3[:, dc, :], src3[:, dc, :], BCAST[:, 1, :], ALU.add))
            if gcol3 is not None:
                ev(prog, lambda E, dc=dc: E.tensor_scalar(
                    src3[:, dc, :], src3[:, dc, :], gcol3[:, dc:dc + 1], bcol3[:, dc:dc + 1],
                    ALU.mult, ALU.add))

    mh_chunks = [(i * 128, 128, i) for i in range(21)] + [(2688, 42, 21)]
    ffn_groups = [mh_chunks[0:4], mh_chunks[4:8], mh_chunks[8:12],
                  mh_chunks[12:16], mh_chunks[16:20], mh_chunks[20:22]]

    for mt in range(NMT):
        t0c, t1c = mt * MT, (mt + 1) * MT
        XNB = A[0]
        GACC = A[1]
        W("sync", "s_v")
        W("sync", "s_s")
        W("sync", "s_pe")
        W("sync", "s_st")
        lvxn = ld(XNB, xn_s[:, t0c:t1c].rearrange("(dc p) t -> p dc t", p=128))
        W("vector", "s_st")
        EM("vector", lambda E: E.memset(Af[1], 0.0), inc="s_v")
        W("tensor", "s_ld", lvxn)
        for gi in range(8):
            HT = GBf[0][:, 0:2048]
            HT3 = GB[0]
            for sub in range(4):
                mh = gi * 4 + sub
                wsl = WS[mh % 2]
                W("sync", "s_pe", ws_lastuse[mh % 2])
                lv = ld(wsl, w1T_s[:, mh * 128:(mh + 1) * 128].rearrange(
                    "(dc p) h -> p dc h", p=128))
                W("tensor", "s_ld", lv)
                bi, ps = psum.acquire()
                for dc in range(8):
                    EM("tensor", lambda E, ps=ps, wsl=wsl, dc=dc: E.matmul(
                        ps, wsl[:, dc, :], XNB[:, dc, :],
                        start=(dc == 0), stop=(dc == 7)),
                        inc=("s_pe" if dc == 7 else None))
                ws_lastuse[mh % 2] = prog.cnt["s_pe"]
                W("scalar", "s_pe")
                prog.wait("scalar", "s_v")
                vv = prog.emit("scalar", lambda E, ps=ps, mh=mh, sub=sub: E.activation(
                    GBf[0][:, sub * 512:(sub + 1) * 512], ps, ACT.Relu,
                    bias=p_fb1[:, mh:mh + 1]), inc="s_s")
                psum.release(bi, "s_s", vv)
            W("vector", "s_s")
            emit_quantize(prog, HT.rearrange("p (a b) -> p a b", b=512),
                          [q.rearrange("p (a b) -> p a b", b=512) for q in q2],
                          HT.rearrange("p (a b) -> p a b", b=512),
                          man=3, min_e=-6, max_val=240.0, signed=False,
                          nb_e=4, sat=1000.0)
            wd_stage = KVf[:, 0:4096].rearrange("p (kh d) -> p kh d", d=1024)
            W("sync", "s_v")
            lv2 = ld(wd_stage, w2qT_s[gi * 512:(gi + 1) * 512, :].rearrange(
                "(kh p) d -> p kh d", p=128))
            W("tensor", "s_ld", lv2)
            W("tensor", "s_v")
            for md in range(8):
                bi, ps = psum.acquire()
                for kh in range(4):
                    EM("tensor", lambda E, ps=ps, wd=wd_stage, kh=kh, md=md: E.matmul(
                        ps, wd[:, kh, md * 128:(md + 1) * 128],
                        GBf[0][:, kh * 512:(kh + 1) * 512],
                        start=(kh == 0), stop=(kh == 3)),
                        inc=("s_pe" if kh == 3 else None))
                W("vector", "s_pe")
                vv = ev(prog, lambda E, ps=ps, md=md: E.tensor_tensor(
                    GACC[:, md, :], GACC[:, md, :], ps, ALU.add))
                psum.release(bi, "s_v", vv)
        for dc in range(8):
            ev(prog, lambda E, dc=dc: E.tensor_scalar_add(
                GACC[:, dc, :], GACC[:, dc, :], p_fb2q[:, dc:dc + 1]))
        GACCf = Af[1]
        for h in range(2):
            sl = GACCf[:, h * 2048:(h + 1) * 2048].rearrange("p (a b) -> p a b", b=512)
            emit_quantize(prog, sl, [q.rearrange("p (a b) -> p a b", b=512) for q in q2],
                          sl, man=10, min_e=-14, max_val=65504.0, signed=True,
                          nb_e=5, sat=1.0e5)
        W("scalar", "s_v")
        prog.emit("scalar", lambda E: E.activation(GACCf, GACCf, ACT.Silu), inc="s_s")
        if "g" in tap_out:
            vtp = prog.cnt["s_s"]
            W("sync", "s_s", vtp)
            prog.dma("sync", tap_out["g"][:, t0c:t1c].rearrange("(dc p) t -> p dc t", p=128),
                     A[1], "s_st")
            W("vector", "s_st")
        QP = A[2]
        X1 = A[3]
        lvqp = ld(QP, qp_s[:, t0c:t1c].rearrange("(dc p) t -> p dc t", p=128))
        bden, ps_den = psum.acquire()
        W("tensor", "s_ld", lvqp)
        for dc in range(8):
            EM("tensor", lambda E, ps=ps_den, dc=dc: E.matmul(
                ps[0:1, :], KSUM[:, dc:dc + 1], QP[:, dc, :],
                start=(dc == 0), stop=(dc == 7)),
                inc=("s_pe" if dc == 7 else None))
        W("vector", "s_pe")
        ev(prog, lambda E, ps=ps_den: E.tensor_scalar_add(rstat[0:1, 0:512], ps[0:1, :], 1e-6))
        if "denr" in tap_out:
            W("sync", "s_v")
            prog.dma("sync", tap_out["denr"][0:1, t0c:t1c], rstat[0:1, 0:512], "s_st")
            W("vector", "s_st")
        v1 = ev(prog, lambda E: E.reciprocal(rstat[0:1, 0:512], rstat[0:1, 0:512]))
        psum.release(bden, "s_v", v1)
        W("tensor", "s_v")
        brd, ps_rd = psum.acquire()
        EM("tensor", lambda E, ps=ps_rd: E.matmul(ps, ones1, rstat[0:1, 0:512],
                                                  start=True, stop=True), inc="s_pe")
        W("scalar", "s_pe")
        vrd = prog.emit("scalar", lambda E, ps=ps_rd: E.copy(BCAST[:, 2, :], ps), inc="s_s")
        psum.release(brd, "s_s", vrd)
        W("vector", "s_s", vrd)
        for me in range(8):
            wsl = WS[me % 2]
            W("sync", "s_pe", ws_lastuse[me % 2])
            lv = ld(wsl, kv_s[:, me * 128:(me + 1) * 128].rearrange(
                "(dc p) e -> p dc e", p=128))
            W("tensor", "s_ld", lv)
            bi, ps = psum.acquire()
            for dc in range(8):
                EM("tensor", lambda E, ps=ps, wsl=wsl, dc=dc: E.matmul(
                    ps, wsl[:, dc, :], QP[:, dc, :],
                    start=(dc == 0), stop=(dc == 7)),
                    inc=("s_pe" if dc == 7 else None))
            ws_lastuse[me % 2] = prog.cnt["s_pe"]
            W("vector", "s_pe")
            ev(prog, lambda E, ps=ps, me=me: E.tensor_tensor(
                X1[:, me, :], ps, BCAST[:, 2, :], ALU.mult))
            vv = ev(prog, lambda E, me=me: E.tensor_tensor(
                X1[:, me, :], X1[:, me, :], XNB[:, me, :], ALU.add))
            psum.release(bi, "s_v", vv)
        if "x1r" in tap_out:
            W("sync", "s_v")
            prog.dma("sync", tap_out["x1r"][:, t0c:t1c].rearrange("(dc p) t -> p dc t", p=128),
                     A[3], "s_st")
            W("vector", "s_st")
        if "rden" in tap_out:
            W("sync", "s_v")
            prog.dma("sync", tap_out["rden"][0:1, t0c:t1c], rstat[0:1, 0:512], "s_st")
            W("vector", "s_st")
        W("tensor", "s_v")
        emit_fmln(X1, p_g1, p_b1)
        if "x1" in tap_out:
            W("sync", "s_v")
            prog.dma("sync", tap_out["x1"][:, t0c:t1c].rearrange("(dc p) t -> p dc t", p=128),
                     A[3], "s_st")
        FFN = A[0]
        W("vector", "s_v")
        EM("vector", lambda E: E.memset(Af[0], 0.0), inc="s_v")
        for grp in ffn_groups:
            for si, (h0, w, bc) in enumerate(grp):
                for which, wmat, pb in (("g", Wg, p_bg), ("u", Wu, p_bu)):
                    wsl = WS[0] if which == "g" else WS[1]
                    W("sync", "s_pe", ws_lastuse[0 if which == "g" else 1])
                    lv = ld(wsl[:, :, 0:w], wmat[:, h0:h0 + w].rearrange(
                        "(dc p) h -> p dc h", p=128))
                    W("tensor", "s_ld", lv)
                    bi, ps = psum.acquire()
                    for dc in range(8):
                        EM("tensor", lambda E, ps=ps, wsl=wsl, dc=dc, w=w: E.matmul(
                            ps[0:w, :], wsl[:, dc, 0:w], X1[:, dc, :],
                            start=(dc == 0), stop=(dc == 7)),
                            inc=("s_pe" if dc == 7 else None))
                    ws_lastuse[0 if which == "g" else 1] = prog.cnt["s_pe"]
                    W("scalar", "s_pe")
                    prog.wait("scalar", "s_v")
                    if which == "g":
                        vv = prog.emit("scalar", lambda E, ps=ps, w=w, bc=bc, pb=pb: E.activation(
                            GB[1][0:w, 0, 0:512], ps[0:w, :],
                            ACT.Silu, bias=pb[0:w, bc:bc + 1]), inc="s_s")
                        psum.release(bi, "s_s", vv)
                    else:
                        vv = prog.emit("scalar", lambda E, ps=ps, w=w, bc=bc, pb=pb: E.activation(
                            GB[1][0:w, 1, 0:512], ps[0:w, :],
                            ACT.Identity, bias=pb[0:w, bc:bc + 1]), inc="s_s")
                        psum.release(bi, "s_s", vv)
                W("vector", "s_s")
                W("vector", "s_pe")
                ev(prog, lambda E, si=si, w=w: E.tensor_tensor(
                    GB[0][0:w, si, 0:512], GB[1][0:w, 0, 0:512], GB[1][0:w, 1, 0:512],
                    ALU.mult))
            gh0 = grp[0][0]
            gw = sum(w for (_, w, _) in grp)
            nkh = len(grp)
            wd_stage = KVf[:, 0:4096].rearrange("p (kh d) -> p kh d", d=1024)
            W("sync", "s_v")
            W("sync", "s_pe")
            lvw = None
            for ki, (h0, w, _) in enumerate(grp):
                lvw = ld(wd_stage[0:w, ki, :], Wd[h0:h0 + w, :])
            W("tensor", "s_ld", lvw)
            W("tensor", "s_v")
            for md in range(8):
                bi, ps = psum.acquire()
                for ki, (h0, w, _) in enumerate(grp):
                    is_last = (ki == nkh - 1)
                    EM("tensor", lambda E, ps=ps, wd=wd_stage, ki=ki, md=md, w=w,
                            first=(ki == 0), last=is_last: E.matmul(
                        ps, wd[0:w, ki, md * 128:(md + 1) * 128],
                        GB[0][0:w, ki, 0:512],
                        start=first, stop=last),
                        inc=("s_pe" if is_last else None))
                W("vector", "s_pe")
                vv = ev(prog, lambda E, ps=ps, md=md: E.tensor_tensor(
                    FFN[:, md, :], FFN[:, md, :], ps, ALU.add))
                psum.release(bi, "s_v", vv)
        for dc in range(8):
            ev(prog, lambda E, dc=dc: E.tensor_scalar_add(
                FFN[:, dc, :], FFN[:, dc, :], p_bd[:, dc:dc + 1]))
            ev(prog, lambda E, dc=dc: E.tensor_tensor(
                FFN[:, dc, :], FFN[:, dc, :], X1[:, dc, :], ALU.add))
        W("tensor", "s_v")
        emit_fmln(FFN, p_g2, p_b2)
        if "s0" in tap_out:
            W("sync", "s_v")
            prog.dma("sync", tap_out["s0"][:, t0c:t1c].rearrange("(dc p) t -> p dc t", p=128),
                     A[0], "s_st")
            W("vector", "s_st")
        for dc in range(8):
            ev(prog, lambda E, dc=dc: E.tensor_tensor(
                GACC[:, dc, :], A[0][:, dc, :], GACC[:, dc, :], ALU.mult))
        XT2 = A[2]
        W("sync", "s_v")
        W("sync", "s_pe")
        lvx = ld(Af[2].rearrange("p (s d) -> p s d", d=D),
                 x_in[t0c:t1c, :].rearrange("(s p) d -> p s d", p=128))
        W("tensor", "s_ld", lvx)
        XT2f = Af[2]
        for dc in range(8):
            bi, ps = psum.acquire()
            for s in range(4):
                EM("tensor", lambda E, ps=ps, dc=dc, s=s: E.transpose(
                    ps[:, s * 128:(s + 1) * 128],
                    XT2f[:, s * D + dc * 128: s * D + (dc + 1) * 128], ident),
                    inc=("s_pe" if s == 3 else None))
            W("vector", "s_pe")
            vv = ev(prog, lambda E, ps=ps, dc=dc: E.tensor_tensor(
                GB[2][:, 0, 0:512] if False else GB[2].rearrange("p a b -> p (a b)")[:, 0:512],
                ps, GACC[:, dc, :], ALU.add))
            psum.release(bi, "s_v", vv)
            W("sync", "s_v", vv)
            prog.dma("sync", xnew_s[dc * 128:(dc + 1) * 128, t0c:t1c],
                     GB[2].rearrange("p a b -> p (a b)")[:, 0:512], "s_st")
            W("vector", "s_st")
        XNEW = A[2]
        W("sync", "s_pe")
        W("sync", "s_st")
        lvn = ld(XNEW, xnew_s[:, t0c:t1c].rearrange("(dc p) t -> p dc t", p=128))
        W("vector", "s_ld", lvn)
        W("tensor", "s_ld", lvn)
        W("scalar", "s_ld", lvn)
        emit_fmln(XNEW, None, None)
        W("sync", "s_v")
        st(xn1_s[:, t0c:t1c].rearrange("(dc p) t -> p dc t", p=128), XNEW)

    if stop_after == "phB":
        _finish(prog, nc, psum, out_hbm, tap_out, locals())
        return _close(nc, prog, stack, tap_out)

    EM("vector", lambda E: E.memset(halo.rearrange("p a b -> p (a b)"), 0.0), inc="s_v")
    for mt in range(NMT):
        t0c, t1c = mt * MT, (mt + 1) * MT
        XN1 = A[0]
        W("sync", "s_v")
        W("sync", "s_s")
        W("sync", "s_pe")
        W("sync", "s_st")
        lvx = ld(XN1, xn1_s[:, t0c:t1c].rearrange("(dc p) t -> p dc t", p=128))
        W("tensor", "s_ld", lvx)
        XD = BCAST[0:96, 3, :]
        W("vector", "s_pe")
        W("vector", "s_st")
        EM("vector", lambda E, XD=XD: E.memset(XD, 0.0), inc="s_v")
        for grp in range(4):
            for sub in range(4):
                idx = grp * 4 + sub
                wsl = WS[idx % 2]
                W("sync", "s_pe", ws_lastuse[idx % 2])
                lv = ld(wsl, inpT_s[:, idx * 128:(idx + 1) * 128].rearrange(
                    "(dc p) e -> p dc e", p=128))
                W("tensor", "s_ld", lv)
                bi, ps = psum.acquire()
                for dc in range(8):
                    EM("tensor", lambda E, ps=ps, wsl=wsl, dc=dc, XN1=XN1: E.matmul(
                        ps, wsl[:, dc, :], XN1[:, dc, :],
                        start=(dc == 0), stop=(dc == 7)),
                        inc=("s_pe" if dc == 7 else None))
                ws_lastuse[idx % 2] = prog.cnt["s_pe"]
                W("scalar", "s_pe")
                prog.wait("scalar", "s_v")
                vv = prog.emit("scalar", lambda E, ps=ps, sub=sub: E.copy(
                    GB[0][:, sub, 3:515], ps), inc="s_s")
                psum.release(bi, "s_s", vv)
                prog.emit("scalar", lambda E, sub=sub, idx=idx: E.copy(
                    GB[0][:, sub, 0:3], halo[:, idx, 0:3]), inc="s_s")
                vhl = prog.emit("scalar", lambda E, sub=sub, idx=idx: E.copy(
                    halo[:, idx, 0:3], GB[0][:, sub, 512:515]), inc="s_s")
                W("vector", "s_s", vhl)
                XC = GB[1]
                W("vector", "s_st")
                ev(prog, lambda E, sub=sub, idx=idx: E.tensor_scalar(
                    XC[:, sub, 0:512], GB[0][:, sub, 0:512],
                    p_convw[:, idx, 0:1], None, ALU.mult))
                for j in range(1, 4):
                    ev(prog, lambda E, sub=sub, idx=idx, j=j: E.scalar_tensor_tensor(
                        XC[:, sub, 0:512], GB[0][:, sub, j:j + 512],
                        p_convw[:, idx, j:j + 1], XC[:, sub, 0:512],
                        ALU.mult, ALU.add))
                prog.wait("scalar", "s_v")
                vsl = prog.emit("scalar", lambda E, sub=sub, idx=idx: E.activation(
                    XC[:, sub, 0:512], XC[:, sub, 0:512], ACT.Silu,
                    bias=p_convb[:, idx:idx + 1]), inc="s_s")
            W("sync", "s_s")
            st(xc_s[grp * 512:(grp + 1) * 512, t0c:t1c].rearrange(
                "(s p) t -> p s t", p=128), GB[1][:, :, 0:512])
            bi, ps = psum.acquire()
            W("tensor", "s_s")
            for sub in range(4):
                idx = grp * 4 + sub
                EM("tensor", lambda E, ps=ps, idx=idx, sub=sub: E.matmul(
                    ps[0:96, :], XPW[:, idx, :], GB[1][:, sub, 0:512],
                    start=(sub == 0), stop=(sub == 3)),
                    inc=("s_pe" if sub == 3 else None))
            W("vector", "s_pe")
            vv = ev(prog, lambda E, ps=ps, XD=XD: E.tensor_tensor(
                XD, XD, ps[0:96, :], ALU.add))
            psum.release(bi, "s_v", vv)
            for sub in range(4):
                idx = 16 + grp * 4 + sub
                wsl = WS[idx % 2]
                W("sync", "s_pe", ws_lastuse[idx % 2])
                lv = ld(wsl, inpT_s[:, idx * 128:(idx + 1) * 128].rearrange(
                    "(dc p) e -> p dc e", p=128))
                W("tensor", "s_ld", lv)
                bi, ps = psum.acquire()
                for dc in range(8):
                    EM("tensor", lambda E, ps=ps, wsl=wsl, dc=dc, XN1=XN1: E.matmul(
                        ps, wsl[:, dc, :], XN1[:, dc, :],
                        start=(dc == 0), stop=(dc == 7)),
                        inc=("s_pe" if dc == 7 else None))
                ws_lastuse[idx % 2] = prog.cnt["s_pe"]
                W("scalar", "s_pe")
                prog.wait("scalar", "s_st")
                vv = prog.emit("scalar", lambda E, ps=ps, sub=sub: E.activation(
                    GB[2][:, sub, 0:512], ps, ACT.Silu), inc="s_s")
                psum.release(bi, "s_s", vv)
            W("sync", "s_s")
            st(zs_s[grp * 512:(grp + 1) * 512, t0c:t1c].rearrange(
                "(s p) t -> p s t", p=128), GB[2][:, :, 0:512])
        W("sync", "s_v")
        st(bc_s[:, t0c:t1c], BCAST[64:96, 3, :])
        for mdi in range(16):
            bi, ps = psum.acquire()
            EM("tensor", lambda E, ps=ps, mdi=mdi: E.matmul(
                ps, DTW[:, mdi * 128:(mdi + 1) * 128], XD[0:64, :],
                start=True, stop=True), inc="s_pe")
            W("scalar", "s_pe")
            prog.wait("scalar", "s_st")
            vv = prog.emit("scalar", lambda E, ps=ps, mdi=mdi: E.activation(
                GB[3][:, 0, 0:512], ps, ACT.Identity,
                bias=p_dtpb[:, mdi:mdi + 1]), inc="s_s")
            psum.release(bi, "s_s", vv)
            W("vector", "s_s", vv)
            emit_softplus(prog, GB[3][:, 0, 0:512], GB[3][:, 1, 0:512], GB[3][:, 0, 0:512])
            ev(prog, lambda E, mdi=mdi: E.tensor_scalar_add(
                GB[3][:, 0, 0:512], GB[3][:, 0, 0:512], p_dtpb[:, mdi:mdi + 1]))
            emit_softplus(prog, GB[3][:, 0, 0:512], GB[3][:, 1, 0:512], GB[3][:, 0, 0:512])
            W("sync", "s_v")
            st(delta_s[mdi * 128:(mdi + 1) * 128, t0c:t1c], GB[3][:, 0, 0:512])

    if stop_after == "phC1":
        _finish(prog, nc, psum, out_hbm, tap_out, locals())
        return _close(nc, prog, stack, tap_out)

    for mt in range(NMT):
        t0c, t1c = mt * MT, (mt + 1) * MT
        XN1 = A[0]
        G2 = A[2]
        HE = A[1]
        W("sync", "s_v")
        W("sync", "s_s")
        W("sync", "s_pe")
        W("sync", "s_st")
        lvx = ld(XN1, xn1_s[:, t0c:t1c].rearrange("(dc p) t -> p dc t", p=128))
        W("vector", "s_pe")
        W("vector", "s_s")
        W("vector", "s_st")
        EM("vector", lambda E: E.memset(Af[2], 0.0), inc="s_v")
        bi, ps = psum.acquire()
        W("tensor", "s_ld", lvx)
        for dc in range(8):
            EM("tensor", lambda E, ps=ps, dc=dc, XN1=XN1: E.matmul(
                ps[0:8, :], p_mwg[:, dc, :], XN1[:, dc, :],
                start=(dc == 0), stop=(dc == 7)),
                inc=("s_pe" if dc == 7 else None))
        W("scalar", "s_pe")
        vlg = prog.emit("scalar", lambda E, ps=ps: E.activation(
            LOG, ps[0:8, :], ACT.Identity, bias=p_mbg), inc="s_s")
        psum.release(bi, "s_s", vlg)
        W("vector", "s_s", vlg)
        EM("vector", lambda E: E.memset(ONES8.rearrange("p a b -> p (a b)"), 1.0), inc="s_v")
        W("tensor", "s_s", vlg)
        bi2, ps2 = psum.acquire()
        for s in range(4):
            EM("tensor", lambda E, ps=ps2, s=s: E.transpose(
                ps[0:128, s * 8:(s + 1) * 8], LOG[0:8, s * 128:(s + 1) * 128],
                ident[0:8, 0:8]), inc=("s_pe" if s == 3 else None))
        W("vector", "s_pe")
        vtr = ev(prog, lambda E, ps=ps2: E.tensor_copy(
            LGT.rearrange("p a b -> p (a b)"), ps[:, 0:32]))
        psum.release(bi2, "s_pe" if False else "s_v", vtr)
        for s in range(4):
            ev(prog, lambda E, s=s: E.reduce_max(MXC[:, s:s + 1], LGT[:, s, :], axis=AX.X))
        for s in range(4):
            ev(prog, lambda E, s=s: E.tensor_scalar(
                LGT[:, s, :], LGT[:, s, :], MXC[:, s:s + 1], None, ALU.is_ge))
        for s in range(4):
            ev(prog, lambda E, s=s: E.tensor_tensor_scan(
                MTOK[:, s, :], ONES8[:, s, :], LGT[:, s, :], 0.0, ALU.mult, ALU.add))
        for s in range(4):
            ev(prog, lambda E, s=s: E.tensor_tensor(
                MTOK[:, s, :], MTOK[:, s, :], LGT[:, s, :], ALU.subtract))
        for s in range(4):
            ev(prog, lambda E, s=s: E.tensor_single_scalar(
                MTOK[:, s, :], MTOK[:, s, :], 0.5, ALU.is_le))
        for s in range(4):
            ev(prog, lambda E, s=s: E.tensor_tensor(
                MTOK[:, s, :], MTOK[:, s, :], LGT[:, s, :], ALU.mult))
        for e in range(NE):
            W("tensor", "s_v")
            bb0, psb0 = psum.acquire()
            for s in range(4):
                EM("tensor", lambda E, ps=psb0, s=s, e=e: E.transpose(
                    ps[0:1, s * 128:(s + 1) * 128], MTOK[:, s, e:e + 1],
                    ident), inc=("s_pe" if s == 3 else None))
            W("scalar", "s_pe")
            vmr = prog.emit("scalar", lambda E, ps=psb0: E.copy(
                rstat[0:1, 1536:2048], ps[0:1, 0:512]), inc="s_s")
            psum.release(bb0, "s_s", vmr)
            if "mrow" in tap_out and mt == NMT - 1:
                W("sync", "s_s", vmr)
                prog.dma("sync", tap_out["mrow"][e:e + 1, :], rstat[0:1, 1536:2048], "s_st")
                W("scalar", "s_st")
            W("tensor", "s_s", vmr)
            bb, psb = psum.acquire()
            EM("tensor", lambda E, ps=psb: E.matmul(
                ps, ones1, rstat[0:1, 1536:2048], start=True, stop=True), inc="s_pe")
            W("scalar", "s_pe")
            vmb = prog.emit("scalar", lambda E, ps=psb: E.copy(BCAST[:, 2, :], ps), inc="s_s")
            psum.release(bb, "s_s", vmb)
            for mi in range(8):
                wsl = WS[mi % 2]
                W("sync", "s_pe", ws_lastuse[mi % 2])
                lv = ld(wsl, mW1[e, :, mi * 128:(mi + 1) * 128].rearrange(
                    "(dc p) i -> p dc i", p=128))
                W("tensor", "s_ld", lv)
                bi, ps = psum.acquire()
                for dc in range(8):
                    EM("tensor", lambda E, ps=ps, wsl=wsl, dc=dc, XN1=XN1: E.matmul(
                        ps, wsl[:, dc, :], XN1[:, dc, :],
                        start=(dc == 0), stop=(dc == 7)),
                        inc=("s_pe" if dc == 7 else None))
                ws_lastuse[mi % 2] = prog.cnt["s_pe"]
                W("scalar", "s_pe")
                prog.wait("scalar", "s_pe")
                vv = prog.emit("scalar", lambda E, ps=ps, e=e, mi=mi: E.activation(
                    HE[:, mi, :], ps, ACT.Gelu_apprx_tanh,
                    bias=p_mb1[:, e, mi:mi + 1]), inc="s_s")
                psum.release(bi, "s_s", vv)
            W("tensor", "s_s")
            for md in range(8):
                wsl = WS[md % 2]
                W("sync", "s_pe", ws_lastuse[md % 2])
                lv = ld(wsl, mW2[e, :, md * 128:(md + 1) * 128].rearrange(
                    "(ic p) h -> p ic h", p=128))
                W("tensor", "s_ld", lv)
                bi, ps = psum.acquire()
                for ic in range(8):
                    EM("tensor", lambda E, ps=ps, wsl=wsl, ic=ic: E.matmul(
                        ps, wsl[:, ic, :], HE[:, ic, :],
                        start=(ic == 0), stop=(ic == 7)),
                        inc=("s_pe" if ic == 7 else None))
                ws_lastuse[md % 2] = prog.cnt["s_pe"]
                W("scalar", "s_pe")
                prog.wait("scalar", "s_v")
                vv = prog.emit("scalar", lambda E, ps=ps, e=e, md=md: E.activation(
                    GB[0][:, 0, 0:512], ps, ACT.Identity,
                    bias=p_mb2[:, e, md:md + 1]), inc="s_s")
                psum.release(bi, "s_s", vv)
                W("vector", "s_s", vv)
                ev(prog, lambda E, md=md: E.tensor_tensor(
                    GB[0][:, 0, 0:512], GB[0][:, 0, 0:512], BCAST[:, 2, :], ALU.mult))
                ev(prog, lambda E, md=md, G2=G2: E.tensor_tensor(
                    G2[:, md, :], G2[:, md, :], GB[0][:, 0, 0:512], ALU.add))
        if "log8" in tap_out and mt == NMT - 1:
            W("sync", "s_s")
            prog.dma("sync", tap_out["log8"], LOG, "s_st")
        if "mtok" in tap_out and mt == NMT - 1:
            W("sync", "s_v")
            prog.dma("sync", tap_out["mtok"], MTOK.rearrange("p a b -> p (a b)"), "s_st")
        W("scalar", "s_v")
        vs2 = prog.emit("scalar", lambda E: E.activation(Af[2], Af[2], ACT.Silu), inc="s_s")
        W("sync", "s_s", vs2)
        st(g2_s[:, t0c:t1c].rearrange("(dc p) t -> p dc t", p=128), G2)

    if stop_after == "phD":
        _finish(prog, nc, psum, out_hbm, tap_out, locals())
        return _close(nc, prog, stack, tap_out)

    TH = NT // 2
    for dc in range(16):
        DLT = Af[0]
        XCC = Af[1]
        DU = Af[2]
        YA = Af[3]
        W("sync", "s_v")
        W("sync", "s_s")
        W("sync", "s_pe")
        W("sync", "s_st")
        lvd = ld(DLT.rearrange("p (h x) -> p h x", h=2), delta_s[dc * 128:(dc + 1) * 128, :].rearrange("p (h x) -> p h x", h=2))
        lvc = ld(XCC.rearrange("p (h x) -> p h x", h=2), xc_s[dc * 128:(dc + 1) * 128, :].rearrange("p (h x) -> p h x", h=2))
        W("vector", "s_ld", lvc)
        W("scalar", "s_ld", lvc)
        ev(prog, lambda E: E.tensor_tensor(DU, DLT, XCC, ALU.mult))
        W("vector", "s_st")
        EM("vector", lambda E: E.memset(YA, 0.0), inc="s_v")
        EB = GBf[0][:, 0:2048]
        FB = GBf[1][:, 0:2048]
        SCO = GBf[2][:, 0:2048]
        for half in range(2):
            hsl = slice(half * TH, (half + 1) * TH)
            for n in range(DS):
                W("sync", "s_pe")
                lvb = prog.dma("sync", rowbuf[0:1, 0:2048],
                               bc_s[n:n + 1, half * TH:(half + 1) * TH], "s_ld")
                W("tensor", "s_ld", lvb)
                W("tensor", "s_v")
                for q in range(4):
                    EM("tensor", lambda E, q=q: E.matmul(
                        psBC[:, q * 512:(q + 1) * 512], ones1,
                        rowbuf[0:1, q * 512:(q + 1) * 512],
                        start=True, stop=True),
                        inc=("s_pe" if q == 3 else None))
                prog.wait("scalar", "s_v")
                vse = prog.emit("scalar", lambda E, dc=dc, n=n, hsl=hsl: E.activation(
                    EB, DLT[:, hsl], ACT.Exp, scale=p_aneg[:, dc, n:n + 1]), inc="s_s")
                W("vector", "s_pe")
                ev(prog, lambda E, hsl=hsl: E.tensor_tensor(FB, DU[:, hsl], psBC, ALU.mult))
                W("vector", "s_s", vse)
                if half == 0:
                    ev(prog, lambda E: E.tensor_tensor_scan(SCO, EB, FB, 0.0, ALU.mult, ALU.add))
                    ev(prog, lambda E, n=n: E.tensor_copy(carry[:, n:n + 1], SCO[:, TH - 1:TH]))
                else:
                    ev(prog, lambda E, n=n: E.tensor_tensor_scan(
                        SCO, EB, FB, carry[:, n:n + 1], ALU.mult, ALU.add))
                W("sync", "s_pe")
                lvb2 = prog.dma("sync", rowbuf[0:1, 0:2048],
                                bc_s[DS + n:DS + n + 1, half * TH:(half + 1) * TH], "s_ld")
                W("tensor", "s_ld", lvb2)
                W("tensor", "s_v")
                for q in range(4):
                    EM("tensor", lambda E, q=q: E.matmul(
                        psBC[:, q * 512:(q + 1) * 512], ones1,
                        rowbuf[0:1, q * 512:(q + 1) * 512],
                        start=True, stop=True),
                        inc=("s_pe" if q == 3 else None))
                W("vector", "s_pe")
                ev(prog, lambda E: E.tensor_tensor(FB, SCO, psBC, ALU.mult))
                ev(prog, lambda E, hsl=hsl: E.tensor_tensor(YA[:, hsl], YA[:, hsl], FB, ALU.add))
        ev(prog, lambda E, dc=dc: E.scalar_tensor_tensor(
            YA, XCC, p_dpar[:, dc:dc + 1], YA, ALU.mult, ALU.add))
        ZSC = Af[1]
        W("sync", "s_v")
        lvz = ld(ZSC.rearrange("p (h x) -> p h x", h=2), zs_s[dc * 128:(dc + 1) * 128, :].rearrange("p (h x) -> p h x", h=2))
        W("vector", "s_ld", lvz)
        vy = ev(prog, lambda E: E.tensor_tensor(YA, YA, ZSC, ALU.mult))
        W("sync", "s_v", vy)
        st(y_s[dc * 128:(dc + 1) * 128, :].rearrange("p (h x) -> p h x", h=2),
           YA.rearrange("p (h x) -> p h x", h=2))

    if stop_after == "phC2":
        _finish(prog, nc, psum, out_hbm, tap_out, locals())
        return _close(nc, prog, stack, tap_out)

    for mt in range(NMT):
        t0c, t1c = mt * MT, (mt + 1) * MT
        W("sync", "s_v")
        W("sync", "s_s")
        W("sync", "s_pe")
        W("sync", "s_st")
        Y0, Y1 = A[0], A[1]
        XNEW = A[2]
        G2 = A[3]
        lv0 = ld(Y0, y_s[0:1024, t0c:t1c].rearrange("(dc p) t -> p dc t", p=128))
        lv1 = ld(Y1, y_s[1024:2048, t0c:t1c].rearrange("(dc p) t -> p dc t", p=128))
        lv2 = ld(XNEW, xnew_s[:, t0c:t1c].rearrange("(dc p) t -> p dc t", p=128))
        lv3 = ld(G2, g2_s[:, t0c:t1c].rearrange("(dc p) t -> p dc t", p=128))
        import os as _os
        if _os.environ.get("PHC3_LOADS_ONLY"):
            continue
        for me in range(8):
            wsl = WB[me % 2]
            W("sync", "s_pe", ws_lastuse[me % 2])
            lvw = prog.dma("sync", wsl.rearrange("p (k m) -> p k m", m=128),
                           outpT_s[:, me * 128:(me + 1) * 128].rearrange(
                               "(k p) m -> p k m", p=128), "s_ld")
            W("tensor", "s_ld", lvw)
            bi, ps = psum.acquire()
            for k in range(16):
                EM("tensor", lambda E, ps=ps, wsl=wsl, k=k: E.matmul(
                    ps, wsl[:, k * 128:(k + 1) * 128],
                    (Y0 if k < 8 else Y1)[:, k % 8, :],
                    start=(k == 0), stop=(k == 15)),
                    inc=("s_pe" if k == 15 else None))
            ws_lastuse[me % 2] = prog.cnt["s_pe"]
            W("vector", "s_pe")
            FIN = GB[0].rearrange("p a b -> p (a b)")[:, 0:512]
            W("vector", "s_st")
            ev(prog, lambda E, ps=ps, me=me, FIN=FIN, G2=G2: E.tensor_tensor(
                FIN, ps, G2[:, me, :], ALU.mult))
            vv = ev(prog, lambda E, me=me, FIN=FIN, XNEW=XNEW: E.tensor_tensor(
                FIN, FIN, XNEW[:, me, :], ALU.add))
            psum.release(bi, "s_v", vv)
            W("tensor", "s_v", vv)
            bi2, ps2 = psum.acquire()
            for s in range(4):
                EM("tensor", lambda E, ps=ps2, FIN=FIN, s=s: E.transpose(
                    ps[:, s * 128:(s + 1) * 128], FIN[:, s * 128:(s + 1) * 128],
                    ident), inc=("s_pe" if s == 3 else None))
            W("scalar", "s_pe")
            prog.wait("scalar", "s_st")
            OUTT = GB[1].rearrange("p a b -> p (a b)")[:, 0:512]
            vv2 = prog.emit("scalar", lambda E, ps=ps2, OUTT=OUTT: E.copy(OUTT, ps), inc="s_s")
            psum.release(bi2, "s_s", vv2)
            W("sync", "s_s", vv2)
            for s in range(4):
                prog.dma("sync",
                         out_hbm[t0c + s * 128:t0c + (s + 1) * 128,
                                 me * 128:(me + 1) * 128],
                         OUTT[0:128, s * 128:(s + 1) * 128], "s_st")
            W("scalar", "s_st")

    out_written = True
    _finish(prog, nc, psum, out_hbm, tap_out, locals())
    return _close(nc, prog, stack, tap_out)


def _finish(prog, nc, psum, out_hbm, tap_out, L):
    W = prog.wait
    prog.steps["sync"]
    bounce = L["Af"][0]
    srcmap = {"xn": L["xn_s"], "qp": L["qp_s"], "kv": L["kv_s"], "ksum": L["ksum_s"],
              "w1T": L["w1T_s"], "w2qT": L["w2qT_s"],
              "xnew": L["xnew_s"], "xn1": L["xn1_s"], "xc": L["xc_s"],
              "zs": L["zs_s"], "delta": L["delta_s"], "bc": L["bc_s"],
              "yts": L["y_s"], "g2": L["g2_s"]}
    for s in ("s_v", "s_s", "s_pe", "s_st", "s_ld"):
        W("sync", s)
    sbmap = {"sb_log": L.get("LOG"), "sb_bcast2": None, "sb_ones1": L.get("ones1")}
    if "MTOK" in L:
        sbmap["sb_mtok"] = L["MTOK"].rearrange("p a b -> p (a b)")
        sbmap["sb_lgt"] = L["LGT"].rearrange("p a b -> p (a b)")
        sbmap["sb_mrow"] = L["rstat"][0:1, 1536:2048]
        sbmap["sb_bcast2"] = L["BCAST"][:, 2, :]
    for name, dst in tap_out.items():
        src = sbmap.get(name)
        if src is not None:
            prog.dma("sync", dst, src, "s_st")
            W("sync", "s_st")
    for name, dst in tap_out.items():
        src = srcmap.get(name)
        if src is None:
            continue
        R, C = src.shape
        for r0 in range(0, R, 128):
            rr = min(128, R - r0)
            lv = prog.dma("sync", bounce[0:rr, 0:C], src[r0:r0 + rr, :], "s_ld")
            W("sync", "s_ld", lv)
            prog.dma("sync", dst[r0:r0 + rr, :], bounce[0:rr, 0:C], "s_st")
            W("sync", "s_st")
    if not L.get("out_written"):
        zb = L["Af"][1]
        prog.emit("vector", lambda E, zb=zb: E.memset(zb, 0.0), inc="s_v")
        W("sync", "s_v")
        for mt in range(NMT):
            prog.dma("sync",
                     out_hbm[mt * MT:(mt + 1) * MT, :].rearrange("(s p) d -> p s d", p=128),
                     zb.rearrange("p (s d) -> p s d", d=D)[:, 0:4, :], "s_st")
    W("sync", "s_st")


def _close(nc, prog, stack, tap_out):
    with nc.Block() as block:
        prog.finalize(block)
    return nc, list(tap_out)



def kernel(**inputs):
    xfull = np.ascontiguousarray(np.asarray(inputs["L0_attn_x"], dtype=np.float32))
    B = xfull.shape[0]
    nc, taps = build(stop_after=STOP_AFTER, taps=TAPS)
    ident = np.eye(128, dtype=np.float32)
    base = {k: np.ascontiguousarray(np.asarray(v, dtype=np.float32))
            for k, v in inputs.items() if k != "L0_attn_x"}
    in_maps = []
    for c in range(B):
        m = dict(base)
        m["x"] = xfull[c]
        m["ident"] = ident
        in_maps.append(m)
    res = run_bass_kernel_spmd(nc, in_maps, core_ids=list(range(B)))
    out = np.stack([r["out"] for r in res.results], axis=0)
    kernel.last_results = res.results
    return out


# version 06
# speedup vs baseline: 1.1392x; 1.1185x over previous
import math
import contextlib
import numpy as np

import concourse.bass as bass
import concourse.mybir as mybir
from concourse.bass_utils import run_bass_kernel_spmd

F32 = mybir.dt.float32
BF16 = mybir.dt.bfloat16
ALU = mybir.AluOpType
ACT = mybir.ActivationFunctionType
AX = mybir.AxisListType

NT = 4096
D = 1024
MT = 512
NMT = NT // MT
E3 = 3072
FH = 2730
L0H = 4096
DI = 2048
DS = 16
DCONV = 4
DTR = 64
NE = 8
LN2HI = float(np.float32(0.6931471805599453))
LN2LO = float(np.float32(0.6931471805599453 - np.float64(np.float32(0.6931471805599453))))
LOG2E = float(np.float32(1.4426950408889634))
FLT_MIN = 1.17549435e-38

ENGS = ("sync", "vector", "scalar", "tensor", "gpsimd")

STOP_AFTER = None
TAPS = ()
DBG_OUT_SCRATCH = set()


class Prog:

    DMA_ENG = {"s_ld": "sync", "s_st": "sync", "s_g": "gpsimd"}

    def __init__(self, nc):
        self.nc = nc
        self.steps = {e: [] for e in ENGS}
        self.sem = {}
        self.cnt = {}
        self.waited = {e: {} for e in ENGS}
        self.pending_gate = {}

    def new_sem(self, name):
        self.sem[name] = self.nc.alloc_semaphore(name)
        self.cnt[name] = 0
        self.pending_gate[name] = 0

    def wait(self, eng, name, val=None):
        v = self.cnt[name] if val is None else val
        if v <= 0 or self.waited[eng].get(name, 0) >= v:
            return
        if name in self.DMA_ENG:
            self.pending_gate[name] = max(self.pending_gate[name], v)
        self.waited[eng][name] = v
        h = self.sem[name]
        self.steps[eng].append(lambda E, h=h, v=v: E.wait_ge(h, v))

    def emit(self, eng, fn, inc=None, inc_n=1):
        if inc is None:
            self.steps[eng].append(fn)
            return None
        self.cnt[inc] += inc_n
        h = self.sem[inc]

        def wrapped(E, fn=fn, h=h, n=inc_n):
            ins = fn(E)
            ins.then_inc(h, n)
        self.steps[eng].append(wrapped)
        return self.cnt[inc]

    def dma(self, eng, out, in_, inc, slow=False):
        g = self.pending_gate.get(inc, 0)
        if g > self.waited[eng].get(inc, 0):
            self.waited[eng][inc] = g
            h0 = self.sem[inc]
            self.steps[eng].append(lambda E, h=h0, v=g: E.wait_ge(h, v))
        if slow:
            return self.emit(eng, lambda E, o=out, i=in_: E.dma_start(
                out=o, in_=i, allow_slow_non_contiguous=True), inc=inc, inc_n=16)
        return self.emit(eng, lambda E, o=out, i=in_: E.dma_start(out=o, in_=i),
                         inc=inc, inc_n=16)

    def finalize(self, block):
        for e in ENGS:
            items = self.steps[e]
            if not items:
                continue
            def run(E, items=items):
                for it in items:
                    it(E)
            getattr(block, e)(run)


class PsumPool:

    def __init__(self, prog, nbanks=4):
        self.prog = prog
        nc = prog.nc
        self.tiles = [nc.alloc_psum_tensor(f"pb{i}", [128, 512], F32).ap()
                      for i in range(nbanks)]
        self.free_at = [None] * nbanks
        self.rr = 0
        self.n = nbanks

    def acquire(self):
        i = self.rr % self.n
        self.rr += 1
        if self.free_at[i] is not None:
            sem, val = self.free_at[i]
            self.prog.wait("tensor", sem, val)
        return i, self.tiles[i]

    def release(self, i, sem, val):
        self.free_at[i] = (sem, val)

    def manual(self, i):
        return self.tiles[i]


def ceil_div(a, b):
    return (a + b - 1) // b



def ev(prog, fn, inc="s_v", eng="vector"):
    return prog.emit(eng, fn, inc=inc)


def emit_elup1(prog, z_ap, tmp_ap, out_ap, ksum_ap=None):
    ev(prog, lambda E: E.tensor_scalar_min(tmp_ap, z_ap, 0.0))
    prog.wait("scalar", "s_v")
    prog.emit("scalar", lambda E: E.activation(tmp_ap, tmp_ap, ACT.Exp), inc="s_s")
    prog.wait("vector", "s_s")
    if ksum_ap is None:
        ev(prog, lambda E: E.scalar_tensor_tensor(out_ap, z_ap, 0.0, tmp_ap, ALU.max, ALU.add))
    else:
        ev(prog, lambda E: E.scalar_tensor_tensor(out_ap, z_ap, 0.0, tmp_ap, ALU.max, ALU.add,
                                                  accum_out=ksum_ap))


def emit_softplus(prog, x_ap, tmp_ap, out_ap):
    prog.wait("scalar", "s_v")
    prog.emit("scalar", lambda E: E.activation(tmp_ap, x_ap, ACT.Abs), inc="s_s")
    prog.emit("scalar", lambda E: E.activation(tmp_ap, tmp_ap, ACT.Exp, scale=-1.0), inc="s_s")
    prog.emit("scalar", lambda E: E.activation(tmp_ap, tmp_ap, ACT.Ln, bias=1.0), inc="s_s")
    prog.wait("vector", "s_s")
    ev(prog, lambda E: E.scalar_tensor_tensor(out_ap, x_ap, 0.0, tmp_ap, ALU.max, ALU.add))


def emit_quantize(prog, x_ap, t, out_ap, man, min_e, max_val, signed=True,
                  nb_e=4, sat=1000.0):
    T0, T1, T2, T3, T4, T5 = t[:6]
    nb_r = man + 2
    prog.wait("scalar", "s_v")
    if signed:
        prog.emit("scalar", lambda E: E.activation(T0, x_ap, ACT.Sign), inc="s_s")
    prog.emit("scalar", lambda E: E.activation(T1, x_ap, ACT.Abs), inc="s_s")
    prog.wait("vector", "s_s")
    ev(prog, lambda E: E.tensor_scalar(T1, T1, FLT_MIN, float(sat), ALU.max, ALU.min))
    prog.wait("scalar", "s_v")
    prog.emit("scalar", lambda E: E.activation(T1, T1, ACT.Ln), inc="s_s")
    prog.wait("vector", "s_s")
    ev(prog, lambda E: E.tensor_scalar(T1, T1, LOG2E, float(-min_e), ALU.mult, ALU.add))
    ev(prog, lambda E: E.tensor_scalar(T3, T1, 0.0, float(2 ** nb_e) - 1e-4,
                                       ALU.max, ALU.min))
    ev(prog, lambda E: E.tensor_copy(T1, T3))
    for k in range(nb_e - 1, -1, -1):
        b = float(1 << k)
        ev(prog, lambda E, b=b: E.tensor_scalar(T4, T1, b, b, ALU.is_ge, ALU.mult))
        ev(prog, lambda E: E.tensor_tensor(T1, T1, T4, ALU.subtract))
    ev(prog, lambda E: E.tensor_tensor(T2, T3, T1, ALU.subtract))
    C = float(man - min_e)
    ev(prog, lambda E: E.tensor_scalar(T1, T2, -1.0, C, ALU.mult, ALU.add))
    ev(prog, lambda E: E.tensor_scalar(T3, T1, LN2HI, None, ALU.mult))
    ev(prog, lambda E: E.scalar_tensor_tensor(T3, T1, LN2LO, T3, ALU.mult, ALU.add))
    prog.wait("scalar", "s_v")
    prog.emit("scalar", lambda E: E.activation(T1, T3, ACT.Exp), inc="s_s")
    prog.emit("scalar", lambda E: E.activation(T3, T3, ACT.Exp, scale=-1.0), inc="s_s")
    prog.wait("vector", "s_s")
    ev(prog, lambda E: E.tensor_tensor(T1, x_ap, T1, ALU.mult))
    if signed:
        prog.wait("scalar", "s_v")
        prog.emit("scalar", lambda E: E.activation(T1, T1, ACT.Abs), inc="s_s")
        prog.wait("vector", "s_s")
        ev(prog, lambda E: E.tensor_scalar(T1, T1, 0.5 + 1e-9, None, ALU.add))
    else:
        ev(prog, lambda E: E.tensor_scalar(T1, T1, 0.5 + 1e-9, None, ALU.add))
    ev(prog, lambda E: E.tensor_copy(T2, T1))
    for k in range(nb_r - 1, -1, -1):
        b = float(1 << k)
        ev(prog, lambda E, b=b: E.tensor_scalar(T4, T1, b, b, ALU.is_ge, ALU.mult))
        ev(prog, lambda E: E.tensor_tensor(T1, T1, T4, ALU.subtract))
    ev(prog, lambda E: E.tensor_single_scalar(T5, T1, 2e-9, ALU.is_le))
    ev(prog, lambda E: E.tensor_tensor(T5, T5, T4, ALU.mult))
    ev(prog, lambda E: E.tensor_tensor(T1, T1, T5, ALU.add))
    ev(prog, lambda E: E.tensor_tensor(T1, T2, T1, ALU.subtract))
    ev(prog, lambda E: E.tensor_tensor(T1, T1, T3, ALU.mult))
    ovf_thr = max_val * (1.0 + 2.0 ** -(man + 2))
    ev(prog, lambda E: E.tensor_single_scalar(T2, T1, ovf_thr, ALU.is_gt))
    if signed:
        ev(prog, lambda E: E.tensor_tensor(T1, T1, T0, ALU.mult))
    prog.wait("scalar", "s_v")
    prog.emit("scalar", lambda E: E.activation(T5, T2, ACT.Exp, scale=710.0), inc="s_s")
    prog.wait("vector", "s_s")
    ev(prog, lambda E: E.tensor_tensor(out_ap, T1, T5, ALU.mult))



def build(stop_after=None, taps=()):
    nc = bass.Bass("TRN2", target_bir_lowering=False,
                   detect_race_conditions=False)
    stack = contextlib.ExitStack()

    def din(name, shape):
        return nc.dram_tensor(name, list(shape), F32, kind="ExternalInput").ap()

    x_in = din("x", (NT, D))
    Wqkv = din("L0_attn_Wqkv", (D, E3))
    bqkv = din("L0_attn_bqkv", (E3,))
    Wg = din("L0_attn_Wg", (D, FH))
    bg = din("L0_attn_bg", (FH,))
    Wu = din("L0_attn_Wu", (D, FH))
    bu = din("L0_attn_bu", (FH,))
    Wd = din("L0_attn_Wd", (FH, D))
    bd = din("L0_attn_bd", (D,))
    g1 = din("L0_attn_g1", (D,))
    b1 = din("L0_attn_b1", (D,))
    g2 = din("L0_attn_g2", (D,))
    b2 = din("L0_attn_b2", (D,))
    fw1 = din("L0_ffn_w1", (L0H, D))
    fb1 = din("L0_ffn_b1", (L0H,))
    fw2 = din("L0_ffn_w2", (D, L0H))
    fb2 = din("L0_ffn_b2", (D,))
    inpw = din("L1_attn_in_proj_w", (2 * DI, D))
    convw = din("L1_attn_conv_w", (DI, DCONV))
    convb = din("L1_attn_conv_b", (DI,))
    xpw = din("L1_attn_x_proj_w", (DTR + 2 * DS, DI))
    dtpw = din("L1_attn_dt_proj_w", (DI, DTR))
    dtpb = din("L1_attn_dt_proj_b", (DI,))
    alog = din("L1_attn_A_log", (DI, DS))
    dpar = din("L1_attn_D", (DI,))
    outpw = din("L1_attn_out_proj_w", (D, DI))
    mWg = din("L1_ffn_Wg", (D, NE))
    mbg = din("L1_ffn_bg", (NE,))
    mW1 = din("L1_ffn_W1", (NE, D, D))
    mb1 = din("L1_ffn_b1", (NE, D))
    mW2 = din("L1_ffn_W2", (NE, D, D))
    mb2 = din("L1_ffn_b2", (NE, D))
    ident_in = din("ident", (128, 128))

    out_hbm = nc.dram_tensor("out", [NT, D], F32, kind="ExternalOutput").ap()

    def scr(name, shape, dt=F32):
        kind = "ExternalOutput" if name in DBG_OUT_SCRATCH else "Internal"
        return nc.dram_tensor(name, list(shape), dt, kind=kind).ap()

    xn_s = scr("xn_s", (D, NT))
    qp_s = scr("qp_s", (D, NT))
    kv_s = scr("kv_s", (D, D))
    ksum_s = scr("ksum_s", (1, D))
    xnew_s = scr("xnew_s", (D, NT))
    xn1_s = scr("xn1_s", (D, NT))
    xc_s = scr("xc_s", (DI, NT))
    zs_s = scr("zs_s", (DI, NT))
    delta_s = scr("delta_s", (DI, NT))
    bc_s = scr("bc_s", (2 * DS, NT))
    y_s = scr("y_s", (DI, NT))
    g2_s = scr("g2_s", (D, NT))
    w1T_s = scr("w1T_s", (D, L0H))
    w2qT_s = scr("w2qT_s", (L0H, D))
    inpT_s = scr("inpT_s", (D, 2 * DI))
    outpT_s = scr("outpT_s", (DI, D))
    xpT_s = scr("xpT_s", (DI, DTR + 2 * DS))
    dtpT_s = scr("dtpT_s", (DTR, DI))
    bc_rep = scr("bc_rep", (32, 128, NT))

    tap_out = {}
    for tname in taps:
        shp = {"xn": (D, NT), "qp": (D, NT), "kv": (D, D), "ksum": (1, D),
               "w1T": (D, L0H), "w2qT": (L0H, D), "g": (D, NT), "x1": (D, NT), "x1r": (D, NT), "rden": (1, NT), "denr": (1, NT),
               "s0": (D, NT), "xnew": (D, NT), "xn1": (D, NT),
               "xc": (DI, NT), "zs": (DI, NT), "delta": (DI, NT),
               "bc": (2 * DS, NT), "yts": (DI, NT), "g2": (D, NT),
               "kpt": (128, 4096), "vtm": (128, 4096), "kpf": (128, 4096), "log8": (8, MT), "mtok": (128, 32), "mrow": (NE, MT),
               "sb_log": (8, MT), "sb_mtok": (128, 32), "sb_mrow": (1, MT),
               "sb_bcast2": (128, MT), "sb_ones1": (1, 128), "sb_lgt": (128, 32),
               }[tname]
        tap_out[tname] = nc.dram_tensor("tap_" + tname, list(shp), F32,
                                        kind="ExternalOutput").ap()

    def sb(name, shape, dt=F32):
        return nc.alloc_sbuf_tensor(name, list(shape), dt).ap()

    A = [sb(f"A{i}", (128, NMT, MT)) for i in range(4)]
    Af = [a.rearrange("p a b -> p (a b)") for a in A]
    GB = [sb("GB0", (128, 4, 544)), sb("GB1", (128, 4, 512)),
          sb("GB2", (128, 4, 512)), sb("GB3", (128, 2, 512))]
    QT = sb("QT", (128, 8, 512))
    WS = [sb(f"WS{i}", (128, 8, 128)) for i in range(2)]
    WB = [sb(f"WB{i}", (128, 2048)) for i in range(2)]
    ident = sb("identsb", (128, 128))
    ones1 = sb("ones1", (1, 128))
    ones_col = sb("ones_col", (128, 1))
    KV = sb("KV", (128, 8, 1024))
    KSUM = sb("KSUM", (128, 8))
    stat = sb("stat", (128, 32))
    rstat = sb("rstat", (1, 4 * MT))
    rowbuf = rstat
    BCAST = sb("BCAST", (128, 4, MT))
    carry = sb("carry", (128, 256))
    halo = sb("halo", (128, 16, 4))
    p_bqkv = sb("p_bqkv", (128, 24))
    p_bg = sb("p_bg", (128, 22))
    p_bu = sb("p_bu", (128, 22))
    p_bd = sb("p_bd", (128, 8))
    p_g1 = sb("p_g1", (128, 8))
    p_b1 = sb("p_b1", (128, 8))
    p_g2 = sb("p_g2", (128, 8))
    p_b2 = sb("p_b2", (128, 8))
    p_fb1 = sb("p_fb1", (128, 32))
    p_fb2q = sb("p_fb2q", (128, 8))
    p_convw = sb("p_convw", (128, 16, 4))
    p_convb = sb("p_convb", (128, 16))
    p_dtpb = sb("p_dtpb", (128, 16))
    p_aneg = sb("p_aneg", (128, 16, 16))
    p_dpar = sb("p_dpar", (128, 16))
    p_mb1 = sb("p_mb1", (128, NE, 8))
    p_mb2 = sb("p_mb2", (128, NE, 8))
    p_mbg = sb("p_mbg", (8, 1))
    p_mwg = sb("p_mwg", (128, 8, 8))
    DTW = sb("DTW", (64, DI))
    XPW = sb("XPW", (128, 16, 96))
    LOG = sb("LOG", (8, MT))
    LGT = sb("LGT", (128, 4, 8))
    MXC = sb("MXC", (128, 4))
    ONES8 = sb("ONES8", (128, 4, 8))
    MTOK = sb("MTOK", (128, 4, 8))
    qtmp = [QT[:, i, :] for i in range(8)]

    prog = Prog(nc)
    for s in ("s_ld", "s_st", "s_pe", "s_v", "s_s", "s_g"):
        prog.new_sem(s)
    psum = PsumPool(prog, nbanks=4)
    psBC = nc.alloc_psum_tensor("psBC", [128, 2048], F32).ap()

    W = prog.wait
    EM = prog.emit

    def ld(out, in_, slow=False):
        return prog.dma("sync", out, in_, "s_ld", slow=slow)

    def st(out, in_):
        return prog.dma("sync", out, in_, "s_st")

    EM("vector", lambda E: E.memset(ones1, 1.0), inc="s_v")
    EM("vector", lambda E: E.memset(ones_col, 1.0), inc="s_v")
    for cv in (1e-5, -0.5):
        cst = sb(f"cst_{cv}", (128, 1))
        nc.const_aps.register(F32, float(cv), cst)
        EM("vector", lambda E, t=cst, v=float(cv): E.memset(t, v), inc="s_v")
    W("scalar", "s_v")
    W("tensor", "s_v")
    ld(ident, ident_in)
    ld(p_bqkv, bqkv.rearrange("(j p) -> p j", p=128), slow=True)
    ld(p_bg[:, 0:21], bg[0:2688].rearrange("(j p) -> p j", p=128), slow=True)
    ld(p_bg[0:42, 21:22], bg[2688:2730].rearrange("(j p) -> p j", p=42), slow=True)
    ld(p_bu[:, 0:21], bu[0:2688].rearrange("(j p) -> p j", p=128), slow=True)
    ld(p_bu[0:42, 21:22], bu[2688:2730].rearrange("(j p) -> p j", p=42), slow=True)
    ld(p_bd, bd.rearrange("(j p) -> p j", p=128), slow=True)
    ld(p_g1, g1.rearrange("(j p) -> p j", p=128), slow=True)
    ld(p_b1, b1.rearrange("(j p) -> p j", p=128), slow=True)
    ld(p_g2, g2.rearrange("(j p) -> p j", p=128), slow=True)
    ld(p_b2, b2.rearrange("(j p) -> p j", p=128), slow=True)
    ld(p_fb1, fb1.rearrange("(j p) -> p j", p=128), slow=True)
    ld(p_fb2q, fb2.rearrange("(j p) -> p j", p=128), slow=True)
    ld(p_convw, convw.rearrange("(j p) c -> p j c", p=128), slow=True)
    ld(p_convb, convb.rearrange("(j p) -> p j", p=128), slow=True)
    ld(p_dtpb, dtpb.rearrange("(j p) -> p j", p=128), slow=True)
    ld(p_aneg, alog.rearrange("(j p) n -> p j n", p=128), slow=True)
    ld(p_dpar, dpar.rearrange("(j p) -> p j", p=128), slow=True)
    ld(p_mb1, mb1.rearrange("e (j p) -> p e j", p=128), slow=True)
    ld(p_mb2, mb2.rearrange("e (j p) -> p e j", p=128), slow=True)
    ld(p_mbg, mbg[:, None], slow=True)
    ld(p_mwg, mWg.rearrange("(dc p) e -> p dc e", p=128), slow=True)
    ld(rowbuf[:, 0:E3 - D], bqkv[D:E3][None, :])
    W("vector", "s_ld")
    W("scalar", "s_ld")
    EM("scalar", lambda E: E.activation(p_aneg, p_aneg, ACT.Exp), inc="s_s")
    W("vector", "s_s")
    ev(prog, lambda E: E.tensor_scalar_mul(p_aneg, p_aneg, -1.0))
    b2slab = QT[:, 6, :]
    EM("vector", lambda E: E.memset(b2slab, 1.0), inc="s_v")
    W("vector", "s_ld")
    ev(prog, lambda E: E.tensor_copy(b2slab[:, 0:8], p_fb2q))
    emit_quantize(prog, b2slab, [QT[:, i, :] for i in range(6)], b2slab,
                  man=10, min_e=-14, max_val=65504.0, signed=True,
                  nb_e=5, sat=1.0e5)
    ev(prog, lambda E: E.tensor_copy(p_fb2q, b2slab[:, 0:8]))

    prep_state = {}
    def prep_transpose(src_ap, dst_ap, R, C, quant=None):
        CB = min(C, 1024)
        stg = WS[0].rearrange("p a b -> p (a b)")[:, 0:512]
        wb_lastuse = prep_state.setdefault("wb_lastuse", [0, 0])
        for r0 in range(0, R, 128):
            rr = min(128, R - r0)
            wi = (r0 // 128) % 2
            wsl = WB[wi]
            for c0 in range(0, C, CB):
                cb = min(CB, C - c0)
                W("sync", "s_pe", wb_lastuse[wi])
                lv = ld(wsl[0:rr, 0:cb], src_ap[r0:r0 + rr, c0:c0 + cb])
                if quant is not None:
                    W("vector", "s_ld", lv)
                    W("scalar", "s_ld", lv)
                    quant(wsl[0:rr, 0:cb])
                    W("tensor", "s_v")
                W("tensor", "s_ld", lv)
                for cc in range(0, cb, 512):
                    nj = ceil_div(min(512, cb - cc), 128)
                    bi, ps = psum.acquire()
                    for j in range(nj):
                        jw = min(128, cb - cc - j * 128)
                        EM("tensor", lambda E, ps=ps, wsl=wsl, cc=cc, j=j, jw=jw, rr=rr: E.transpose(
                            ps[0:jw, j * 128:j * 128 + rr],
                            wsl[0:rr, cc + j * 128:cc + j * 128 + jw],
                            ident[0:rr, 0:rr]),
                            inc=("s_pe" if j == nj - 1 else None))
                    W("scalar", "s_pe")
                    W("scalar", "s_st")
                    vv = prog.emit("scalar", lambda E, ps=ps: E.copy(stg, ps),
                                   inc="s_s")
                    psum.release(bi, "s_s", vv)
                    W("sync", "s_s", vv)
                    for j in range(nj):
                        jw = min(128, cb - cc - j * 128)
                        st(dst_ap[c0 + cc + j * 128: c0 + cc + j * 128 + jw,
                                  r0:r0 + rr],
                           stg[0:jw, j * 128:j * 128 + rr])
                wb_lastuse[wi] = prog.cnt["s_pe"]

    def wq_quant_slab(ap_slab):
        for h in range(2):
            sl = ap_slab[:, h * 512:(h + 1) * 512]
            emit_quantize(prog, sl, [qtmp[i] for i in range(6)], sl,
                          man=3, min_e=-6, max_val=240.0, signed=True,
                          nb_e=4, sat=1000.0)

    prep_transpose(fw1, w1T_s, L0H, D)
    prep_transpose(fw2, w2qT_s, D, L0H, quant=wq_quant_slab)
    prep_transpose(inpw, inpT_s, 2 * DI, D)
    prep_transpose(outpw, outpT_s, D, DI)
    prep_transpose(xpw, xpT_s, DTR + 2 * DS, DI)
    prep_transpose(dtpw, dtpT_s, DI, DTR)
    W("sync", "s_st")
    ld(DTW, dtpT_s)
    ld(XPW, xpT_s.rearrange("(dc p) r -> p dc r", p=128))

    SQB = QT[:, 0:2, :].rearrange("p a b -> p (a b)")
    EM("vector", lambda E: E.memset(KV.rearrange("p a b -> p (a b)"), 0.0), inc="s_v")
    EM("vector", lambda E: E.memset(rowbuf[0:1, 0:1024], 0.0), inc="s_v")
    W("sync", "s_st")
    ks_started = [False]
    ws_lastuse = [0, 0]
    W("scalar", "s_v")
    for mt in range(NMT):
        XTf = Af[0]
        W("sync", "s_pe")
        W("vector", "s_pe")
        lv = ld(XTf.rearrange("p (s d) -> p s d", d=D),
                x_in[mt * MT:(mt + 1) * MT, :].rearrange("(s p) d -> p s d", p=128))
        W("vector", "s_ld", lv)
        W("scalar", "s_ld", lv)
        for s in range(4):
            xs = XTf[:, s * D:(s + 1) * D]
            ev(prog, lambda E, xs=xs, s=s: E.reduce_sum(stat[:, s:s + 1], xs, axis=AX.X))
            W("scalar", "s_v")
            prog.emit("scalar", lambda E, xs=xs: E.activation(SQB, xs, ACT.Square), inc="s_s")
            W("vector", "s_s")
            ev(prog, lambda E, s=s: E.reduce_sum(stat[:, 4 + s:5 + s], SQB, axis=AX.X))
        W("vector", "s_v")
        ev(prog, lambda E: E.tensor_scalar_mul(stat[:, 8:12], stat[:, 0:4], 1.0 / D))
        ev(prog, lambda E: E.tensor_scalar_mul(stat[:, 12:16], stat[:, 4:8], 1.0 / D))
        W("scalar", "s_v")
        prog.emit("scalar", lambda E: E.activation(stat[:, 16:20], stat[:, 8:12], ACT.Square), inc="s_s")
        W("vector", "s_s")
        ev(prog, lambda E: E.tensor_tensor(stat[:, 12:16], stat[:, 12:16], stat[:, 16:20], ALU.subtract))
        W("scalar", "s_v")
        prog.emit("scalar", lambda E: E.activation(stat[:, 12:16], stat[:, 12:16], ACT.Ln, bias=1e-5), inc="s_s")
        prog.emit("scalar", lambda E: E.activation(stat[:, 12:16], stat[:, 12:16], ACT.Exp, scale=-0.5), inc="s_s")
        W("vector", "s_s")
        XN = Af[1]
        for s in range(4):
            ev(prog, lambda E, s=s: E.tensor_scalar(
                XN[:, s * D:(s + 1) * D], XTf[:, s * D:(s + 1) * D],
                stat[:, 8 + s:9 + s], stat[:, 12 + s:13 + s], ALU.subtract, ALU.mult))
        XF = A[2]
        W("tensor", "s_v")
        for dc in range(8):
            bi, ps = psum.acquire()
            for s in range(4):
                EM("tensor", lambda E, ps=ps, dc=dc, s=s: E.transpose(
                    ps[:, s * 128:(s + 1) * 128],
                    XN[:, s * D + dc * 128: s * D + (dc + 1) * 128], ident),
                    inc=("s_pe" if s == 3 else None))
            W("scalar", "s_pe")
            v = prog.emit("scalar", lambda E, ps=ps, dc=dc: E.copy(XF[:, dc, :], ps), inc="s_s")
            psum.release(bi, "s_s", v)
        W("sync", "s_s")
        st(xn_s[:, mt * MT:(mt + 1) * MT].rearrange("(dc p) t -> p dc t", p=128),
           XF)
        KPF = Af[0]
        VF = Af[1]
        for ec in range(24):
            wsl = WS[ec % 2]
            W("sync", "s_pe", ws_lastuse[ec % 2])
            lv = ld(wsl, Wqkv[:, ec * 128:(ec + 1) * 128].rearrange("(dc p) e -> p dc e", p=128))
            W("tensor", "s_ld", lv)
            W("tensor", "s_s")
            bi, ps = psum.acquire()
            for dc in range(8):
                EM("tensor", lambda E, ps=ps, wsl=wsl, dc=dc: E.matmul(
                    ps, wsl[:, dc, :], XF[:, dc, :],
                    start=(dc == 0), stop=(dc == 7)),
                    inc=("s_pe" if dc == 7 else None))
            ws_lastuse[ec % 2] = prog.cnt["s_pe"]
            W("scalar", "s_pe")
            W("vector", "s_pe")
            if ec < 8:
                zb = QT[:, 6, :]
                W("scalar", "s_st")
                v1 = prog.emit("scalar", lambda E, ps=ps, ec=ec, zb=zb: E.activation(
                    zb, ps, ACT.Identity, bias=p_bqkv[:, ec:ec + 1]), inc="s_s")
                psum.release(bi, "s_s", v1)
                W("vector", "s_s", v1)
                emit_elup1(prog, zb, QT[:, 7, :], zb)
                W("sync", "s_v")
                st(qp_s[ec * 128:(ec + 1) * 128, mt * MT:(mt + 1) * MT], zb)
                W("vector", "s_st")
            else:
                dstf = KPF if ec < 16 else VF
                cc = (ec - 8) % 8
                zb = dstf[:, cc * 512:(cc + 1) * 512]
                v1 = prog.emit("scalar", lambda E, ps=ps, ec=ec, zb=zb: E.activation(
                    zb, ps, ACT.Identity, bias=p_bqkv[:, ec:ec + 1]), inc="s_s")
                psum.release(bi, "s_s", v1)
                if ec < 16:
                    W("vector", "s_s", v1)
                    emit_elup1(prog, zb, QT[:, 7, :], zb)
        KPT = A[3]
        KPTf = Af[3]
        VTf = Af[2]
        W("tensor", "s_v")
        W("tensor", "s_s")
        W("scalar", "s_st")
        for src, dstf in ((KPF, KPTf), (VF, VTf)):
            for tc in range(4):
                for eh in range(2):
                    bi, ps = psum.acquire()
                    for j in range(4):
                        ecol = eh * 4 + j
                        EM("tensor", lambda E, ps=ps, src=src, tc=tc, ecol=ecol, j=j: E.transpose(
                            ps[:, j * 128:(j + 1) * 128],
                            src[:, ecol * 512 + tc * 128: ecol * 512 + (tc + 1) * 128],
                            ident),
                            inc=("s_pe" if j == 3 else None))
                    W("scalar", "s_pe")
                    v = prog.emit("scalar", lambda E, ps=ps, dstf=dstf, tc=tc, eh=eh: E.copy(
                        dstf[:, tc * 1024 + eh * 512: tc * 1024 + (eh + 1) * 512], ps),
                        inc="s_s")
                    psum.release(bi, "s_s", v)
        W("tensor", "s_s")
        for dc in range(8):
            for ev2 in range(2):
                bi, ps = psum.acquire()
                for tc in range(4):
                    EM("tensor", lambda E, ps=ps, dc=dc, ev2=ev2, tc=tc: E.matmul(
                        ps, KPTf[:, tc * 1024 + dc * 128: tc * 1024 + (dc + 1) * 128],
                        VTf[:, tc * 1024 + ev2 * 512: tc * 1024 + (ev2 + 1) * 512],
                        start=(tc == 0), stop=(tc == 3)),
                        inc=("s_pe" if tc == 3 else None))
                W("vector", "s_pe")
                v = ev(prog, lambda E, ps=ps, dc=dc, ev2=ev2: E.tensor_tensor(
                    KV[:, dc, ev2 * 512:(ev2 + 1) * 512],
                    KV[:, dc, ev2 * 512:(ev2 + 1) * 512], ps, ALU.add))
                psum.release(bi, "s_v", v)
        for eh in range(2):
            bks, ps_ks = psum.acquire()
            for tc in range(4):
                EM("tensor", lambda E, tc=tc, eh=eh, ps=ps_ks: E.matmul(
                    ps[0:1, 0:512], ones_col,
                    KPTf[:, tc * 1024 + eh * 512: tc * 1024 + (eh + 1) * 512],
                    start=(tc == 0), stop=(tc == 3)),
                    inc=("s_pe" if tc == 3 else None))
            W("vector", "s_pe")
            vks = ev(prog, lambda E, eh=eh, ps=ps_ks: E.tensor_tensor(
                rowbuf[0:1, eh * 512:(eh + 1) * 512],
                rowbuf[0:1, eh * 512:(eh + 1) * 512], ps[0:1, 0:512], ALU.add))
            psum.release(bks, "s_v", vks)


    if "kpt" in tap_out:
        W("sync", "s_v")
        W("sync", "s_s")
        prog.dma("sync", tap_out["kpt"], Af[3], "s_st")
    if "vtm" in tap_out:
        W("sync", "s_v")
        W("sync", "s_s")
        prog.dma("sync", tap_out["vtm"], Af[2], "s_st")
    if "kpf" in tap_out:
        W("sync", "s_v")
        W("sync", "s_s")
        prog.dma("sync", tap_out["kpf"], Af[0], "s_st")
    W("sync", "s_v")
    st(ksum_s, rowbuf[0:1, 0:1024])
    st(kv_s.rearrange("(dc p) e -> p dc e", p=128), KV)
    W("sync", "s_st")
    lv = ld(KSUM, ksum_s[0, :].rearrange("(j p) -> p j", p=128), slow=True)

    if stop_after == "phA":
        _finish(prog, nc, psum, out_hbm, tap_out, locals())
        return _close(nc, prog, stack, tap_out)

    KVf = KV.rearrange("p a b -> p (a b)")
    QTf = QT.rearrange("p a b -> p (a b)")
    GBf = [g.rearrange("p a b -> p (a b)") for g in GB]
    q2 = [KVf[:, i * 2048:(i + 1) * 2048] for i in range(4)] + \
         [QTf[:, i * 2048:(i + 1) * 2048] for i in range(2)]
    SQ2 = [QT[:, 0, :], QT[:, 1, :]]

    def emit_fmln(src3, gcol3=None, bcol3=None):
        W("tensor", "s_v")
        W("scalar", "s_v")
        bmu, ps_mu = psum.acquire()
        for dc in range(8):
            EM("tensor", lambda E, ps=ps_mu, dc=dc: E.matmul(
                ps[0:1, :], ones_col, src3[:, dc, :],
                start=(dc == 0), stop=(dc == 7)),
                inc=("s_pe" if dc == 7 else None))
        bm2, ps_m2 = psum.acquire()
        sq_free = [0, 0]
        for dc in range(8):
            sq = SQ2[dc % 2]
            prog.wait("scalar", "s_pe", sq_free[dc % 2])
            vsq = prog.emit("scalar", lambda E, sq=sq, dc=dc: E.activation(
                sq, src3[:, dc, :], ACT.Square), inc="s_s")
            W("tensor", "s_s", vsq)
            sq_free[dc % 2] = EM("tensor", lambda E, ps=ps_m2, sq=sq, dc=dc: E.matmul(
                ps[0:1, :], ones_col, sq,
                start=(dc == 0), stop=(dc == 7)), inc="s_pe")
        r0 = rstat[0:1, 0:512]; r1 = rstat[0:1, 512:1024]
        r2 = rstat[0:1, 1024:1536]; r3 = rstat[0:1, 1536:2048]
        W("vector", "s_pe")
        ev(prog, lambda E, ps=ps_mu: E.tensor_scalar_mul(r0, ps[0:1, :], 1.0 / D))
        ev(prog, lambda E, ps=ps_m2: E.tensor_scalar_mul(r1, ps[0:1, :], 1.0 / D))
        v1 = ev(prog, lambda E: E.tensor_tensor(r3, r0, r0, ALU.mult))
        v2 = ev(prog, lambda E: E.tensor_tensor(r1, r1, r3, ALU.subtract))
        psum.release(bmu, "s_v", v1)
        psum.release(bm2, "s_v", v2)
        W("scalar", "s_v")
        prog.emit("scalar", lambda E: E.activation(r1, r1, ACT.Ln, bias=1e-5), inc="s_s")
        prog.emit("scalar", lambda E: E.activation(r1, r1, ACT.Exp, scale=-0.5), inc="s_s")
        W("vector", "s_s")
        ev(prog, lambda E: E.tensor_tensor(r2, r0, r1, ALU.mult))
        ev(prog, lambda E: E.tensor_scalar_mul(r2, r2, -1.0))
        W("tensor", "s_v")
        for i, row in ((0, r1), (1, r2)):
            bb, psb = psum.acquire()
            EM("tensor", lambda E, ps=psb, row=row: E.matmul(
                ps, ones1, row, start=True, stop=True), inc="s_pe")
            W("scalar", "s_pe")
            vv = prog.emit("scalar", lambda E, ps=psb, i=i: E.copy(BCAST[:, i, :], ps), inc="s_s")
            psum.release(bb, "s_s", vv)
        W("vector", "s_s")
        for dc in range(8):
            ev(prog, lambda E, dc=dc: E.tensor_tensor(src3[:, dc, :], src3[:, dc, :], BCAST[:, 0, :], ALU.mult))
            ev(prog, lambda E, dc=dc: E.tensor_tensor(src3[:, dc, :], src3[:, dc, :], BCAST[:, 1, :], ALU.add))
            if gcol3 is not None:
                ev(prog, lambda E, dc=dc: E.tensor_scalar(
                    src3[:, dc, :], src3[:, dc, :], gcol3[:, dc:dc + 1], bcol3[:, dc:dc + 1],
                    ALU.mult, ALU.add))

    mh_chunks = [(i * 128, 128, i) for i in range(21)] + [(2688, 42, 21)]
    ffn_groups = [mh_chunks[0:4], mh_chunks[4:8], mh_chunks[8:12],
                  mh_chunks[12:16], mh_chunks[16:20], mh_chunks[20:22]]

    for mt in range(NMT):
        t0c, t1c = mt * MT, (mt + 1) * MT
        XNB = A[0]
        GACC = A[1]
        W("sync", "s_v")
        W("sync", "s_s")
        W("sync", "s_pe")
        W("sync", "s_st")
        lvxn = ld(XNB, xn_s[:, t0c:t1c].rearrange("(dc p) t -> p dc t", p=128))
        W("vector", "s_st")
        EM("vector", lambda E: E.memset(Af[1], 0.0), inc="s_v")
        W("tensor", "s_ld", lvxn)
        for gi in range(8):
            HT = GBf[0][:, 0:2048]
            HT3 = GB[0]
            for sub in range(4):
                mh = gi * 4 + sub
                wsl = WS[mh % 2]
                W("sync", "s_pe", ws_lastuse[mh % 2])
                lv = ld(wsl, w1T_s[:, mh * 128:(mh + 1) * 128].rearrange(
                    "(dc p) h -> p dc h", p=128))
                W("tensor", "s_ld", lv)
                bi, ps = psum.acquire()
                for dc in range(8):
                    EM("tensor", lambda E, ps=ps, wsl=wsl, dc=dc: E.matmul(
                        ps, wsl[:, dc, :], XNB[:, dc, :],
                        start=(dc == 0), stop=(dc == 7)),
                        inc=("s_pe" if dc == 7 else None))
                ws_lastuse[mh % 2] = prog.cnt["s_pe"]
                W("scalar", "s_pe")
                prog.wait("scalar", "s_v")
                vv = prog.emit("scalar", lambda E, ps=ps, mh=mh, sub=sub: E.activation(
                    GBf[0][:, sub * 512:(sub + 1) * 512], ps, ACT.Relu,
                    bias=p_fb1[:, mh:mh + 1]), inc="s_s")
                psum.release(bi, "s_s", vv)
            W("vector", "s_s")
            emit_quantize(prog, HT.rearrange("p (a b) -> p a b", b=512),
                          [q.rearrange("p (a b) -> p a b", b=512) for q in q2],
                          HT.rearrange("p (a b) -> p a b", b=512),
                          man=3, min_e=-6, max_val=240.0, signed=False,
                          nb_e=4, sat=1000.0)
            wd_stage = KVf[:, 0:4096].rearrange("p (kh d) -> p kh d", d=1024)
            W("sync", "s_v")
            lv2 = ld(wd_stage, w2qT_s[gi * 512:(gi + 1) * 512, :].rearrange(
                "(kh p) d -> p kh d", p=128))
            W("tensor", "s_ld", lv2)
            W("tensor", "s_v")
            for md in range(8):
                bi, ps = psum.acquire()
                for kh in range(4):
                    EM("tensor", lambda E, ps=ps, wd=wd_stage, kh=kh, md=md: E.matmul(
                        ps, wd[:, kh, md * 128:(md + 1) * 128],
                        GBf[0][:, kh * 512:(kh + 1) * 512],
                        start=(kh == 0), stop=(kh == 3)),
                        inc=("s_pe" if kh == 3 else None))
                W("vector", "s_pe")
                vv = ev(prog, lambda E, ps=ps, md=md: E.tensor_tensor(
                    GACC[:, md, :], GACC[:, md, :], ps, ALU.add))
                psum.release(bi, "s_v", vv)
        for dc in range(8):
            ev(prog, lambda E, dc=dc: E.tensor_scalar_add(
                GACC[:, dc, :], GACC[:, dc, :], p_fb2q[:, dc:dc + 1]))
        GACCf = Af[1]
        for h in range(2):
            sl = GACCf[:, h * 2048:(h + 1) * 2048].rearrange("p (a b) -> p a b", b=512)
            emit_quantize(prog, sl, [q.rearrange("p (a b) -> p a b", b=512) for q in q2],
                          sl, man=10, min_e=-14, max_val=65504.0, signed=True,
                          nb_e=5, sat=1.0e5)
        W("scalar", "s_v")
        prog.emit("scalar", lambda E: E.activation(GACCf, GACCf, ACT.Silu), inc="s_s")
        if "g" in tap_out:
            vtp = prog.cnt["s_s"]
            W("sync", "s_s", vtp)
            prog.dma("sync", tap_out["g"][:, t0c:t1c].rearrange("(dc p) t -> p dc t", p=128),
                     A[1], "s_st")
            W("vector", "s_st")
        QP = A[2]
        X1 = A[3]
        lvqp = ld(QP, qp_s[:, t0c:t1c].rearrange("(dc p) t -> p dc t", p=128))
        bden, ps_den = psum.acquire()
        W("tensor", "s_ld", lvqp)
        for dc in range(8):
            EM("tensor", lambda E, ps=ps_den, dc=dc: E.matmul(
                ps[0:1, :], KSUM[:, dc:dc + 1], QP[:, dc, :],
                start=(dc == 0), stop=(dc == 7)),
                inc=("s_pe" if dc == 7 else None))
        W("vector", "s_pe")
        ev(prog, lambda E, ps=ps_den: E.tensor_scalar_add(rstat[0:1, 0:512], ps[0:1, :], 1e-6))
        if "denr" in tap_out:
            W("sync", "s_v")
            prog.dma("sync", tap_out["denr"][0:1, t0c:t1c], rstat[0:1, 0:512], "s_st")
            W("vector", "s_st")
        v1 = ev(prog, lambda E: E.reciprocal(rstat[0:1, 0:512], rstat[0:1, 0:512]))
        psum.release(bden, "s_v", v1)
        W("tensor", "s_v")
        brd, ps_rd = psum.acquire()
        EM("tensor", lambda E, ps=ps_rd: E.matmul(ps, ones1, rstat[0:1, 0:512],
                                                  start=True, stop=True), inc="s_pe")
        W("scalar", "s_pe")
        vrd = prog.emit("scalar", lambda E, ps=ps_rd: E.copy(BCAST[:, 2, :], ps), inc="s_s")
        psum.release(brd, "s_s", vrd)
        W("vector", "s_s", vrd)
        for me in range(8):
            wsl = WS[me % 2]
            W("sync", "s_pe", ws_lastuse[me % 2])
            lv = ld(wsl, kv_s[:, me * 128:(me + 1) * 128].rearrange(
                "(dc p) e -> p dc e", p=128))
            W("tensor", "s_ld", lv)
            bi, ps = psum.acquire()
            for dc in range(8):
                EM("tensor", lambda E, ps=ps, wsl=wsl, dc=dc: E.matmul(
                    ps, wsl[:, dc, :], QP[:, dc, :],
                    start=(dc == 0), stop=(dc == 7)),
                    inc=("s_pe" if dc == 7 else None))
            ws_lastuse[me % 2] = prog.cnt["s_pe"]
            W("vector", "s_pe")
            ev(prog, lambda E, ps=ps, me=me: E.tensor_tensor(
                X1[:, me, :], ps, BCAST[:, 2, :], ALU.mult))
            vv = ev(prog, lambda E, me=me: E.tensor_tensor(
                X1[:, me, :], X1[:, me, :], XNB[:, me, :], ALU.add))
            psum.release(bi, "s_v", vv)
        if "x1r" in tap_out:
            W("sync", "s_v")
            prog.dma("sync", tap_out["x1r"][:, t0c:t1c].rearrange("(dc p) t -> p dc t", p=128),
                     A[3], "s_st")
            W("vector", "s_st")
        if "rden" in tap_out:
            W("sync", "s_v")
            prog.dma("sync", tap_out["rden"][0:1, t0c:t1c], rstat[0:1, 0:512], "s_st")
            W("vector", "s_st")
        W("tensor", "s_v")
        emit_fmln(X1, p_g1, p_b1)
        if "x1" in tap_out:
            W("sync", "s_v")
            prog.dma("sync", tap_out["x1"][:, t0c:t1c].rearrange("(dc p) t -> p dc t", p=128),
                     A[3], "s_st")
        FFN = A[0]
        W("vector", "s_v")
        EM("vector", lambda E: E.memset(Af[0], 0.0), inc="s_v")
        for grp in ffn_groups:
            for si, (h0, w, bc) in enumerate(grp):
                for which, wmat, pb in (("g", Wg, p_bg), ("u", Wu, p_bu)):
                    wsl = WS[0] if which == "g" else WS[1]
                    W("sync", "s_pe", ws_lastuse[0 if which == "g" else 1])
                    lv = ld(wsl[:, :, 0:w], wmat[:, h0:h0 + w].rearrange(
                        "(dc p) h -> p dc h", p=128))
                    W("tensor", "s_ld", lv)
                    bi, ps = psum.acquire()
                    for dc in range(8):
                        EM("tensor", lambda E, ps=ps, wsl=wsl, dc=dc, w=w: E.matmul(
                            ps[0:w, :], wsl[:, dc, 0:w], X1[:, dc, :],
                            start=(dc == 0), stop=(dc == 7)),
                            inc=("s_pe" if dc == 7 else None))
                    ws_lastuse[0 if which == "g" else 1] = prog.cnt["s_pe"]
                    W("scalar", "s_pe")
                    prog.wait("scalar", "s_v")
                    if which == "g":
                        vv = prog.emit("scalar", lambda E, ps=ps, w=w, bc=bc, pb=pb: E.activation(
                            GB[1][0:w, 0, 0:512], ps[0:w, :],
                            ACT.Silu, bias=pb[0:w, bc:bc + 1]), inc="s_s")
                        psum.release(bi, "s_s", vv)
                    else:
                        vv = prog.emit("scalar", lambda E, ps=ps, w=w, bc=bc, pb=pb: E.activation(
                            GB[1][0:w, 1, 0:512], ps[0:w, :],
                            ACT.Identity, bias=pb[0:w, bc:bc + 1]), inc="s_s")
                        psum.release(bi, "s_s", vv)
                W("vector", "s_s")
                W("vector", "s_pe")
                ev(prog, lambda E, si=si, w=w: E.tensor_tensor(
                    GB[0][0:w, si, 0:512], GB[1][0:w, 0, 0:512], GB[1][0:w, 1, 0:512],
                    ALU.mult))
            gh0 = grp[0][0]
            gw = sum(w for (_, w, _) in grp)
            nkh = len(grp)
            wd_stage = KVf[:, 0:4096].rearrange("p (kh d) -> p kh d", d=1024)
            W("sync", "s_v")
            W("sync", "s_pe")
            lvw = None
            for ki, (h0, w, _) in enumerate(grp):
                lvw = ld(wd_stage[0:w, ki, :], Wd[h0:h0 + w, :])
            W("tensor", "s_ld", lvw)
            W("tensor", "s_v")
            for md in range(8):
                bi, ps = psum.acquire()
                for ki, (h0, w, _) in enumerate(grp):
                    is_last = (ki == nkh - 1)
                    EM("tensor", lambda E, ps=ps, wd=wd_stage, ki=ki, md=md, w=w,
                            first=(ki == 0), last=is_last: E.matmul(
                        ps, wd[0:w, ki, md * 128:(md + 1) * 128],
                        GB[0][0:w, ki, 0:512],
                        start=first, stop=last),
                        inc=("s_pe" if is_last else None))
                W("vector", "s_pe")
                vv = ev(prog, lambda E, ps=ps, md=md: E.tensor_tensor(
                    FFN[:, md, :], FFN[:, md, :], ps, ALU.add))
                psum.release(bi, "s_v", vv)
        for dc in range(8):
            ev(prog, lambda E, dc=dc: E.tensor_scalar_add(
                FFN[:, dc, :], FFN[:, dc, :], p_bd[:, dc:dc + 1]))
            ev(prog, lambda E, dc=dc: E.tensor_tensor(
                FFN[:, dc, :], FFN[:, dc, :], X1[:, dc, :], ALU.add))
        W("tensor", "s_v")
        emit_fmln(FFN, p_g2, p_b2)
        if "s0" in tap_out:
            W("sync", "s_v")
            prog.dma("sync", tap_out["s0"][:, t0c:t1c].rearrange("(dc p) t -> p dc t", p=128),
                     A[0], "s_st")
            W("vector", "s_st")
        for dc in range(8):
            ev(prog, lambda E, dc=dc: E.tensor_tensor(
                GACC[:, dc, :], A[0][:, dc, :], GACC[:, dc, :], ALU.mult))
        XT2 = A[2]
        W("sync", "s_v")
        W("sync", "s_pe")
        lvx = ld(Af[2].rearrange("p (s d) -> p s d", d=D),
                 x_in[t0c:t1c, :].rearrange("(s p) d -> p s d", p=128))
        W("tensor", "s_ld", lvx)
        XT2f = Af[2]
        for dc in range(8):
            bi, ps = psum.acquire()
            for s in range(4):
                EM("tensor", lambda E, ps=ps, dc=dc, s=s: E.transpose(
                    ps[:, s * 128:(s + 1) * 128],
                    XT2f[:, s * D + dc * 128: s * D + (dc + 1) * 128], ident),
                    inc=("s_pe" if s == 3 else None))
            W("vector", "s_pe")
            vv = ev(prog, lambda E, ps=ps, dc=dc: E.tensor_tensor(
                GB[2][:, 0, 0:512] if False else GB[2].rearrange("p a b -> p (a b)")[:, 0:512],
                ps, GACC[:, dc, :], ALU.add))
            psum.release(bi, "s_v", vv)
            W("sync", "s_v", vv)
            prog.dma("sync", xnew_s[dc * 128:(dc + 1) * 128, t0c:t1c],
                     GB[2].rearrange("p a b -> p (a b)")[:, 0:512], "s_st")
            W("vector", "s_st")
        XNEW = A[2]
        W("sync", "s_pe")
        W("sync", "s_st")
        lvn = ld(XNEW, xnew_s[:, t0c:t1c].rearrange("(dc p) t -> p dc t", p=128))
        W("vector", "s_ld", lvn)
        W("tensor", "s_ld", lvn)
        W("scalar", "s_ld", lvn)
        emit_fmln(XNEW, None, None)
        W("sync", "s_v")
        st(xn1_s[:, t0c:t1c].rearrange("(dc p) t -> p dc t", p=128), XNEW)

    if stop_after == "phB":
        _finish(prog, nc, psum, out_hbm, tap_out, locals())
        return _close(nc, prog, stack, tap_out)

    EM("vector", lambda E: E.memset(halo.rearrange("p a b -> p (a b)"), 0.0), inc="s_v")
    for mt in range(NMT):
        t0c, t1c = mt * MT, (mt + 1) * MT
        XN1 = A[0]
        W("sync", "s_v")
        W("sync", "s_s")
        W("sync", "s_pe")
        W("sync", "s_st")
        lvx = ld(XN1, xn1_s[:, t0c:t1c].rearrange("(dc p) t -> p dc t", p=128))
        W("tensor", "s_ld", lvx)
        XD = BCAST[0:96, 3, :]
        W("vector", "s_pe")
        W("vector", "s_st")
        EM("vector", lambda E, XD=XD: E.memset(XD, 0.0), inc="s_v")
        for grp in range(4):
            for sub in range(4):
                idx = grp * 4 + sub
                wsl = WS[idx % 2]
                W("sync", "s_pe", ws_lastuse[idx % 2])
                lv = ld(wsl, inpT_s[:, idx * 128:(idx + 1) * 128].rearrange(
                    "(dc p) e -> p dc e", p=128))
                W("tensor", "s_ld", lv)
                bi, ps = psum.acquire()
                for dc in range(8):
                    EM("tensor", lambda E, ps=ps, wsl=wsl, dc=dc, XN1=XN1: E.matmul(
                        ps, wsl[:, dc, :], XN1[:, dc, :],
                        start=(dc == 0), stop=(dc == 7)),
                        inc=("s_pe" if dc == 7 else None))
                ws_lastuse[idx % 2] = prog.cnt["s_pe"]
                W("scalar", "s_pe")
                prog.wait("scalar", "s_v")
                vv = prog.emit("scalar", lambda E, ps=ps, sub=sub: E.copy(
                    GB[0][:, sub, 3:515], ps), inc="s_s")
                psum.release(bi, "s_s", vv)
                prog.emit("scalar", lambda E, sub=sub, idx=idx: E.copy(
                    GB[0][:, sub, 0:3], halo[:, idx, 0:3]), inc="s_s")
                vhl = prog.emit("scalar", lambda E, sub=sub, idx=idx: E.copy(
                    halo[:, idx, 0:3], GB[0][:, sub, 512:515]), inc="s_s")
                W("vector", "s_s", vhl)
                XC = GB[1]
                W("vector", "s_st")
                ev(prog, lambda E, sub=sub, idx=idx: E.tensor_scalar(
                    XC[:, sub, 0:512], GB[0][:, sub, 0:512],
                    p_convw[:, idx, 0:1], None, ALU.mult))
                for j in range(1, 4):
                    ev(prog, lambda E, sub=sub, idx=idx, j=j: E.scalar_tensor_tensor(
                        XC[:, sub, 0:512], GB[0][:, sub, j:j + 512],
                        p_convw[:, idx, j:j + 1], XC[:, sub, 0:512],
                        ALU.mult, ALU.add))
                prog.wait("scalar", "s_v")
                vsl = prog.emit("scalar", lambda E, sub=sub, idx=idx: E.activation(
                    XC[:, sub, 0:512], XC[:, sub, 0:512], ACT.Silu,
                    bias=p_convb[:, idx:idx + 1]), inc="s_s")
            W("sync", "s_s")
            st(xc_s[grp * 512:(grp + 1) * 512, t0c:t1c].rearrange(
                "(s p) t -> p s t", p=128), GB[1][:, :, 0:512])
            bi, ps = psum.acquire()
            W("tensor", "s_s")
            for sub in range(4):
                idx = grp * 4 + sub
                EM("tensor", lambda E, ps=ps, idx=idx, sub=sub: E.matmul(
                    ps[0:96, :], XPW[:, idx, :], GB[1][:, sub, 0:512],
                    start=(sub == 0), stop=(sub == 3)),
                    inc=("s_pe" if sub == 3 else None))
            W("vector", "s_pe")
            vv = ev(prog, lambda E, ps=ps, XD=XD: E.tensor_tensor(
                XD, XD, ps[0:96, :], ALU.add))
            psum.release(bi, "s_v", vv)
            for sub in range(4):
                idx = 16 + grp * 4 + sub
                wsl = WS[idx % 2]
                W("sync", "s_pe", ws_lastuse[idx % 2])
                lv = ld(wsl, inpT_s[:, idx * 128:(idx + 1) * 128].rearrange(
                    "(dc p) e -> p dc e", p=128))
                W("tensor", "s_ld", lv)
                bi, ps = psum.acquire()
                for dc in range(8):
                    EM("tensor", lambda E, ps=ps, wsl=wsl, dc=dc, XN1=XN1: E.matmul(
                        ps, wsl[:, dc, :], XN1[:, dc, :],
                        start=(dc == 0), stop=(dc == 7)),
                        inc=("s_pe" if dc == 7 else None))
                ws_lastuse[idx % 2] = prog.cnt["s_pe"]
                W("scalar", "s_pe")
                prog.wait("scalar", "s_st")
                vv = prog.emit("scalar", lambda E, ps=ps, sub=sub: E.activation(
                    GB[2][:, sub, 0:512], ps, ACT.Silu), inc="s_s")
                psum.release(bi, "s_s", vv)
            W("sync", "s_s")
            st(zs_s[grp * 512:(grp + 1) * 512, t0c:t1c].rearrange(
                "(s p) t -> p s t", p=128), GB[2][:, :, 0:512])
        W("sync", "s_v")
        st(bc_s[:, t0c:t1c], BCAST[64:96, 3, :])
        for mdi in range(16):
            bi, ps = psum.acquire()
            EM("tensor", lambda E, ps=ps, mdi=mdi: E.matmul(
                ps, DTW[:, mdi * 128:(mdi + 1) * 128], XD[0:64, :],
                start=True, stop=True), inc="s_pe")
            W("scalar", "s_pe")
            prog.wait("scalar", "s_st")
            vv = prog.emit("scalar", lambda E, ps=ps, mdi=mdi: E.activation(
                GB[3][:, 0, 0:512], ps, ACT.Identity,
                bias=p_dtpb[:, mdi:mdi + 1]), inc="s_s")
            psum.release(bi, "s_s", vv)
            W("vector", "s_s", vv)
            emit_softplus(prog, GB[3][:, 0, 0:512], GB[3][:, 1, 0:512], GB[3][:, 0, 0:512])
            ev(prog, lambda E, mdi=mdi: E.tensor_scalar_add(
                GB[3][:, 0, 0:512], GB[3][:, 0, 0:512], p_dtpb[:, mdi:mdi + 1]))
            emit_softplus(prog, GB[3][:, 0, 0:512], GB[3][:, 1, 0:512], GB[3][:, 0, 0:512])
            W("sync", "s_v")
            st(delta_s[mdi * 128:(mdi + 1) * 128, t0c:t1c], GB[3][:, 0, 0:512])

    if stop_after == "phC1":
        _finish(prog, nc, psum, out_hbm, tap_out, locals())
        return _close(nc, prog, stack, tap_out)

    for mt in range(NMT):
        t0c, t1c = mt * MT, (mt + 1) * MT
        XN1 = A[0]
        G2 = A[2]
        HE = A[1]
        W("sync", "s_v")
        W("sync", "s_s")
        W("sync", "s_pe")
        W("sync", "s_st")
        lvx = ld(XN1, xn1_s[:, t0c:t1c].rearrange("(dc p) t -> p dc t", p=128))
        W("vector", "s_pe")
        W("vector", "s_s")
        W("vector", "s_st")
        EM("vector", lambda E: E.memset(Af[2], 0.0), inc="s_v")
        bi, ps = psum.acquire()
        W("tensor", "s_ld", lvx)
        for dc in range(8):
            EM("tensor", lambda E, ps=ps, dc=dc, XN1=XN1: E.matmul(
                ps[0:8, :], p_mwg[:, dc, :], XN1[:, dc, :],
                start=(dc == 0), stop=(dc == 7)),
                inc=("s_pe" if dc == 7 else None))
        W("scalar", "s_pe")
        vlg = prog.emit("scalar", lambda E, ps=ps: E.activation(
            LOG, ps[0:8, :], ACT.Identity, bias=p_mbg), inc="s_s")
        psum.release(bi, "s_s", vlg)
        W("vector", "s_s", vlg)
        EM("vector", lambda E: E.memset(ONES8.rearrange("p a b -> p (a b)"), 1.0), inc="s_v")
        W("tensor", "s_s", vlg)
        bi2, ps2 = psum.acquire()
        for s in range(4):
            EM("tensor", lambda E, ps=ps2, s=s: E.transpose(
                ps[0:128, s * 8:(s + 1) * 8], LOG[0:8, s * 128:(s + 1) * 128],
                ident[0:8, 0:8]), inc=("s_pe" if s == 3 else None))
        W("vector", "s_pe")
        vtr = ev(prog, lambda E, ps=ps2: E.tensor_copy(
            LGT.rearrange("p a b -> p (a b)"), ps[:, 0:32]))
        psum.release(bi2, "s_pe" if False else "s_v", vtr)
        for s in range(4):
            ev(prog, lambda E, s=s: E.reduce_max(MXC[:, s:s + 1], LGT[:, s, :], axis=AX.X))
        for s in range(4):
            ev(prog, lambda E, s=s: E.tensor_scalar(
                LGT[:, s, :], LGT[:, s, :], MXC[:, s:s + 1], None, ALU.is_ge))
        for s in range(4):
            ev(prog, lambda E, s=s: E.tensor_tensor_scan(
                MTOK[:, s, :], ONES8[:, s, :], LGT[:, s, :], 0.0, ALU.mult, ALU.add))
        for s in range(4):
            ev(prog, lambda E, s=s: E.tensor_tensor(
                MTOK[:, s, :], MTOK[:, s, :], LGT[:, s, :], ALU.subtract))
        for s in range(4):
            ev(prog, lambda E, s=s: E.tensor_single_scalar(
                MTOK[:, s, :], MTOK[:, s, :], 0.5, ALU.is_le))
        for s in range(4):
            ev(prog, lambda E, s=s: E.tensor_tensor(
                MTOK[:, s, :], MTOK[:, s, :], LGT[:, s, :], ALU.mult))
        for e in range(NE):
            W("tensor", "s_v")
            bb0, psb0 = psum.acquire()
            for s in range(4):
                EM("tensor", lambda E, ps=psb0, s=s, e=e: E.transpose(
                    ps[0:1, s * 128:(s + 1) * 128], MTOK[:, s, e:e + 1],
                    ident), inc=("s_pe" if s == 3 else None))
            W("scalar", "s_pe")
            vmr = prog.emit("scalar", lambda E, ps=psb0: E.copy(
                rstat[0:1, 1536:2048], ps[0:1, 0:512]), inc="s_s")
            psum.release(bb0, "s_s", vmr)
            if "mrow" in tap_out and mt == NMT - 1:
                W("sync", "s_s", vmr)
                prog.dma("sync", tap_out["mrow"][e:e + 1, :], rstat[0:1, 1536:2048], "s_st")
                W("scalar", "s_st")
            W("tensor", "s_s", vmr)
            bb, psb = psum.acquire()
            EM("tensor", lambda E, ps=psb: E.matmul(
                ps, ones1, rstat[0:1, 1536:2048], start=True, stop=True), inc="s_pe")
            W("scalar", "s_pe")
            vmb = prog.emit("scalar", lambda E, ps=psb: E.copy(BCAST[:, 2, :], ps), inc="s_s")
            psum.release(bb, "s_s", vmb)
            for mi in range(8):
                wsl = WS[mi % 2]
                W("sync", "s_pe", ws_lastuse[mi % 2])
                lv = ld(wsl, mW1[e, :, mi * 128:(mi + 1) * 128].rearrange(
                    "(dc p) i -> p dc i", p=128))
                W("tensor", "s_ld", lv)
                bi, ps = psum.acquire()
                for dc in range(8):
                    EM("tensor", lambda E, ps=ps, wsl=wsl, dc=dc, XN1=XN1: E.matmul(
                        ps, wsl[:, dc, :], XN1[:, dc, :],
                        start=(dc == 0), stop=(dc == 7)),
                        inc=("s_pe" if dc == 7 else None))
                ws_lastuse[mi % 2] = prog.cnt["s_pe"]
                W("scalar", "s_pe")
                prog.wait("scalar", "s_pe")
                vv = prog.emit("scalar", lambda E, ps=ps, e=e, mi=mi: E.activation(
                    HE[:, mi, :], ps, ACT.Gelu,
                    bias=p_mb1[:, e, mi:mi + 1]), inc="s_s")
                psum.release(bi, "s_s", vv)
            W("tensor", "s_s")
            for md in range(8):
                wsl = WS[md % 2]
                W("sync", "s_pe", ws_lastuse[md % 2])
                lv = ld(wsl, mW2[e, :, md * 128:(md + 1) * 128].rearrange(
                    "(ic p) h -> p ic h", p=128))
                W("tensor", "s_ld", lv)
                bi, ps = psum.acquire()
                for ic in range(8):
                    EM("tensor", lambda E, ps=ps, wsl=wsl, ic=ic: E.matmul(
                        ps, wsl[:, ic, :], HE[:, ic, :],
                        start=(ic == 0), stop=(ic == 7)),
                        inc=("s_pe" if ic == 7 else None))
                ws_lastuse[md % 2] = prog.cnt["s_pe"]
                W("scalar", "s_pe")
                prog.wait("scalar", "s_v")
                vv = prog.emit("scalar", lambda E, ps=ps, e=e, md=md: E.activation(
                    GB[0][:, 0, 0:512], ps, ACT.Identity,
                    bias=p_mb2[:, e, md:md + 1]), inc="s_s")
                psum.release(bi, "s_s", vv)
                W("vector", "s_s", vv)
                ev(prog, lambda E, md=md: E.tensor_tensor(
                    GB[0][:, 0, 0:512], GB[0][:, 0, 0:512], BCAST[:, 2, :], ALU.mult))
                ev(prog, lambda E, md=md, G2=G2: E.tensor_tensor(
                    G2[:, md, :], G2[:, md, :], GB[0][:, 0, 0:512], ALU.add))
        if "log8" in tap_out and mt == NMT - 1:
            W("sync", "s_s")
            prog.dma("sync", tap_out["log8"], LOG, "s_st")
        if "mtok" in tap_out and mt == NMT - 1:
            W("sync", "s_v")
            prog.dma("sync", tap_out["mtok"], MTOK.rearrange("p a b -> p (a b)"), "s_st")
        W("scalar", "s_v")
        vs2 = prog.emit("scalar", lambda E: E.activation(Af[2], Af[2], ACT.Silu), inc="s_s")
        W("sync", "s_s", vs2)
        st(g2_s[:, t0c:t1c].rearrange("(dc p) t -> p dc t", p=128), G2)

    if stop_after == "phD":
        _finish(prog, nc, psum, out_hbm, tap_out, locals())
        return _close(nc, prog, stack, tap_out)

    REPS = [KVf[:, 0:2048], KVf[:, 2048:4096]]
    rep_store = [0, 0]
    W("sync", "s_st")
    for nrow in range(32):
        for hc in range(2):
            W("sync", "s_pe")
            lvr = ld(rowbuf[0:1, 0:2048], bc_s[nrow:nrow + 1, hc * 2048:(hc + 1) * 2048])
            W("tensor", "s_ld", lvr)
            W("tensor", "s_v")
            for q in range(4):
                EM("tensor", lambda E, q=q: E.matmul(
                    psBC[:, q * 512:(q + 1) * 512], ones1,
                    rowbuf[0:1, q * 512:(q + 1) * 512],
                    start=True, stop=True), inc=("s_pe" if q == 3 else None))
            sl = (nrow * 2 + hc) % 2
            W("vector", "s_pe")
            W("vector", "s_st", rep_store[sl])
            vv = ev(prog, lambda E, sl=sl: E.tensor_copy(REPS[sl], psBC))
            W("sync", "s_v", vv)
            rep_store[sl] = st(bc_rep[nrow, :, hc * 2048:(hc + 1) * 2048], REPS[sl])

    TH = NT // 2
    bc_use = [0, 0]
    bc_lv = [(0, 0), (0, 0)]
    prog.wait("gpsimd", "s_st")
    for dc in range(16):
        DLT = Af[0]
        XCC = Af[1]
        DU = Af[2]
        YA = Af[3]
        W("sync", "s_v")
        W("sync", "s_s")
        W("sync", "s_pe")
        W("sync", "s_st")
        lvd = ld(DLT.rearrange("p (h x) -> p h x", h=2), delta_s[dc * 128:(dc + 1) * 128, :].rearrange("p (h x) -> p h x", h=2))
        lvc = ld(XCC.rearrange("p (h x) -> p h x", h=2), xc_s[dc * 128:(dc + 1) * 128, :].rearrange("p (h x) -> p h x", h=2))
        W("vector", "s_ld", lvc)
        W("scalar", "s_ld", lvc)
        ev(prog, lambda E: E.tensor_tensor(DU, DLT, XCC, ALU.mult))
        W("vector", "s_st")
        EM("vector", lambda E: E.memset(YA, 0.0), inc="s_v")
        EB = GBf[0][:, 0:2048]
        FB = GBf[1][:, 0:2048]
        SCO = GBf[2][:, 0:2048]
        for half in range(2):
            hsl = slice(half * TH, (half + 1) * TH)
            for n in range(DS):
                sl = n % 2
                BBUF = KVf[:, sl * 4096: sl * 4096 + 2048]
                CBUF = KVf[:, sl * 4096 + 2048: (sl + 1) * 4096]
                W("sync", "s_v", bc_use[sl])
                lvb = prog.dma("sync", BBUF,
                               bc_rep[n, :, half * TH:(half + 1) * TH], "s_ld")
                prog.wait("gpsimd", "s_v", bc_use[sl])
                lvc2 = prog.dma("gpsimd", CBUF,
                                bc_rep[DS + n, :, half * TH:(half + 1) * TH], "s_g")
                bc_lv[sl] = (lvb, lvc2)
                prog.wait("scalar", "s_v")
                vse = prog.emit("scalar", lambda E, dc=dc, n=n, hsl=hsl: E.activation(
                    EB, DLT[:, hsl], ACT.Exp, scale=p_aneg[:, dc, n:n + 1]), inc="s_s")
                W("vector", "s_ld", bc_lv[sl][0])
                ev(prog, lambda E, hsl=hsl, BBUF=BBUF: E.tensor_tensor(
                    FB, DU[:, hsl], BBUF, ALU.mult))
                W("vector", "s_s", vse)
                if half == 0:
                    ev(prog, lambda E: E.tensor_tensor_scan(SCO, EB, FB, 0.0, ALU.mult, ALU.add))
                    ev(prog, lambda E, n=n: E.tensor_copy(carry[:, n:n + 1], SCO[:, TH - 1:TH]))
                else:
                    ev(prog, lambda E, n=n: E.tensor_tensor_scan(
                        SCO, EB, FB, carry[:, n:n + 1], ALU.mult, ALU.add))
                W("vector", "s_g", bc_lv[sl][1])
                ev(prog, lambda E, CBUF=CBUF: E.tensor_tensor(FB, SCO, CBUF, ALU.mult))
                bc_use[sl] = ev(prog, lambda E, hsl=hsl: E.tensor_tensor(
                    YA[:, hsl], YA[:, hsl], FB, ALU.add))
        ev(prog, lambda E, dc=dc: E.scalar_tensor_tensor(
            YA, XCC, p_dpar[:, dc:dc + 1], YA, ALU.mult, ALU.add))
        ZSC = Af[1]
        W("sync", "s_v")
        lvz = ld(ZSC.rearrange("p (h x) -> p h x", h=2), zs_s[dc * 128:(dc + 1) * 128, :].rearrange("p (h x) -> p h x", h=2))
        W("vector", "s_ld", lvz)
        vy = ev(prog, lambda E: E.tensor_tensor(YA, YA, ZSC, ALU.mult))
        W("sync", "s_v", vy)
        st(y_s[dc * 128:(dc + 1) * 128, :].rearrange("p (h x) -> p h x", h=2),
           YA.rearrange("p (h x) -> p h x", h=2))

    if stop_after == "phC2":
        _finish(prog, nc, psum, out_hbm, tap_out, locals())
        return _close(nc, prog, stack, tap_out)

    for mt in range(NMT):
        t0c, t1c = mt * MT, (mt + 1) * MT
        W("sync", "s_v")
        W("sync", "s_s")
        W("sync", "s_pe")
        W("sync", "s_st")
        Y0, Y1 = A[0], A[1]
        XNEW = A[2]
        G2 = A[3]
        lv0 = ld(Y0, y_s[0:1024, t0c:t1c].rearrange("(dc p) t -> p dc t", p=128))
        lv1 = ld(Y1, y_s[1024:2048, t0c:t1c].rearrange("(dc p) t -> p dc t", p=128))
        lv2 = ld(XNEW, xnew_s[:, t0c:t1c].rearrange("(dc p) t -> p dc t", p=128))
        lv3 = ld(G2, g2_s[:, t0c:t1c].rearrange("(dc p) t -> p dc t", p=128))
        import os as _os
        if _os.environ.get("PHC3_LOADS_ONLY"):
            continue
        for me in range(8):
            wsl = WB[me % 2]
            W("sync", "s_pe", ws_lastuse[me % 2])
            lvw = prog.dma("sync", wsl.rearrange("p (k m) -> p k m", m=128),
                           outpT_s[:, me * 128:(me + 1) * 128].rearrange(
                               "(k p) m -> p k m", p=128), "s_ld")
            W("tensor", "s_ld", lvw)
            bi, ps = psum.acquire()
            for k in range(16):
                EM("tensor", lambda E, ps=ps, wsl=wsl, k=k: E.matmul(
                    ps, wsl[:, k * 128:(k + 1) * 128],
                    (Y0 if k < 8 else Y1)[:, k % 8, :],
                    start=(k == 0), stop=(k == 15)),
                    inc=("s_pe" if k == 15 else None))
            ws_lastuse[me % 2] = prog.cnt["s_pe"]
            W("vector", "s_pe")
            FIN = GB[0].rearrange("p a b -> p (a b)")[:, 0:512]
            W("vector", "s_st")
            ev(prog, lambda E, ps=ps, me=me, FIN=FIN, G2=G2: E.tensor_tensor(
                FIN, ps, G2[:, me, :], ALU.mult))
            vv = ev(prog, lambda E, me=me, FIN=FIN, XNEW=XNEW: E.tensor_tensor(
                FIN, FIN, XNEW[:, me, :], ALU.add))
            psum.release(bi, "s_v", vv)
            W("tensor", "s_v", vv)
            bi2, ps2 = psum.acquire()
            for s in range(4):
                EM("tensor", lambda E, ps=ps2, FIN=FIN, s=s: E.transpose(
                    ps[:, s * 128:(s + 1) * 128], FIN[:, s * 128:(s + 1) * 128],
                    ident), inc=("s_pe" if s == 3 else None))
            W("scalar", "s_pe")
            prog.wait("scalar", "s_st")
            OUTT = GB[1].rearrange("p a b -> p (a b)")[:, 0:512]
            vv2 = prog.emit("scalar", lambda E, ps=ps2, OUTT=OUTT: E.copy(OUTT, ps), inc="s_s")
            psum.release(bi2, "s_s", vv2)
            W("sync", "s_s", vv2)
            for s in range(4):
                prog.dma("sync",
                         out_hbm[t0c + s * 128:t0c + (s + 1) * 128,
                                 me * 128:(me + 1) * 128],
                         OUTT[0:128, s * 128:(s + 1) * 128], "s_st")
            W("scalar", "s_st")

    out_written = True
    _finish(prog, nc, psum, out_hbm, tap_out, locals())
    return _close(nc, prog, stack, tap_out)


def _finish(prog, nc, psum, out_hbm, tap_out, L):
    W = prog.wait
    prog.steps["sync"]
    bounce = L["Af"][0]
    srcmap = {"xn": L["xn_s"], "qp": L["qp_s"], "kv": L["kv_s"], "ksum": L["ksum_s"],
              "w1T": L["w1T_s"], "w2qT": L["w2qT_s"],
              "xnew": L["xnew_s"], "xn1": L["xn1_s"], "xc": L["xc_s"],
              "zs": L["zs_s"], "delta": L["delta_s"], "bc": L["bc_s"],
              "yts": L["y_s"], "g2": L["g2_s"]}
    for s in ("s_v", "s_s", "s_pe", "s_st", "s_ld"):
        W("sync", s)
    sbmap = {"sb_log": L.get("LOG"), "sb_bcast2": None, "sb_ones1": L.get("ones1")}
    if "MTOK" in L:
        sbmap["sb_mtok"] = L["MTOK"].rearrange("p a b -> p (a b)")
        sbmap["sb_lgt"] = L["LGT"].rearrange("p a b -> p (a b)")
        sbmap["sb_mrow"] = L["rstat"][0:1, 1536:2048]
        sbmap["sb_bcast2"] = L["BCAST"][:, 2, :]
    for name, dst in tap_out.items():
        src = sbmap.get(name)
        if src is not None:
            prog.dma("sync", dst, src, "s_st")
            W("sync", "s_st")
    for name, dst in tap_out.items():
        src = srcmap.get(name)
        if src is None:
            continue
        R, C = src.shape
        for r0 in range(0, R, 128):
            rr = min(128, R - r0)
            lv = prog.dma("sync", bounce[0:rr, 0:C], src[r0:r0 + rr, :], "s_ld")
            W("sync", "s_ld", lv)
            prog.dma("sync", dst[r0:r0 + rr, :], bounce[0:rr, 0:C], "s_st")
            W("sync", "s_st")
    if not L.get("out_written"):
        zb = L["Af"][1]
        prog.emit("vector", lambda E, zb=zb: E.memset(zb, 0.0), inc="s_v")
        W("sync", "s_v")
        for mt in range(NMT):
            prog.dma("sync",
                     out_hbm[mt * MT:(mt + 1) * MT, :].rearrange("(s p) d -> p s d", p=128),
                     zb.rearrange("p (s d) -> p s d", d=D)[:, 0:4, :], "s_st")
    W("sync", "s_st")


def _close(nc, prog, stack, tap_out):
    with nc.Block() as block:
        prog.finalize(block)
    return nc, list(tap_out)



def kernel(**inputs):
    xfull = np.ascontiguousarray(np.asarray(inputs["L0_attn_x"], dtype=np.float32))
    B = xfull.shape[0]
    nc, taps = build(stop_after=STOP_AFTER, taps=TAPS)
    ident = np.eye(128, dtype=np.float32)
    base = {k: np.ascontiguousarray(np.asarray(v, dtype=np.float32))
            for k, v in inputs.items() if k != "L0_attn_x"}
    in_maps = []
    for c in range(B):
        m = dict(base)
        m["x"] = xfull[c]
        m["ident"] = ident
        in_maps.append(m)
    res = run_bass_kernel_spmd(nc, in_maps, core_ids=list(range(B)))
    out = np.stack([r["out"] for r in res.results], axis=0)
    kernel.last_results = res.results
    return out


# version 08 (final)
# speedup vs baseline: 1.2051x; 1.0578x over previous
import math
import contextlib
import numpy as np

import concourse.bass as bass
import concourse.mybir as mybir
from concourse.bass_utils import run_bass_kernel_spmd

F32 = mybir.dt.float32
BF16 = mybir.dt.bfloat16
ALU = mybir.AluOpType
ACT = mybir.ActivationFunctionType
AX = mybir.AxisListType

NT = 4096
D = 1024
MT = 512
NMT = NT // MT
E3 = 3072
FH = 2730
L0H = 4096
DI = 2048
DS = 16
DCONV = 4
DTR = 64
NE = 8
LN2HI = float(np.float32(0.6931471805599453))
LN2LO = float(np.float32(0.6931471805599453 - np.float64(np.float32(0.6931471805599453))))
LOG2E = float(np.float32(1.4426950408889634))
FLT_MIN = 1.17549435e-38

ENGS = ("sync", "vector", "scalar", "tensor", "gpsimd")

STOP_AFTER = None
TAPS = ()
DBG_OUT_SCRATCH = set()


class Prog:

    DMA_ENG = {"s_ld": "sync", "s_st": "sync", "s_g": "gpsimd"}

    def __init__(self, nc):
        self.nc = nc
        self.steps = {e: [] for e in ENGS}
        self.sem = {}
        self.cnt = {}
        self.waited = {e: {} for e in ENGS}
        self.pending_gate = {}

    def new_sem(self, name):
        self.sem[name] = self.nc.alloc_semaphore(name)
        self.cnt[name] = 0
        self.pending_gate[name] = 0

    def wait(self, eng, name, val=None):
        v = self.cnt[name] if val is None else val
        if v <= 0 or self.waited[eng].get(name, 0) >= v:
            return
        if name in self.DMA_ENG:
            self.pending_gate[name] = max(self.pending_gate[name], v)
        self.waited[eng][name] = v
        h = self.sem[name]
        self.steps[eng].append(lambda E, h=h, v=v: E.wait_ge(h, v))

    def emit(self, eng, fn, inc=None, inc_n=1):
        if inc is None:
            self.steps[eng].append(fn)
            return None
        self.cnt[inc] += inc_n
        h = self.sem[inc]

        def wrapped(E, fn=fn, h=h, n=inc_n):
            ins = fn(E)
            ins.then_inc(h, n)
        self.steps[eng].append(wrapped)
        return self.cnt[inc]

    def dma(self, eng, out, in_, inc, slow=False):
        g = self.pending_gate.get(inc, 0)
        if g > self.waited[eng].get(inc, 0):
            self.waited[eng][inc] = g
            h0 = self.sem[inc]
            self.steps[eng].append(lambda E, h=h0, v=g: E.wait_ge(h, v))
        if slow:
            return self.emit(eng, lambda E, o=out, i=in_: E.dma_start(
                out=o, in_=i, allow_slow_non_contiguous=True), inc=inc, inc_n=16)
        return self.emit(eng, lambda E, o=out, i=in_: E.dma_start(out=o, in_=i),
                         inc=inc, inc_n=16)

    def finalize(self, block):
        for e in ENGS:
            items = self.steps[e]
            if not items:
                continue
            def run(E, items=items):
                for it in items:
                    it(E)
            getattr(block, e)(run)


class PsumPool:

    def __init__(self, prog, nbanks=4):
        self.prog = prog
        nc = prog.nc
        self.tiles = [nc.alloc_psum_tensor(f"pb{i}", [128, 512], F32).ap()
                      for i in range(nbanks)]
        self.free_at = [None] * nbanks
        self.rr = 0
        self.n = nbanks

    def acquire(self):
        i = self.rr % self.n
        self.rr += 1
        if self.free_at[i] is not None:
            sem, val = self.free_at[i]
            self.prog.wait("tensor", sem, val)
        return i, self.tiles[i]

    def release(self, i, sem, val):
        self.free_at[i] = (sem, val)

    def manual(self, i):
        return self.tiles[i]


def ceil_div(a, b):
    return (a + b - 1) // b



def ev(prog, fn, inc="s_v", eng="vector"):
    return prog.emit(eng, fn, inc=inc)


def emit_elup1(prog, z_ap, tmp_ap, out_ap, ksum_ap=None):
    ev(prog, lambda E: E.tensor_scalar_min(tmp_ap, z_ap, 0.0))
    prog.wait("scalar", "s_v")
    prog.emit("scalar", lambda E: E.activation(tmp_ap, tmp_ap, ACT.Exp), inc="s_s")
    prog.wait("vector", "s_s")
    if ksum_ap is None:
        ev(prog, lambda E: E.scalar_tensor_tensor(out_ap, z_ap, 0.0, tmp_ap, ALU.max, ALU.add))
    else:
        ev(prog, lambda E: E.scalar_tensor_tensor(out_ap, z_ap, 0.0, tmp_ap, ALU.max, ALU.add,
                                                  accum_out=ksum_ap))


def emit_softplus(prog, x_ap, tmp_ap, out_ap):
    prog.wait("scalar", "s_v")
    prog.emit("scalar", lambda E: E.activation(tmp_ap, x_ap, ACT.Abs), inc="s_s")
    prog.emit("scalar", lambda E: E.activation(tmp_ap, tmp_ap, ACT.Exp, scale=-1.0), inc="s_s")
    prog.emit("scalar", lambda E: E.activation(tmp_ap, tmp_ap, ACT.Ln, bias=1.0), inc="s_s")
    prog.wait("vector", "s_s")
    ev(prog, lambda E: E.scalar_tensor_tensor(out_ap, x_ap, 0.0, tmp_ap, ALU.max, ALU.add))


def emit_quantize(prog, x_ap, t, out_ap, man, min_e, max_val, signed=True,
                  nb_e=4, sat=1000.0):
    T0, T1, T2, T3, T4, T5 = t[:6]
    nb_r = man + 2
    prog.wait("scalar", "s_v")
    if signed:
        prog.emit("scalar", lambda E: E.activation(T0, x_ap, ACT.Sign), inc="s_s")
    prog.emit("scalar", lambda E: E.activation(T1, x_ap, ACT.Abs), inc="s_s")
    prog.wait("vector", "s_s")
    ev(prog, lambda E: E.tensor_scalar(T1, T1, FLT_MIN, float(sat), ALU.max, ALU.min))
    prog.wait("scalar", "s_v")
    prog.emit("scalar", lambda E: E.activation(T1, T1, ACT.Ln), inc="s_s")
    prog.wait("vector", "s_s")
    ev(prog, lambda E: E.tensor_scalar(T1, T1, LOG2E, float(-min_e), ALU.mult, ALU.add))
    ev(prog, lambda E: E.tensor_scalar(T3, T1, 0.0, float(2 ** nb_e) - 1e-4,
                                       ALU.max, ALU.min))
    ev(prog, lambda E: E.tensor_copy(T1, T3))
    for k in range(nb_e - 1, -1, -1):
        b = float(1 << k)
        ev(prog, lambda E, b=b: E.tensor_scalar(T4, T1, b, b, ALU.is_ge, ALU.mult))
        ev(prog, lambda E: E.tensor_tensor(T1, T1, T4, ALU.subtract))
    ev(prog, lambda E: E.tensor_tensor(T2, T3, T1, ALU.subtract))
    C = float(man - min_e)
    ev(prog, lambda E: E.tensor_scalar(T1, T2, -1.0, C, ALU.mult, ALU.add))
    ev(prog, lambda E: E.tensor_scalar(T3, T1, LN2HI, None, ALU.mult))
    ev(prog, lambda E: E.scalar_tensor_tensor(T3, T1, LN2LO, T3, ALU.mult, ALU.add))
    prog.wait("scalar", "s_v")
    prog.emit("scalar", lambda E: E.activation(T1, T3, ACT.Exp), inc="s_s")
    prog.emit("scalar", lambda E: E.activation(T3, T3, ACT.Exp, scale=-1.0), inc="s_s")
    prog.wait("vector", "s_s")
    ev(prog, lambda E: E.tensor_tensor(T1, x_ap, T1, ALU.mult))
    if signed:
        prog.wait("scalar", "s_v")
        prog.emit("scalar", lambda E: E.activation(T1, T1, ACT.Abs), inc="s_s")
        prog.wait("vector", "s_s")
        ev(prog, lambda E: E.tensor_scalar(T1, T1, 0.5 + 1e-9, None, ALU.add))
    else:
        ev(prog, lambda E: E.tensor_scalar(T1, T1, 0.5 + 1e-9, None, ALU.add))
    ev(prog, lambda E: E.tensor_copy(T2, T1))
    for k in range(nb_r - 1, -1, -1):
        b = float(1 << k)
        ev(prog, lambda E, b=b: E.tensor_scalar(T4, T1, b, b, ALU.is_ge, ALU.mult))
        ev(prog, lambda E: E.tensor_tensor(T1, T1, T4, ALU.subtract))
    ev(prog, lambda E: E.tensor_single_scalar(T5, T1, 2e-9, ALU.is_le))
    ev(prog, lambda E: E.tensor_tensor(T5, T5, T4, ALU.mult))
    ev(prog, lambda E: E.tensor_tensor(T1, T1, T5, ALU.add))
    ev(prog, lambda E: E.tensor_tensor(T1, T2, T1, ALU.subtract))
    ev(prog, lambda E: E.tensor_tensor(T1, T1, T3, ALU.mult))
    ovf_thr = max_val * (1.0 + 2.0 ** -(man + 2))
    ev(prog, lambda E: E.tensor_single_scalar(T2, T1, ovf_thr, ALU.is_gt))
    if signed:
        ev(prog, lambda E: E.tensor_tensor(T1, T1, T0, ALU.mult))
    prog.wait("scalar", "s_v")
    prog.emit("scalar", lambda E: E.activation(T5, T2, ACT.Exp, scale=710.0), inc="s_s")
    prog.wait("vector", "s_s")
    ev(prog, lambda E: E.tensor_tensor(out_ap, T1, T5, ALU.mult))



def build(stop_after=None, taps=()):
    nc = bass.Bass("TRN2", target_bir_lowering=False,
                   detect_race_conditions=False)
    stack = contextlib.ExitStack()

    def din(name, shape):
        return nc.dram_tensor(name, list(shape), F32, kind="ExternalInput").ap()

    x_in = din("x", (NT, D))
    Wqkv = din("L0_attn_Wqkv", (D, E3))
    bqkv = din("L0_attn_bqkv", (E3,))
    Wg = din("L0_attn_Wg", (D, FH))
    bg = din("L0_attn_bg", (FH,))
    Wu = din("L0_attn_Wu", (D, FH))
    bu = din("L0_attn_bu", (FH,))
    Wd = din("L0_attn_Wd", (FH, D))
    bd = din("L0_attn_bd", (D,))
    g1 = din("L0_attn_g1", (D,))
    b1 = din("L0_attn_b1", (D,))
    g2 = din("L0_attn_g2", (D,))
    b2 = din("L0_attn_b2", (D,))
    fw1 = din("L0_ffn_w1", (L0H, D))
    fb1 = din("L0_ffn_b1", (L0H,))
    fw2 = din("L0_ffn_w2", (D, L0H))
    fb2 = din("L0_ffn_b2", (D,))
    inpw = din("L1_attn_in_proj_w", (2 * DI, D))
    convw = din("L1_attn_conv_w", (DI, DCONV))
    convb = din("L1_attn_conv_b", (DI,))
    xpw = din("L1_attn_x_proj_w", (DTR + 2 * DS, DI))
    dtpw = din("L1_attn_dt_proj_w", (DI, DTR))
    dtpb = din("L1_attn_dt_proj_b", (DI,))
    alog = din("L1_attn_A_log", (DI, DS))
    dpar = din("L1_attn_D", (DI,))
    outpw = din("L1_attn_out_proj_w", (D, DI))
    mWg = din("L1_ffn_Wg", (D, NE))
    mbg = din("L1_ffn_bg", (NE,))
    mW1 = din("L1_ffn_W1", (NE, D, D))
    mb1 = din("L1_ffn_b1", (NE, D))
    mW2 = din("L1_ffn_W2", (NE, D, D))
    mb2 = din("L1_ffn_b2", (NE, D))
    ident_in = din("ident", (128, 128))

    out_hbm = nc.dram_tensor("out", [NT, D], F32, kind="ExternalOutput").ap()

    def scr(name, shape, dt=F32):
        kind = "ExternalOutput" if name in DBG_OUT_SCRATCH else "Internal"
        return nc.dram_tensor(name, list(shape), dt, kind=kind).ap()

    xn_s = scr("xn_s", (D, NT))
    qp_s = scr("qp_s", (D, NT))
    kv_s = scr("kv_s", (D, D))
    ksum_s = scr("ksum_s", (1, D))
    xnew_s = scr("xnew_s", (D, NT))
    xn1_s = scr("xn1_s", (D, NT))
    xc_s = scr("xc_s", (DI, NT))
    zs_s = scr("zs_s", (DI, NT))
    delta_s = scr("delta_s", (DI, NT))
    bc_s = scr("bc_s", (2 * DS, NT))
    y_s = scr("y_s", (DI, NT))
    g2_s = scr("g2_s", (D, NT))
    w1T_s = scr("w1T_s", (D, L0H))
    w2qT_s = scr("w2qT_s", (L0H, D))
    inpT_s = scr("inpT_s", (D, 2 * DI))
    outpT_s = scr("outpT_s", (DI, D))
    xpT_s = scr("xpT_s", (DI, DTR + 2 * DS))
    dtpT_s = scr("dtpT_s", (DTR, DI))
    bc_rep = scr("bc_rep", (32, 128, NT))

    tap_out = {}
    for tname in taps:
        shp = {"xn": (D, NT), "qp": (D, NT), "kv": (D, D), "ksum": (1, D),
               "w1T": (D, L0H), "w2qT": (L0H, D), "g": (D, NT), "x1": (D, NT), "x1r": (D, NT), "rden": (1, NT), "denr": (1, NT),
               "s0": (D, NT), "xnew": (D, NT), "xn1": (D, NT),
               "xc": (DI, NT), "zs": (DI, NT), "delta": (DI, NT),
               "bc": (2 * DS, NT), "yts": (DI, NT), "g2": (D, NT),
               "kpt": (128, 4096), "vtm": (128, 4096), "kpf": (128, 4096), "log8": (8, MT), "mtok": (128, 32), "mrow": (NE, MT),
               "sb_log": (8, MT), "sb_mtok": (128, 32), "sb_mrow": (1, MT),
               "sb_bcast2": (128, MT), "sb_ones1": (1, 128), "sb_lgt": (128, 32),
               }[tname]
        tap_out[tname] = nc.dram_tensor("tap_" + tname, list(shp), F32,
                                        kind="ExternalOutput").ap()

    def sb(name, shape, dt=F32):
        return nc.alloc_sbuf_tensor(name, list(shape), dt).ap()

    A = [sb(f"A{i}", (128, NMT, MT)) for i in range(4)]
    Af = [a.rearrange("p a b -> p (a b)") for a in A]
    GB = [sb("GB0", (128, 4, 544)), sb("GB1", (128, 4, 512)),
          sb("GB2", (128, 4, 512)), sb("GB3", (128, 2, 512))]
    QT = sb("QT", (128, 8, 512))
    WS = [sb(f"WS{i}", (128, 8, 128)) for i in range(2)]
    WB = [sb(f"WB{i}", (128, 2048)) for i in range(2)]
    ident = sb("identsb", (128, 128))
    ones1 = sb("ones1", (1, 128))
    ones_col = sb("ones_col", (128, 1))
    KV = sb("KV", (128, 8, 1024))
    KSUM = sb("KSUM", (128, 8))
    stat = sb("stat", (128, 32))
    rstat = sb("rstat", (1, 4 * MT))
    rowbuf = rstat
    BCAST = sb("BCAST", (128, 4, MT))
    carry = sb("carry", (128, 16))
    halo = sb("halo", (128, 16, 4))
    p_bqkv = sb("p_bqkv", (128, 24))
    p_bg = sb("p_bg", (128, 22))
    p_bu = sb("p_bu", (128, 22))
    p_bd = sb("p_bd", (128, 8))
    p_g1 = sb("p_g1", (128, 8))
    p_b1 = sb("p_b1", (128, 8))
    p_g2 = sb("p_g2", (128, 8))
    p_b2 = sb("p_b2", (128, 8))
    p_fb1 = sb("p_fb1", (128, 32))
    p_fb2q = sb("p_fb2q", (128, 8))
    p_convw = sb("p_convw", (128, 16, 4))
    p_convb = sb("p_convb", (128, 16))
    p_dtpb = sb("p_dtpb", (128, 16))
    p_aneg = sb("p_aneg", (128, 16, 16))
    p_dpar = sb("p_dpar", (128, 16))
    p_mb1 = sb("p_mb1", (128, NE, 8))
    p_mb2 = sb("p_mb2", (128, NE, 8))
    p_mbg = sb("p_mbg", (8, 1))
    p_mwg = sb("p_mwg", (128, 8, 8))
    DTW = sb("DTW", (64, DI))
    XPW = sb("XPW", (128, 16, 96))
    HTB = sb("HTB", (128, 4, 512), BF16)
    LOG = GB[3][0:8, 0, :]
    LGT = sb("LGT", (128, 4, 8))
    MXC = sb("MXC", (128, 4))
    ONES8 = sb("ONES8", (128, 4, 8))
    MTOK = sb("MTOK", (128, 4, 8))
    qtmp = [QT[:, i, :] for i in range(8)]

    prog = Prog(nc)
    for s in ("s_ld", "s_st", "s_pe", "s_v", "s_s", "s_g"):
        prog.new_sem(s)
    psum = PsumPool(prog, nbanks=4)
    psBC = nc.alloc_psum_tensor("psBC", [128, 2048], F32).ap()

    W = prog.wait
    EM = prog.emit

    def ld(out, in_, slow=False):
        return prog.dma("sync", out, in_, "s_ld", slow=slow)

    def st(out, in_):
        return prog.dma("sync", out, in_, "s_st")

    EM("vector", lambda E: E.memset(ones1, 1.0), inc="s_v")
    EM("vector", lambda E: E.memset(ones_col, 1.0), inc="s_v")
    for cv in (1e-5, -0.5):
        cst = sb(f"cst_{cv}", (128, 1))
        nc.const_aps.register(F32, float(cv), cst)
        EM("vector", lambda E, t=cst, v=float(cv): E.memset(t, v), inc="s_v")
    W("scalar", "s_v")
    W("tensor", "s_v")
    ld(ident, ident_in)
    ld(p_bqkv, bqkv.rearrange("(j p) -> p j", p=128), slow=True)
    ld(p_bg[:, 0:21], bg[0:2688].rearrange("(j p) -> p j", p=128), slow=True)
    ld(p_bg[0:42, 21:22], bg[2688:2730].rearrange("(j p) -> p j", p=42), slow=True)
    ld(p_bu[:, 0:21], bu[0:2688].rearrange("(j p) -> p j", p=128), slow=True)
    ld(p_bu[0:42, 21:22], bu[2688:2730].rearrange("(j p) -> p j", p=42), slow=True)
    ld(p_bd, bd.rearrange("(j p) -> p j", p=128), slow=True)
    ld(p_g1, g1.rearrange("(j p) -> p j", p=128), slow=True)
    ld(p_b1, b1.rearrange("(j p) -> p j", p=128), slow=True)
    ld(p_g2, g2.rearrange("(j p) -> p j", p=128), slow=True)
    ld(p_b2, b2.rearrange("(j p) -> p j", p=128), slow=True)
    ld(p_fb1, fb1.rearrange("(j p) -> p j", p=128), slow=True)
    ld(p_fb2q, fb2.rearrange("(j p) -> p j", p=128), slow=True)
    ld(p_convw, convw.rearrange("(j p) c -> p j c", p=128), slow=True)
    ld(p_convb, convb.rearrange("(j p) -> p j", p=128), slow=True)
    ld(p_dtpb, dtpb.rearrange("(j p) -> p j", p=128), slow=True)
    ld(p_aneg, alog.rearrange("(j p) n -> p j n", p=128), slow=True)
    ld(p_dpar, dpar.rearrange("(j p) -> p j", p=128), slow=True)
    ld(p_mb1, mb1.rearrange("e (j p) -> p e j", p=128), slow=True)
    ld(p_mb2, mb2.rearrange("e (j p) -> p e j", p=128), slow=True)
    ld(p_mbg, mbg[:, None], slow=True)
    ld(p_mwg, mWg.rearrange("(dc p) e -> p dc e", p=128), slow=True)
    ld(rowbuf[:, 0:E3 - D], bqkv[D:E3][None, :])
    W("vector", "s_ld")
    W("scalar", "s_ld")
    EM("scalar", lambda E: E.activation(p_aneg, p_aneg, ACT.Exp), inc="s_s")
    W("vector", "s_s")
    ev(prog, lambda E: E.tensor_scalar_mul(p_aneg, p_aneg, -1.0))
    b2slab = QT[:, 6, :]
    EM("vector", lambda E: E.memset(b2slab, 1.0), inc="s_v")
    W("vector", "s_ld")
    ev(prog, lambda E: E.tensor_copy(b2slab[:, 0:8], p_fb2q))
    emit_quantize(prog, b2slab, [QT[:, i, :] for i in range(6)], b2slab,
                  man=10, min_e=-14, max_val=65504.0, signed=True,
                  nb_e=5, sat=1.0e5)
    ev(prog, lambda E: E.tensor_copy(p_fb2q, b2slab[:, 0:8]))

    prep_state = {}
    def prep_transpose(src_ap, dst_ap, R, C, quant=None):
        CB = min(C, 1024)
        stg = WS[0].rearrange("p a b -> p (a b)")[:, 0:512]
        wb_lastuse = prep_state.setdefault("wb_lastuse", [0, 0])
        for r0 in range(0, R, 128):
            rr = min(128, R - r0)
            wi = (r0 // 128) % 2
            wsl = WB[wi]
            for c0 in range(0, C, CB):
                cb = min(CB, C - c0)
                W("sync", "s_pe", wb_lastuse[wi])
                lv = ld(wsl[0:rr, 0:cb], src_ap[r0:r0 + rr, c0:c0 + cb])
                if quant is not None:
                    W("vector", "s_ld", lv)
                    W("scalar", "s_ld", lv)
                    quant(wsl[0:rr, 0:cb])
                    W("tensor", "s_v")
                W("tensor", "s_ld", lv)
                for cc in range(0, cb, 512):
                    nj = ceil_div(min(512, cb - cc), 128)
                    bi, ps = psum.acquire()
                    for j in range(nj):
                        jw = min(128, cb - cc - j * 128)
                        EM("tensor", lambda E, ps=ps, wsl=wsl, cc=cc, j=j, jw=jw, rr=rr: E.transpose(
                            ps[0:jw, j * 128:j * 128 + rr],
                            wsl[0:rr, cc + j * 128:cc + j * 128 + jw],
                            ident[0:rr, 0:rr]),
                            inc=("s_pe" if j == nj - 1 else None))
                    W("scalar", "s_pe")
                    W("scalar", "s_st")
                    vv = prog.emit("scalar", lambda E, ps=ps: E.copy(stg, ps),
                                   inc="s_s")
                    psum.release(bi, "s_s", vv)
                    W("sync", "s_s", vv)
                    for j in range(nj):
                        jw = min(128, cb - cc - j * 128)
                        st(dst_ap[c0 + cc + j * 128: c0 + cc + j * 128 + jw,
                                  r0:r0 + rr],
                           stg[0:jw, j * 128:j * 128 + rr])
                wb_lastuse[wi] = prog.cnt["s_pe"]

    def wq_quant_slab(ap_slab):
        for h in range(2):
            sl = ap_slab[:, h * 512:(h + 1) * 512]
            emit_quantize(prog, sl, [qtmp[i] for i in range(6)], sl,
                          man=3, min_e=-6, max_val=240.0, signed=True,
                          nb_e=4, sat=1000.0)

    prep_transpose(fw1, w1T_s, L0H, D)
    prep_transpose(fw2, w2qT_s, D, L0H, quant=wq_quant_slab)
    prep_transpose(inpw, inpT_s, 2 * DI, D)
    prep_transpose(outpw, outpT_s, D, DI)
    prep_transpose(xpw, xpT_s, DTR + 2 * DS, DI)
    prep_transpose(dtpw, dtpT_s, DI, DTR)
    W("sync", "s_st")
    ld(DTW, dtpT_s)
    ld(XPW, xpT_s.rearrange("(dc p) r -> p dc r", p=128))

    SQB = QT[:, 0:2, :].rearrange("p a b -> p (a b)")
    EM("vector", lambda E: E.memset(KV.rearrange("p a b -> p (a b)"), 0.0), inc="s_v")
    EM("vector", lambda E: E.memset(rowbuf[0:1, 0:1024], 0.0), inc="s_v")
    W("sync", "s_st")
    ks_started = [False]
    ws_lastuse = [0, 0]
    wdb_lastuse = [prog.cnt["s_pe"], prog.cnt["s_pe"]]
    WDB = WB[0].bitcast(BF16).rearrange("p (kh d) -> p kh d", d=1024)
    W("scalar", "s_v")
    for mt in range(NMT):
        XTf = Af[0]
        W("sync", "s_pe")
        W("vector", "s_pe")
        lv = ld(XTf.rearrange("p (s d) -> p s d", d=D),
                x_in[mt * MT:(mt + 1) * MT, :].rearrange("(s p) d -> p s d", p=128))
        W("vector", "s_ld", lv)
        W("scalar", "s_ld", lv)
        for s in range(4):
            xs = XTf[:, s * D:(s + 1) * D]
            ev(prog, lambda E, xs=xs, s=s: E.reduce_sum(stat[:, s:s + 1], xs, axis=AX.X))
            W("scalar", "s_v")
            prog.emit("scalar", lambda E, xs=xs: E.activation(SQB, xs, ACT.Square), inc="s_s")
            W("vector", "s_s")
            ev(prog, lambda E, s=s: E.reduce_sum(stat[:, 4 + s:5 + s], SQB, axis=AX.X))
        W("vector", "s_v")
        ev(prog, lambda E: E.tensor_scalar_mul(stat[:, 8:12], stat[:, 0:4], 1.0 / D))
        ev(prog, lambda E: E.tensor_scalar_mul(stat[:, 12:16], stat[:, 4:8], 1.0 / D))
        W("scalar", "s_v")
        prog.emit("scalar", lambda E: E.activation(stat[:, 16:20], stat[:, 8:12], ACT.Square), inc="s_s")
        W("vector", "s_s")
        ev(prog, lambda E: E.tensor_tensor(stat[:, 12:16], stat[:, 12:16], stat[:, 16:20], ALU.subtract))
        W("scalar", "s_v")
        prog.emit("scalar", lambda E: E.activation(stat[:, 12:16], stat[:, 12:16], ACT.Ln, bias=1e-5), inc="s_s")
        prog.emit("scalar", lambda E: E.activation(stat[:, 12:16], stat[:, 12:16], ACT.Exp, scale=-0.5), inc="s_s")
        W("vector", "s_s")
        XN = Af[1]
        for s in range(4):
            ev(prog, lambda E, s=s: E.tensor_scalar(
                XN[:, s * D:(s + 1) * D], XTf[:, s * D:(s + 1) * D],
                stat[:, 8 + s:9 + s], stat[:, 12 + s:13 + s], ALU.subtract, ALU.mult))
        XF = A[2]
        W("tensor", "s_v")
        for dc in range(8):
            bi, ps = psum.acquire()
            for s in range(4):
                EM("tensor", lambda E, ps=ps, dc=dc, s=s: E.transpose(
                    ps[:, s * 128:(s + 1) * 128],
                    XN[:, s * D + dc * 128: s * D + (dc + 1) * 128], ident),
                    inc=("s_pe" if s == 3 else None))
            W("scalar", "s_pe")
            v = prog.emit("scalar", lambda E, ps=ps, dc=dc: E.copy(XF[:, dc, :], ps), inc="s_s")
            psum.release(bi, "s_s", v)
        W("sync", "s_s")
        st(xn_s[:, mt * MT:(mt + 1) * MT].rearrange("(dc p) t -> p dc t", p=128),
           XF)
        KPF = Af[0]
        VF = Af[1]
        for ec in range(24):
            wsl = WS[ec % 2]
            W("sync", "s_pe", ws_lastuse[ec % 2])
            lv = ld(wsl, Wqkv[:, ec * 128:(ec + 1) * 128].rearrange("(dc p) e -> p dc e", p=128))
            W("tensor", "s_ld", lv)
            W("tensor", "s_s")
            bi, ps = psum.acquire()
            for dc in range(8):
                EM("tensor", lambda E, ps=ps, wsl=wsl, dc=dc: E.matmul(
                    ps, wsl[:, dc, :], XF[:, dc, :],
                    start=(dc == 0), stop=(dc == 7)),
                    inc=("s_pe" if dc == 7 else None))
            ws_lastuse[ec % 2] = prog.cnt["s_pe"]
            W("scalar", "s_pe")
            W("vector", "s_pe")
            if ec < 8:
                zb = QT[:, 6, :]
                W("scalar", "s_st")
                v1 = prog.emit("scalar", lambda E, ps=ps, ec=ec, zb=zb: E.activation(
                    zb, ps, ACT.Identity, bias=p_bqkv[:, ec:ec + 1]), inc="s_s")
                psum.release(bi, "s_s", v1)
                W("vector", "s_s", v1)
                emit_elup1(prog, zb, QT[:, 7, :], zb)
                W("sync", "s_v")
                st(qp_s[ec * 128:(ec + 1) * 128, mt * MT:(mt + 1) * MT], zb)
                W("vector", "s_st")
            else:
                dstf = KPF if ec < 16 else VF
                cc = (ec - 8) % 8
                zb = dstf[:, cc * 512:(cc + 1) * 512]
                v1 = prog.emit("scalar", lambda E, ps=ps, ec=ec, zb=zb: E.activation(
                    zb, ps, ACT.Identity, bias=p_bqkv[:, ec:ec + 1]), inc="s_s")
                psum.release(bi, "s_s", v1)
                if ec < 16:
                    W("vector", "s_s", v1)
                    emit_elup1(prog, zb, QT[:, 7, :], zb)
        KPT = A[3]
        KPTf = Af[3]
        VTf = Af[2]
        W("tensor", "s_v")
        W("tensor", "s_s")
        W("scalar", "s_st")
        for src, dstf in ((KPF, KPTf), (VF, VTf)):
            for tc in range(4):
                for eh in range(2):
                    bi, ps = psum.acquire()
                    for j in range(4):
                        ecol = eh * 4 + j
                        EM("tensor", lambda E, ps=ps, src=src, tc=tc, ecol=ecol, j=j: E.transpose(
                            ps[:, j * 128:(j + 1) * 128],
                            src[:, ecol * 512 + tc * 128: ecol * 512 + (tc + 1) * 128],
                            ident),
                            inc=("s_pe" if j == 3 else None))
                    W("scalar", "s_pe")
                    v = prog.emit("scalar", lambda E, ps=ps, dstf=dstf, tc=tc, eh=eh: E.copy(
                        dstf[:, tc * 1024 + eh * 512: tc * 1024 + (eh + 1) * 512], ps),
                        inc="s_s")
                    psum.release(bi, "s_s", v)
        W("tensor", "s_s")
        for dc in range(8):
            for ev2 in range(2):
                bi, ps = psum.acquire()
                for tc in range(4):
                    EM("tensor", lambda E, ps=ps, dc=dc, ev2=ev2, tc=tc: E.matmul(
                        ps, KPTf[:, tc * 1024 + dc * 128: tc * 1024 + (dc + 1) * 128],
                        VTf[:, tc * 1024 + ev2 * 512: tc * 1024 + (ev2 + 1) * 512],
                        start=(tc == 0), stop=(tc == 3)),
                        inc=("s_pe" if tc == 3 else None))
                W("vector", "s_pe")
                v = ev(prog, lambda E, ps=ps, dc=dc, ev2=ev2: E.tensor_tensor(
                    KV[:, dc, ev2 * 512:(ev2 + 1) * 512],
                    KV[:, dc, ev2 * 512:(ev2 + 1) * 512], ps, ALU.add))
                psum.release(bi, "s_v", v)
        for eh in range(2):
            bks, ps_ks = psum.acquire()
            for tc in range(4):
                EM("tensor", lambda E, tc=tc, eh=eh, ps=ps_ks: E.matmul(
                    ps[0:1, 0:512], ones_col,
                    KPTf[:, tc * 1024 + eh * 512: tc * 1024 + (eh + 1) * 512],
                    start=(tc == 0), stop=(tc == 3)),
                    inc=("s_pe" if tc == 3 else None))
            W("vector", "s_pe")
            vks = ev(prog, lambda E, eh=eh, ps=ps_ks: E.tensor_tensor(
                rowbuf[0:1, eh * 512:(eh + 1) * 512],
                rowbuf[0:1, eh * 512:(eh + 1) * 512], ps[0:1, 0:512], ALU.add))
            psum.release(bks, "s_v", vks)


    if "kpt" in tap_out:
        W("sync", "s_v")
        W("sync", "s_s")
        prog.dma("sync", tap_out["kpt"], Af[3], "s_st")
    if "vtm" in tap_out:
        W("sync", "s_v")
        W("sync", "s_s")
        prog.dma("sync", tap_out["vtm"], Af[2], "s_st")
    if "kpf" in tap_out:
        W("sync", "s_v")
        W("sync", "s_s")
        prog.dma("sync", tap_out["kpf"], Af[0], "s_st")
    W("sync", "s_v")
    st(ksum_s, rowbuf[0:1, 0:1024])
    st(kv_s.rearrange("(dc p) e -> p dc e", p=128), KV)
    W("sync", "s_st")
    lv = ld(KSUM, ksum_s[0, :].rearrange("(j p) -> p j", p=128), slow=True)

    if stop_after == "phA":
        _finish(prog, nc, psum, out_hbm, tap_out, locals())
        return _close(nc, prog, stack, tap_out)

    KVf = KV.rearrange("p a b -> p (a b)")
    QTf = QT.rearrange("p a b -> p (a b)")
    GBf = [g.rearrange("p a b -> p (a b)") for g in GB]
    q2 = [KVf[:, i * 2048:(i + 1) * 2048] for i in range(4)] + \
         [QTf[:, i * 2048:(i + 1) * 2048] for i in range(2)]
    SQ2 = [QT[:, 0, :], QT[:, 1, :]]

    def emit_fmln(src3, gcol3=None, bcol3=None):
        W("tensor", "s_v")
        W("scalar", "s_v")
        bmu, ps_mu = psum.acquire()
        for dc in range(8):
            EM("tensor", lambda E, ps=ps_mu, dc=dc: E.matmul(
                ps[0:1, :], ones_col, src3[:, dc, :],
                start=(dc == 0), stop=(dc == 7)),
                inc=("s_pe" if dc == 7 else None))
        bm2, ps_m2 = psum.acquire()
        sq_free = [0, 0]
        for dc in range(8):
            sq = SQ2[dc % 2]
            prog.wait("scalar", "s_pe", sq_free[dc % 2])
            vsq = prog.emit("scalar", lambda E, sq=sq, dc=dc: E.activation(
                sq, src3[:, dc, :], ACT.Square), inc="s_s")
            W("tensor", "s_s", vsq)
            sq_free[dc % 2] = EM("tensor", lambda E, ps=ps_m2, sq=sq, dc=dc: E.matmul(
                ps[0:1, :], ones_col, sq,
                start=(dc == 0), stop=(dc == 7)), inc="s_pe")
        r0 = rstat[0:1, 0:512]; r1 = rstat[0:1, 512:1024]
        r2 = rstat[0:1, 1024:1536]; r3 = rstat[0:1, 1536:2048]
        W("vector", "s_pe")
        ev(prog, lambda E, ps=ps_mu: E.tensor_scalar_mul(r0, ps[0:1, :], 1.0 / D))
        ev(prog, lambda E, ps=ps_m2: E.tensor_scalar_mul(r1, ps[0:1, :], 1.0 / D))
        v1 = ev(prog, lambda E: E.tensor_tensor(r3, r0, r0, ALU.mult))
        v2 = ev(prog, lambda E: E.tensor_tensor(r1, r1, r3, ALU.subtract))
        psum.release(bmu, "s_v", v1)
        psum.release(bm2, "s_v", v2)
        W("scalar", "s_v")
        prog.emit("scalar", lambda E: E.activation(r1, r1, ACT.Ln, bias=1e-5), inc="s_s")
        prog.emit("scalar", lambda E: E.activation(r1, r1, ACT.Exp, scale=-0.5), inc="s_s")
        W("vector", "s_s")
        ev(prog, lambda E: E.tensor_tensor(r2, r0, r1, ALU.mult))
        ev(prog, lambda E: E.tensor_scalar_mul(r2, r2, -1.0))
        W("tensor", "s_v")
        for i, row in ((0, r1), (1, r2)):
            bb, psb = psum.acquire()
            EM("tensor", lambda E, ps=psb, row=row: E.matmul(
                ps, ones1, row, start=True, stop=True), inc="s_pe")
            W("scalar", "s_pe")
            vv = prog.emit("scalar", lambda E, ps=psb, i=i: E.copy(BCAST[:, i, :], ps), inc="s_s")
            psum.release(bb, "s_s", vv)
        W("vector", "s_s")
        for dc in range(8):
            ev(prog, lambda E, dc=dc: E.tensor_tensor(src3[:, dc, :], src3[:, dc, :], BCAST[:, 0, :], ALU.mult))
            ev(prog, lambda E, dc=dc: E.tensor_tensor(src3[:, dc, :], src3[:, dc, :], BCAST[:, 1, :], ALU.add))
            if gcol3 is not None:
                ev(prog, lambda E, dc=dc: E.tensor_scalar(
                    src3[:, dc, :], src3[:, dc, :], gcol3[:, dc:dc + 1], bcol3[:, dc:dc + 1],
                    ALU.mult, ALU.add))

    mh_chunks = [(i * 128, 128, i) for i in range(21)] + [(2688, 42, 21)]
    ffn_groups = [mh_chunks[0:4], mh_chunks[4:8], mh_chunks[8:12],
                  mh_chunks[12:16], mh_chunks[16:20], mh_chunks[20:22]]

    for mt in range(NMT):
        t0c, t1c = mt * MT, (mt + 1) * MT
        XNB = A[0]
        GACC = A[1]
        W("sync", "s_v")
        W("sync", "s_s")
        W("sync", "s_pe")
        W("sync", "s_st")
        lvxn = ld(XNB, xn_s[:, t0c:t1c].rearrange("(dc p) t -> p dc t", p=128))
        W("vector", "s_st")
        EM("vector", lambda E: E.memset(Af[1], 0.0), inc="s_v")
        W("tensor", "s_ld", lvxn)
        for gi in range(8):
            HT = GBf[0][:, 0:2048]
            HT3 = GB[0]
            for sub in range(4):
                mh = gi * 4 + sub
                wsl = WS[mh % 2]
                W("sync", "s_pe", ws_lastuse[mh % 2])
                lv = ld(wsl, w1T_s[:, mh * 128:(mh + 1) * 128].rearrange(
                    "(dc p) h -> p dc h", p=128))
                W("tensor", "s_ld", lv)
                bi, ps = psum.acquire()
                for dc in range(8):
                    EM("tensor", lambda E, ps=ps, wsl=wsl, dc=dc: E.matmul(
                        ps, wsl[:, dc, :], XNB[:, dc, :],
                        start=(dc == 0), stop=(dc == 7)),
                        inc=("s_pe" if dc == 7 else None))
                ws_lastuse[mh % 2] = prog.cnt["s_pe"]
                W("scalar", "s_pe")
                prog.wait("scalar", "s_v")
                prog.wait("scalar", "s_g")
                vv = prog.emit("scalar", lambda E, ps=ps, mh=mh, sub=sub: E.activation(
                    GBf[0][:, sub * 512:(sub + 1) * 512], ps, ACT.Relu,
                    bias=p_fb1[:, mh:mh + 1]), inc="s_s")
                psum.release(bi, "s_s", vv)
            W("vector", "s_s")
            emit_quantize(prog, HT.rearrange("p (a b) -> p a b", b=512),
                          [q.rearrange("p (a b) -> p a b", b=512) for q in q2],
                          HT.rearrange("p (a b) -> p a b", b=512),
                          man=3, min_e=-6, max_val=240.0, signed=False,
                          nb_e=4, sat=1000.0)
            prog.wait("gpsimd", "s_pe", wdb_lastuse[0])
            lv2 = prog.dma("gpsimd", WDB, w2qT_s[gi * 512:(gi + 1) * 512, :].rearrange(
                "(kh p) d -> p kh d", p=128), "s_g")
            prog.wait("gpsimd", "s_v")
            prog.wait("gpsimd", "s_pe", wdb_lastuse[1])
            lvh = prog.dma("gpsimd", HTB.rearrange("p a b -> p (a b)"),
                           GBf[0][:, 0:2048], "s_g")
            W("tensor", "s_g", lvh)
            for md in range(8):
                bi, ps = psum.acquire()
                for kh in range(4):
                    EM("tensor", lambda E, ps=ps, kh=kh, md=md: E.matmul(
                        ps, WDB[:, kh, md * 128:(md + 1) * 128],
                        HTB[:, kh, :],
                        start=(kh == 0), stop=(kh == 3)),
                        inc=("s_pe" if kh == 3 else None))
                W("vector", "s_pe")
                vv = ev(prog, lambda E, ps=ps, md=md: E.tensor_tensor(
                    GACC[:, md, :], GACC[:, md, :], ps, ALU.add))
                psum.release(bi, "s_v", vv)
            wdb_lastuse[0] = wdb_lastuse[1] = prog.cnt["s_pe"]
        for dc in range(8):
            ev(prog, lambda E, dc=dc: E.tensor_scalar_add(
                GACC[:, dc, :], GACC[:, dc, :], p_fb2q[:, dc:dc + 1]))
        GACCf = Af[1]
        for h in range(2):
            sl = GACCf[:, h * 2048:(h + 1) * 2048].rearrange("p (a b) -> p a b", b=512)
            emit_quantize(prog, sl, [q.rearrange("p (a b) -> p a b", b=512) for q in q2],
                          sl, man=10, min_e=-14, max_val=65504.0, signed=True,
                          nb_e=5, sat=1.0e5)
        W("scalar", "s_v")
        prog.emit("scalar", lambda E: E.activation(GACCf, GACCf, ACT.Silu), inc="s_s")
        if "g" in tap_out:
            vtp = prog.cnt["s_s"]
            W("sync", "s_s", vtp)
            prog.dma("sync", tap_out["g"][:, t0c:t1c].rearrange("(dc p) t -> p dc t", p=128),
                     A[1], "s_st")
            W("vector", "s_st")
        QP = A[2]
        X1 = A[3]
        lvqp = ld(QP, qp_s[:, t0c:t1c].rearrange("(dc p) t -> p dc t", p=128))
        bden, ps_den = psum.acquire()
        W("tensor", "s_ld", lvqp)
        for dc in range(8):
            EM("tensor", lambda E, ps=ps_den, dc=dc: E.matmul(
                ps[0:1, :], KSUM[:, dc:dc + 1], QP[:, dc, :],
                start=(dc == 0), stop=(dc == 7)),
                inc=("s_pe" if dc == 7 else None))
        W("vector", "s_pe")
        ev(prog, lambda E, ps=ps_den: E.tensor_scalar_add(rstat[0:1, 0:512], ps[0:1, :], 1e-6))
        if "denr" in tap_out:
            W("sync", "s_v")
            prog.dma("sync", tap_out["denr"][0:1, t0c:t1c], rstat[0:1, 0:512], "s_st")
            W("vector", "s_st")
        v1 = ev(prog, lambda E: E.reciprocal(rstat[0:1, 0:512], rstat[0:1, 0:512]))
        psum.release(bden, "s_v", v1)
        W("tensor", "s_v")
        brd, ps_rd = psum.acquire()
        EM("tensor", lambda E, ps=ps_rd: E.matmul(ps, ones1, rstat[0:1, 0:512],
                                                  start=True, stop=True), inc="s_pe")
        W("scalar", "s_pe")
        vrd = prog.emit("scalar", lambda E, ps=ps_rd: E.copy(BCAST[:, 2, :], ps), inc="s_s")
        psum.release(brd, "s_s", vrd)
        W("vector", "s_s", vrd)
        for me in range(8):
            wsl = WS[me % 2]
            W("sync", "s_pe", ws_lastuse[me % 2])
            lv = ld(wsl, kv_s[:, me * 128:(me + 1) * 128].rearrange(
                "(dc p) e -> p dc e", p=128))
            W("tensor", "s_ld", lv)
            bi, ps = psum.acquire()
            for dc in range(8):
                EM("tensor", lambda E, ps=ps, wsl=wsl, dc=dc: E.matmul(
                    ps, wsl[:, dc, :], QP[:, dc, :],
                    start=(dc == 0), stop=(dc == 7)),
                    inc=("s_pe" if dc == 7 else None))
            ws_lastuse[me % 2] = prog.cnt["s_pe"]
            W("vector", "s_pe")
            ev(prog, lambda E, ps=ps, me=me: E.tensor_tensor(
                X1[:, me, :], ps, BCAST[:, 2, :], ALU.mult))
            vv = ev(prog, lambda E, me=me: E.tensor_tensor(
                X1[:, me, :], X1[:, me, :], XNB[:, me, :], ALU.add))
            psum.release(bi, "s_v", vv)
        if "x1r" in tap_out:
            W("sync", "s_v")
            prog.dma("sync", tap_out["x1r"][:, t0c:t1c].rearrange("(dc p) t -> p dc t", p=128),
                     A[3], "s_st")
            W("vector", "s_st")
        if "rden" in tap_out:
            W("sync", "s_v")
            prog.dma("sync", tap_out["rden"][0:1, t0c:t1c], rstat[0:1, 0:512], "s_st")
            W("vector", "s_st")
        W("tensor", "s_v")
        emit_fmln(X1, p_g1, p_b1)
        if "x1" in tap_out:
            W("sync", "s_v")
            prog.dma("sync", tap_out["x1"][:, t0c:t1c].rearrange("(dc p) t -> p dc t", p=128),
                     A[3], "s_st")
        FFN = A[0]
        W("vector", "s_v")
        EM("vector", lambda E: E.memset(Af[0], 0.0), inc="s_v")
        for grp in ffn_groups:
            for si, (h0, w, bc) in enumerate(grp):
                for which, wmat, pb in (("g", Wg, p_bg), ("u", Wu, p_bu)):
                    wsl = WS[0] if which == "g" else WS[1]
                    W("sync", "s_pe", ws_lastuse[0 if which == "g" else 1])
                    lv = ld(wsl[:, :, 0:w], wmat[:, h0:h0 + w].rearrange(
                        "(dc p) h -> p dc h", p=128))
                    W("tensor", "s_ld", lv)
                    bi, ps = psum.acquire()
                    for dc in range(8):
                        EM("tensor", lambda E, ps=ps, wsl=wsl, dc=dc, w=w: E.matmul(
                            ps[0:w, :], wsl[:, dc, 0:w], X1[:, dc, :],
                            start=(dc == 0), stop=(dc == 7)),
                            inc=("s_pe" if dc == 7 else None))
                    ws_lastuse[0 if which == "g" else 1] = prog.cnt["s_pe"]
                    W("scalar", "s_pe")
                    prog.wait("scalar", "s_v")
                    if which == "g":
                        vv = prog.emit("scalar", lambda E, ps=ps, w=w, bc=bc, pb=pb: E.activation(
                            GB[1][0:w, 0, 0:512], ps[0:w, :],
                            ACT.Silu, bias=pb[0:w, bc:bc + 1]), inc="s_s")
                        psum.release(bi, "s_s", vv)
                    else:
                        vv = prog.emit("scalar", lambda E, ps=ps, w=w, bc=bc, pb=pb: E.activation(
                            GB[1][0:w, 1, 0:512], ps[0:w, :],
                            ACT.Identity, bias=pb[0:w, bc:bc + 1]), inc="s_s")
                        psum.release(bi, "s_s", vv)
                W("vector", "s_s")
                W("vector", "s_pe")
                ev(prog, lambda E, si=si, w=w: E.tensor_tensor(
                    GB[0][0:w, si, 0:512], GB[1][0:w, 0, 0:512], GB[1][0:w, 1, 0:512],
                    ALU.mult))
            gh0 = grp[0][0]
            gw = sum(w for (_, w, _) in grp)
            nkh = len(grp)
            wd_stage = KVf[:, 0:4096].rearrange("p (kh d) -> p kh d", d=1024)
            W("sync", "s_v")
            W("sync", "s_pe")
            lvw = None
            for ki, (h0, w, _) in enumerate(grp):
                lvw = ld(wd_stage[0:w, ki, :], Wd[h0:h0 + w, :])
            W("tensor", "s_ld", lvw)
            W("tensor", "s_v")
            for md in range(8):
                bi, ps = psum.acquire()
                for ki, (h0, w, _) in enumerate(grp):
                    is_last = (ki == nkh - 1)
                    EM("tensor", lambda E, ps=ps, wd=wd_stage, ki=ki, md=md, w=w,
                            first=(ki == 0), last=is_last: E.matmul(
                        ps, wd[0:w, ki, md * 128:(md + 1) * 128],
                        GB[0][0:w, ki, 0:512],
                        start=first, stop=last),
                        inc=("s_pe" if is_last else None))
                W("vector", "s_pe")
                vv = ev(prog, lambda E, ps=ps, md=md: E.tensor_tensor(
                    FFN[:, md, :], FFN[:, md, :], ps, ALU.add))
                psum.release(bi, "s_v", vv)
        for dc in range(8):
            ev(prog, lambda E, dc=dc: E.tensor_scalar_add(
                FFN[:, dc, :], FFN[:, dc, :], p_bd[:, dc:dc + 1]))
            ev(prog, lambda E, dc=dc: E.tensor_tensor(
                FFN[:, dc, :], FFN[:, dc, :], X1[:, dc, :], ALU.add))
        W("tensor", "s_v")
        emit_fmln(FFN, p_g2, p_b2)
        if "s0" in tap_out:
            W("sync", "s_v")
            prog.dma("sync", tap_out["s0"][:, t0c:t1c].rearrange("(dc p) t -> p dc t", p=128),
                     A[0], "s_st")
            W("vector", "s_st")
        for dc in range(8):
            ev(prog, lambda E, dc=dc: E.tensor_tensor(
                GACC[:, dc, :], A[0][:, dc, :], GACC[:, dc, :], ALU.mult))
        XT2 = A[2]
        W("sync", "s_v")
        W("sync", "s_pe")
        lvx = ld(Af[2].rearrange("p (s d) -> p s d", d=D),
                 x_in[t0c:t1c, :].rearrange("(s p) d -> p s d", p=128))
        W("tensor", "s_ld", lvx)
        XT2f = Af[2]
        for dc in range(8):
            bi, ps = psum.acquire()
            for s in range(4):
                EM("tensor", lambda E, ps=ps, dc=dc, s=s: E.transpose(
                    ps[:, s * 128:(s + 1) * 128],
                    XT2f[:, s * D + dc * 128: s * D + (dc + 1) * 128], ident),
                    inc=("s_pe" if s == 3 else None))
            W("vector", "s_pe")
            vv = ev(prog, lambda E, ps=ps, dc=dc: E.tensor_tensor(
                GB[2][:, 0, 0:512] if False else GB[2].rearrange("p a b -> p (a b)")[:, 0:512],
                ps, GACC[:, dc, :], ALU.add))
            psum.release(bi, "s_v", vv)
            W("sync", "s_v", vv)
            prog.dma("sync", xnew_s[dc * 128:(dc + 1) * 128, t0c:t1c],
                     GB[2].rearrange("p a b -> p (a b)")[:, 0:512], "s_st")
            W("vector", "s_st")
        XNEW = A[2]
        W("sync", "s_pe")
        W("sync", "s_st")
        lvn = ld(XNEW, xnew_s[:, t0c:t1c].rearrange("(dc p) t -> p dc t", p=128))
        W("vector", "s_ld", lvn)
        W("tensor", "s_ld", lvn)
        W("scalar", "s_ld", lvn)
        emit_fmln(XNEW, None, None)
        W("sync", "s_v")
        st(xn1_s[:, t0c:t1c].rearrange("(dc p) t -> p dc t", p=128), XNEW)

    if stop_after == "phB":
        _finish(prog, nc, psum, out_hbm, tap_out, locals())
        return _close(nc, prog, stack, tap_out)

    EM("vector", lambda E: E.memset(halo.rearrange("p a b -> p (a b)"), 0.0), inc="s_v")
    for mt in range(NMT):
        t0c, t1c = mt * MT, (mt + 1) * MT
        XN1 = A[0]
        W("sync", "s_v")
        W("sync", "s_s")
        W("sync", "s_pe")
        W("sync", "s_st")
        lvx = ld(XN1, xn1_s[:, t0c:t1c].rearrange("(dc p) t -> p dc t", p=128))
        W("tensor", "s_ld", lvx)
        XD = BCAST[0:96, 3, :]
        W("vector", "s_pe")
        W("vector", "s_st")
        EM("vector", lambda E, XD=XD: E.memset(XD, 0.0), inc="s_v")
        for grp in range(4):
            for sub in range(4):
                idx = grp * 4 + sub
                wsl = WS[idx % 2]
                W("sync", "s_pe", ws_lastuse[idx % 2])
                lv = ld(wsl, inpT_s[:, idx * 128:(idx + 1) * 128].rearrange(
                    "(dc p) e -> p dc e", p=128))
                W("tensor", "s_ld", lv)
                bi, ps = psum.acquire()
                for dc in range(8):
                    EM("tensor", lambda E, ps=ps, wsl=wsl, dc=dc, XN1=XN1: E.matmul(
                        ps, wsl[:, dc, :], XN1[:, dc, :],
                        start=(dc == 0), stop=(dc == 7)),
                        inc=("s_pe" if dc == 7 else None))
                ws_lastuse[idx % 2] = prog.cnt["s_pe"]
                W("scalar", "s_pe")
                prog.wait("scalar", "s_v")
                vv = prog.emit("scalar", lambda E, ps=ps, sub=sub: E.copy(
                    GB[0][:, sub, 3:515], ps), inc="s_s")
                psum.release(bi, "s_s", vv)
                prog.emit("scalar", lambda E, sub=sub, idx=idx: E.copy(
                    GB[0][:, sub, 0:3], halo[:, idx, 0:3]), inc="s_s")
                vhl = prog.emit("scalar", lambda E, sub=sub, idx=idx: E.copy(
                    halo[:, idx, 0:3], GB[0][:, sub, 512:515]), inc="s_s")
                W("vector", "s_s", vhl)
                XC = GB[1]
                W("vector", "s_st")
                ev(prog, lambda E, sub=sub, idx=idx: E.tensor_scalar(
                    XC[:, sub, 0:512], GB[0][:, sub, 0:512],
                    p_convw[:, idx, 0:1], None, ALU.mult))
                for j in range(1, 4):
                    ev(prog, lambda E, sub=sub, idx=idx, j=j: E.scalar_tensor_tensor(
                        XC[:, sub, 0:512], GB[0][:, sub, j:j + 512],
                        p_convw[:, idx, j:j + 1], XC[:, sub, 0:512],
                        ALU.mult, ALU.add))
                prog.wait("scalar", "s_v")
                vsl = prog.emit("scalar", lambda E, sub=sub, idx=idx: E.activation(
                    XC[:, sub, 0:512], XC[:, sub, 0:512], ACT.Silu,
                    bias=p_convb[:, idx:idx + 1]), inc="s_s")
            W("sync", "s_s")
            st(xc_s[grp * 512:(grp + 1) * 512, t0c:t1c].rearrange(
                "(s p) t -> p s t", p=128), GB[1][:, :, 0:512])
            bi, ps = psum.acquire()
            W("tensor", "s_s")
            for sub in range(4):
                idx = grp * 4 + sub
                EM("tensor", lambda E, ps=ps, idx=idx, sub=sub: E.matmul(
                    ps[0:96, :], XPW[:, idx, :], GB[1][:, sub, 0:512],
                    start=(sub == 0), stop=(sub == 3)),
                    inc=("s_pe" if sub == 3 else None))
            W("vector", "s_pe")
            vv = ev(prog, lambda E, ps=ps, XD=XD: E.tensor_tensor(
                XD, XD, ps[0:96, :], ALU.add))
            psum.release(bi, "s_v", vv)
            for sub in range(4):
                idx = 16 + grp * 4 + sub
                wsl = WS[idx % 2]
                W("sync", "s_pe", ws_lastuse[idx % 2])
                lv = ld(wsl, inpT_s[:, idx * 128:(idx + 1) * 128].rearrange(
                    "(dc p) e -> p dc e", p=128))
                W("tensor", "s_ld", lv)
                bi, ps = psum.acquire()
                for dc in range(8):
                    EM("tensor", lambda E, ps=ps, wsl=wsl, dc=dc, XN1=XN1: E.matmul(
                        ps, wsl[:, dc, :], XN1[:, dc, :],
                        start=(dc == 0), stop=(dc == 7)),
                        inc=("s_pe" if dc == 7 else None))
                ws_lastuse[idx % 2] = prog.cnt["s_pe"]
                W("scalar", "s_pe")
                prog.wait("scalar", "s_st")
                vv = prog.emit("scalar", lambda E, ps=ps, sub=sub: E.activation(
                    GB[2][:, sub, 0:512], ps, ACT.Silu), inc="s_s")
                psum.release(bi, "s_s", vv)
            W("sync", "s_s")
            st(zs_s[grp * 512:(grp + 1) * 512, t0c:t1c].rearrange(
                "(s p) t -> p s t", p=128), GB[2][:, :, 0:512])
        W("sync", "s_v")
        st(bc_s[:, t0c:t1c], BCAST[64:96, 3, :])
        for mdi in range(16):
            bi, ps = psum.acquire()
            EM("tensor", lambda E, ps=ps, mdi=mdi: E.matmul(
                ps, DTW[:, mdi * 128:(mdi + 1) * 128], XD[0:64, :],
                start=True, stop=True), inc="s_pe")
            W("scalar", "s_pe")
            prog.wait("scalar", "s_st")
            vv = prog.emit("scalar", lambda E, ps=ps, mdi=mdi: E.activation(
                GB[3][:, 0, 0:512], ps, ACT.Identity,
                bias=p_dtpb[:, mdi:mdi + 1]), inc="s_s")
            psum.release(bi, "s_s", vv)
            W("vector", "s_s", vv)
            emit_softplus(prog, GB[3][:, 0, 0:512], GB[3][:, 1, 0:512], GB[3][:, 0, 0:512])
            ev(prog, lambda E, mdi=mdi: E.tensor_scalar_add(
                GB[3][:, 0, 0:512], GB[3][:, 0, 0:512], p_dtpb[:, mdi:mdi + 1]))
            emit_softplus(prog, GB[3][:, 0, 0:512], GB[3][:, 1, 0:512], GB[3][:, 0, 0:512])
            W("sync", "s_v")
            st(delta_s[mdi * 128:(mdi + 1) * 128, t0c:t1c], GB[3][:, 0, 0:512])

    if stop_after == "phC1":
        _finish(prog, nc, psum, out_hbm, tap_out, locals())
        return _close(nc, prog, stack, tap_out)

    for mt in range(NMT):
        t0c, t1c = mt * MT, (mt + 1) * MT
        XN1 = A[0]
        G2 = A[2]
        HE = A[1]
        W("sync", "s_v")
        W("sync", "s_s")
        W("sync", "s_pe")
        W("sync", "s_st")
        lvx = ld(XN1, xn1_s[:, t0c:t1c].rearrange("(dc p) t -> p dc t", p=128))
        W("vector", "s_pe")
        W("vector", "s_s")
        W("vector", "s_st")
        EM("vector", lambda E: E.memset(Af[2], 0.0), inc="s_v")
        bi, ps = psum.acquire()
        W("tensor", "s_ld", lvx)
        for dc in range(8):
            EM("tensor", lambda E, ps=ps, dc=dc, XN1=XN1: E.matmul(
                ps[0:8, :], p_mwg[:, dc, :], XN1[:, dc, :],
                start=(dc == 0), stop=(dc == 7)),
                inc=("s_pe" if dc == 7 else None))
        W("scalar", "s_pe")
        vlg = prog.emit("scalar", lambda E, ps=ps: E.activation(
            LOG, ps[0:8, :], ACT.Identity, bias=p_mbg), inc="s_s")
        psum.release(bi, "s_s", vlg)
        W("vector", "s_s", vlg)
        EM("vector", lambda E: E.memset(ONES8.rearrange("p a b -> p (a b)"), 1.0), inc="s_v")
        W("tensor", "s_s", vlg)
        bi2, ps2 = psum.acquire()
        for s in range(4):
            EM("tensor", lambda E, ps=ps2, s=s: E.transpose(
                ps[0:128, s * 8:(s + 1) * 8], LOG[0:8, s * 128:(s + 1) * 128],
                ident[0:8, 0:8]), inc=("s_pe" if s == 3 else None))
        W("vector", "s_pe")
        vtr = ev(prog, lambda E, ps=ps2: E.tensor_copy(
            LGT.rearrange("p a b -> p (a b)"), ps[:, 0:32]))
        psum.release(bi2, "s_pe" if False else "s_v", vtr)
        for s in range(4):
            ev(prog, lambda E, s=s: E.reduce_max(MXC[:, s:s + 1], LGT[:, s, :], axis=AX.X))
        for s in range(4):
            ev(prog, lambda E, s=s: E.tensor_scalar(
                LGT[:, s, :], LGT[:, s, :], MXC[:, s:s + 1], None, ALU.is_ge))
        for s in range(4):
            ev(prog, lambda E, s=s: E.tensor_tensor_scan(
                MTOK[:, s, :], ONES8[:, s, :], LGT[:, s, :], 0.0, ALU.mult, ALU.add))
        for s in range(4):
            ev(prog, lambda E, s=s: E.tensor_tensor(
                MTOK[:, s, :], MTOK[:, s, :], LGT[:, s, :], ALU.subtract))
        for s in range(4):
            ev(prog, lambda E, s=s: E.tensor_single_scalar(
                MTOK[:, s, :], MTOK[:, s, :], 0.5, ALU.is_le))
        for s in range(4):
            ev(prog, lambda E, s=s: E.tensor_tensor(
                MTOK[:, s, :], MTOK[:, s, :], LGT[:, s, :], ALU.mult))
        for e in range(NE):
            W("tensor", "s_v")
            bb0, psb0 = psum.acquire()
            for s in range(4):
                EM("tensor", lambda E, ps=psb0, s=s, e=e: E.transpose(
                    ps[0:1, s * 128:(s + 1) * 128], MTOK[:, s, e:e + 1],
                    ident), inc=("s_pe" if s == 3 else None))
            W("scalar", "s_pe")
            vmr = prog.emit("scalar", lambda E, ps=psb0: E.copy(
                rstat[0:1, 1536:2048], ps[0:1, 0:512]), inc="s_s")
            psum.release(bb0, "s_s", vmr)
            if "mrow" in tap_out and mt == NMT - 1:
                W("sync", "s_s", vmr)
                prog.dma("sync", tap_out["mrow"][e:e + 1, :], rstat[0:1, 1536:2048], "s_st")
                W("scalar", "s_st")
            W("tensor", "s_s", vmr)
            bb, psb = psum.acquire()
            EM("tensor", lambda E, ps=psb: E.matmul(
                ps, ones1, rstat[0:1, 1536:2048], start=True, stop=True), inc="s_pe")
            W("scalar", "s_pe")
            vmb = prog.emit("scalar", lambda E, ps=psb: E.copy(BCAST[:, 2, :], ps), inc="s_s")
            psum.release(bb, "s_s", vmb)
            for mi in range(8):
                wsl = WS[mi % 2]
                W("sync", "s_pe", ws_lastuse[mi % 2])
                lv = ld(wsl, mW1[e, :, mi * 128:(mi + 1) * 128].rearrange(
                    "(dc p) i -> p dc i", p=128))
                W("tensor", "s_ld", lv)
                bi, ps = psum.acquire()
                for dc in range(8):
                    EM("tensor", lambda E, ps=ps, wsl=wsl, dc=dc, XN1=XN1: E.matmul(
                        ps, wsl[:, dc, :], XN1[:, dc, :],
                        start=(dc == 0), stop=(dc == 7)),
                        inc=("s_pe" if dc == 7 else None))
                ws_lastuse[mi % 2] = prog.cnt["s_pe"]
                W("scalar", "s_pe")
                prog.wait("scalar", "s_pe")
                vv = prog.emit("scalar", lambda E, ps=ps, e=e, mi=mi: E.activation(
                    HE[:, mi, :], ps, ACT.Gelu,
                    bias=p_mb1[:, e, mi:mi + 1]), inc="s_s")
                psum.release(bi, "s_s", vv)
            W("tensor", "s_s")
            for md in range(8):
                wsl = WS[md % 2]
                W("sync", "s_pe", ws_lastuse[md % 2])
                lv = ld(wsl, mW2[e, :, md * 128:(md + 1) * 128].rearrange(
                    "(ic p) h -> p ic h", p=128))
                W("tensor", "s_ld", lv)
                bi, ps = psum.acquire()
                for ic in range(8):
                    EM("tensor", lambda E, ps=ps, wsl=wsl, ic=ic: E.matmul(
                        ps, wsl[:, ic, :], HE[:, ic, :],
                        start=(ic == 0), stop=(ic == 7)),
                        inc=("s_pe" if ic == 7 else None))
                ws_lastuse[md % 2] = prog.cnt["s_pe"]
                W("scalar", "s_pe")
                prog.wait("scalar", "s_v")
                vv = prog.emit("scalar", lambda E, ps=ps, e=e, md=md: E.activation(
                    GB[0][:, 0, 0:512], ps, ACT.Identity,
                    bias=p_mb2[:, e, md:md + 1]), inc="s_s")
                psum.release(bi, "s_s", vv)
                W("vector", "s_s", vv)
                ev(prog, lambda E, md=md: E.tensor_tensor(
                    GB[0][:, 0, 0:512], GB[0][:, 0, 0:512], BCAST[:, 2, :], ALU.mult))
                ev(prog, lambda E, md=md, G2=G2: E.tensor_tensor(
                    G2[:, md, :], G2[:, md, :], GB[0][:, 0, 0:512], ALU.add))
        if "log8" in tap_out and mt == NMT - 1:
            W("sync", "s_s")
            prog.dma("sync", tap_out["log8"], LOG, "s_st")
        if "mtok" in tap_out and mt == NMT - 1:
            W("sync", "s_v")
            prog.dma("sync", tap_out["mtok"], MTOK.rearrange("p a b -> p (a b)"), "s_st")
        W("scalar", "s_v")
        vs2 = prog.emit("scalar", lambda E: E.activation(Af[2], Af[2], ACT.Silu), inc="s_s")
        W("sync", "s_s", vs2)
        st(g2_s[:, t0c:t1c].rearrange("(dc p) t -> p dc t", p=128), G2)

    if stop_after == "phD":
        _finish(prog, nc, psum, out_hbm, tap_out, locals())
        return _close(nc, prog, stack, tap_out)

    REPS = [KVf[:, 0:2048], KVf[:, 2048:4096]]
    rep_store = [0, 0]
    W("sync", "s_st")
    for nrow in range(32):
        for hc in range(2):
            W("sync", "s_pe")
            lvr = ld(rowbuf[0:1, 0:2048], bc_s[nrow:nrow + 1, hc * 2048:(hc + 1) * 2048])
            W("tensor", "s_ld", lvr)
            W("tensor", "s_v")
            for q in range(4):
                EM("tensor", lambda E, q=q: E.matmul(
                    psBC[:, q * 512:(q + 1) * 512], ones1,
                    rowbuf[0:1, q * 512:(q + 1) * 512],
                    start=True, stop=True), inc=("s_pe" if q == 3 else None))
            sl = (nrow * 2 + hc) % 2
            W("vector", "s_pe")
            W("vector", "s_st", rep_store[sl])
            vv = ev(prog, lambda E, sl=sl: E.tensor_copy(REPS[sl], psBC))
            W("sync", "s_v", vv)
            rep_store[sl] = st(bc_rep[nrow, :, hc * 2048:(hc + 1) * 2048], REPS[sl])

    TH = NT // 2
    bc_use = [0, 0]
    bc_lv = [(0, 0), (0, 0)]
    prog.wait("gpsimd", "s_st")
    for dc in range(16):
        DLT = Af[0]
        XCC = Af[1]
        DU = Af[2]
        YA = Af[3]
        W("sync", "s_v")
        W("sync", "s_s")
        W("sync", "s_pe")
        W("sync", "s_st")
        lvd = ld(DLT.rearrange("p (h x) -> p h x", h=2), delta_s[dc * 128:(dc + 1) * 128, :].rearrange("p (h x) -> p h x", h=2))
        lvc = ld(XCC.rearrange("p (h x) -> p h x", h=2), xc_s[dc * 128:(dc + 1) * 128, :].rearrange("p (h x) -> p h x", h=2))
        W("vector", "s_ld", lvc)
        W("scalar", "s_ld", lvc)
        ev(prog, lambda E: E.tensor_tensor(DU, DLT, XCC, ALU.mult))
        W("vector", "s_st")
        EM("vector", lambda E: E.memset(YA, 0.0), inc="s_v")
        EB = GBf[0][:, 0:2048]
        FB = GBf[1][:, 0:2048]
        SCO = GBf[2][:, 0:2048]
        for half in range(2):
            hsl = slice(half * TH, (half + 1) * TH)
            for n in range(DS):
                sl = n % 2
                BBUF = KVf[:, sl * 4096: sl * 4096 + 2048]
                CBUF = KVf[:, sl * 4096 + 2048: (sl + 1) * 4096]
                W("sync", "s_v", bc_use[sl])
                lvb = prog.dma("sync", BBUF,
                               bc_rep[n, :, half * TH:(half + 1) * TH], "s_ld")
                prog.wait("gpsimd", "s_v", bc_use[sl])
                lvc2 = prog.dma("gpsimd", CBUF,
                                bc_rep[DS + n, :, half * TH:(half + 1) * TH], "s_g")
                bc_lv[sl] = (lvb, lvc2)
                prog.wait("scalar", "s_v")
                vse = prog.emit("scalar", lambda E, dc=dc, n=n, hsl=hsl: E.activation(
                    EB, DLT[:, hsl], ACT.Exp, scale=p_aneg[:, dc, n:n + 1]), inc="s_s")
                W("vector", "s_ld", bc_lv[sl][0])
                ev(prog, lambda E, hsl=hsl, BBUF=BBUF: E.tensor_tensor(
                    FB, DU[:, hsl], BBUF, ALU.mult))
                W("vector", "s_s", vse)
                if half == 0:
                    ev(prog, lambda E: E.tensor_tensor_scan(SCO, EB, FB, 0.0, ALU.mult, ALU.add))
                    ev(prog, lambda E, n=n: E.tensor_copy(carry[:, n:n + 1], SCO[:, TH - 1:TH]))
                else:
                    ev(prog, lambda E, n=n: E.tensor_tensor_scan(
                        SCO, EB, FB, carry[:, n:n + 1], ALU.mult, ALU.add))
                W("vector", "s_g", bc_lv[sl][1])
                ev(prog, lambda E, CBUF=CBUF: E.tensor_tensor(FB, SCO, CBUF, ALU.mult))
                bc_use[sl] = ev(prog, lambda E, hsl=hsl: E.tensor_tensor(
                    YA[:, hsl], YA[:, hsl], FB, ALU.add))
        ev(prog, lambda E, dc=dc: E.scalar_tensor_tensor(
            YA, XCC, p_dpar[:, dc:dc + 1], YA, ALU.mult, ALU.add))
        ZSC = Af[1]
        W("sync", "s_v")
        lvz = ld(ZSC.rearrange("p (h x) -> p h x", h=2), zs_s[dc * 128:(dc + 1) * 128, :].rearrange("p (h x) -> p h x", h=2))
        W("vector", "s_ld", lvz)
        vy = ev(prog, lambda E: E.tensor_tensor(YA, YA, ZSC, ALU.mult))
        W("sync", "s_v", vy)
        st(y_s[dc * 128:(dc + 1) * 128, :].rearrange("p (h x) -> p h x", h=2),
           YA.rearrange("p (h x) -> p h x", h=2))

    if stop_after == "phC2":
        _finish(prog, nc, psum, out_hbm, tap_out, locals())
        return _close(nc, prog, stack, tap_out)

    for mt in range(NMT):
        t0c, t1c = mt * MT, (mt + 1) * MT
        W("sync", "s_v")
        W("sync", "s_s")
        W("sync", "s_pe")
        W("sync", "s_st")
        W("sync", "s_g")
        Y0, Y1 = A[0], A[1]
        XNEW = A[2]
        G2 = A[3]
        lv0 = ld(Y0, y_s[0:1024, t0c:t1c].rearrange("(dc p) t -> p dc t", p=128))
        lv1 = ld(Y1, y_s[1024:2048, t0c:t1c].rearrange("(dc p) t -> p dc t", p=128))
        lv2 = ld(XNEW, xnew_s[:, t0c:t1c].rearrange("(dc p) t -> p dc t", p=128))
        lv3 = ld(G2, g2_s[:, t0c:t1c].rearrange("(dc p) t -> p dc t", p=128))
        import os as _os
        if _os.environ.get("PHC3_LOADS_ONLY"):
            continue
        for me in range(8):
            wsl = WB[me % 2]
            W("sync", "s_pe", ws_lastuse[me % 2])
            lvw = prog.dma("sync", wsl.rearrange("p (k m) -> p k m", m=128),
                           outpT_s[:, me * 128:(me + 1) * 128].rearrange(
                               "(k p) m -> p k m", p=128), "s_ld")
            W("tensor", "s_ld", lvw)
            bi, ps = psum.acquire()
            for k in range(16):
                EM("tensor", lambda E, ps=ps, wsl=wsl, k=k: E.matmul(
                    ps, wsl[:, k * 128:(k + 1) * 128],
                    (Y0 if k < 8 else Y1)[:, k % 8, :],
                    start=(k == 0), stop=(k == 15)),
                    inc=("s_pe" if k == 15 else None))
            ws_lastuse[me % 2] = prog.cnt["s_pe"]
            W("vector", "s_pe")
            FIN = GB[0].rearrange("p a b -> p (a b)")[:, 0:512]
            W("vector", "s_st")
            ev(prog, lambda E, ps=ps, me=me, FIN=FIN, G2=G2: E.tensor_tensor(
                FIN, ps, G2[:, me, :], ALU.mult))
            vv = ev(prog, lambda E, me=me, FIN=FIN, XNEW=XNEW: E.tensor_tensor(
                FIN, FIN, XNEW[:, me, :], ALU.add))
            psum.release(bi, "s_v", vv)
            W("tensor", "s_v", vv)
            bi2, ps2 = psum.acquire()
            for s in range(4):
                EM("tensor", lambda E, ps=ps2, FIN=FIN, s=s: E.transpose(
                    ps[:, s * 128:(s + 1) * 128], FIN[:, s * 128:(s + 1) * 128],
                    ident), inc=("s_pe" if s == 3 else None))
            W("scalar", "s_pe")
            prog.wait("scalar", "s_st")
            OUTT = GB[1].rearrange("p a b -> p (a b)")[:, 0:512]
            vv2 = prog.emit("scalar", lambda E, ps=ps2, OUTT=OUTT: E.copy(OUTT, ps), inc="s_s")
            psum.release(bi2, "s_s", vv2)
            W("sync", "s_s", vv2)
            for s in range(4):
                prog.dma("sync",
                         out_hbm[t0c + s * 128:t0c + (s + 1) * 128,
                                 me * 128:(me + 1) * 128],
                         OUTT[0:128, s * 128:(s + 1) * 128], "s_st")
            W("scalar", "s_st")

    out_written = True
    _finish(prog, nc, psum, out_hbm, tap_out, locals())
    return _close(nc, prog, stack, tap_out)


def _finish(prog, nc, psum, out_hbm, tap_out, L):
    W = prog.wait
    prog.steps["sync"]
    bounce = L["Af"][0]
    srcmap = {"xn": L["xn_s"], "qp": L["qp_s"], "kv": L["kv_s"], "ksum": L["ksum_s"],
              "w1T": L["w1T_s"], "w2qT": L["w2qT_s"],
              "xnew": L["xnew_s"], "xn1": L["xn1_s"], "xc": L["xc_s"],
              "zs": L["zs_s"], "delta": L["delta_s"], "bc": L["bc_s"],
              "yts": L["y_s"], "g2": L["g2_s"]}
    for s in ("s_v", "s_s", "s_pe", "s_st", "s_ld"):
        W("sync", s)
    sbmap = {"sb_log": L.get("LOG"), "sb_bcast2": None, "sb_ones1": L.get("ones1")}
    if "MTOK" in L:
        sbmap["sb_mtok"] = L["MTOK"].rearrange("p a b -> p (a b)")
        sbmap["sb_lgt"] = L["LGT"].rearrange("p a b -> p (a b)")
        sbmap["sb_mrow"] = L["rstat"][0:1, 1536:2048]
        sbmap["sb_bcast2"] = L["BCAST"][:, 2, :]
    for name, dst in tap_out.items():
        src = sbmap.get(name)
        if src is not None:
            prog.dma("sync", dst, src, "s_st")
            W("sync", "s_st")
    for name, dst in tap_out.items():
        src = srcmap.get(name)
        if src is None:
            continue
        R, C = src.shape
        for r0 in range(0, R, 128):
            rr = min(128, R - r0)
            lv = prog.dma("sync", bounce[0:rr, 0:C], src[r0:r0 + rr, :], "s_ld")
            W("sync", "s_ld", lv)
            prog.dma("sync", dst[r0:r0 + rr, :], bounce[0:rr, 0:C], "s_st")
            W("sync", "s_st")
    if not L.get("out_written"):
        zb = L["Af"][1]
        prog.emit("vector", lambda E, zb=zb: E.memset(zb, 0.0), inc="s_v")
        W("sync", "s_v")
        for mt in range(NMT):
            prog.dma("sync",
                     out_hbm[mt * MT:(mt + 1) * MT, :].rearrange("(s p) d -> p s d", p=128),
                     zb.rearrange("p (s d) -> p s d", d=D)[:, 0:4, :], "s_st")
    W("sync", "s_st")


def _close(nc, prog, stack, tap_out):
    with nc.Block() as block:
        prog.finalize(block)
    return nc, list(tap_out)



def kernel(**inputs):
    xfull = np.ascontiguousarray(np.asarray(inputs["L0_attn_x"], dtype=np.float32))
    B = xfull.shape[0]
    nc, taps = build(stop_after=STOP_AFTER, taps=TAPS)
    ident = np.eye(128, dtype=np.float32)
    base = {k: np.ascontiguousarray(np.asarray(v, dtype=np.float32))
            for k, v in inputs.items() if k != "L0_attn_x"}
    in_maps = []
    for c in range(B):
        m = dict(base)
        m["x"] = xfull[c]
        m["ident"] = ident
        in_maps.append(m)
    res = run_bass_kernel_spmd(nc, in_maps, core_ids=list(range(B)))
    out = np.stack([r["out"] for r in res.results], axis=0)
    kernel.last_results = res.results
    return out
